# version 08
# baseline (speedup 1.0000x reference)
import numpy as np
import ml_dtypes
from contextlib import ExitStack

import concourse.bass as bass
import concourse.mybir as mybir
from concourse.bass_utils import run_bass_kernel_spmd

F32 = mybir.dt.float32
BF16 = mybir.dt.bfloat16
AF = mybir.ActivationFunctionType
OP = mybir.AluOpType
AX = mybir.AxisListType
BFNP = ml_dtypes.bfloat16

T = 2048
D = 1024
NCH = 8
HEADS = 16
HD = 64
NE = 32
MOE_H = 512
FJ = 22
DI = 2730
NEGBIG = -3.0e38

ALL_STAGES = ("moe", "l0attn", "moe", "ffn", "l1attn", "ffn")
DBG_FFN_ITERS = 0


def host_constants():
    c = {}
    c["ident"] = np.eye(128, dtype=BFNP)
    c["identf"] = np.eye(128, dtype=np.float32)
    c["ones"] = np.ones((128, 128), dtype=BFNP)

    P = np.zeros((128, 128), dtype=BFNP)
    for r in range(128):
        g = (r // 64) * 64
        P[g + (r - g + 32) % 64, r] = 1
    c["pswap"] = P

    n = T
    half = np.arange(0, HD, 2, dtype=np.float64)
    inv_freq = 1.0 / 10000.0 ** (half / HD)
    t = np.arange(n, dtype=np.float64)
    freqs = np.concatenate([t[:, None] * inv_freq[None, :]] * 2, -1)
    base_scale = (half + 0.4 * HD) / (1.4 * HD)
    power = (t - n // 2) / (256 // 2)
    scale = np.concatenate([base_scale[None, :] ** power[:, None]] * 2, -1)
    sgn = np.where(np.arange(HD) < 32, -1.0, 1.0)
    A_q = (np.cos(freqs) * scale).T * 8.0
    B_q = (np.sin(freqs) * scale).T * sgn[:, None] * 8.0
    A_k = (np.cos(freqs) / scale).T
    B_k = (np.sin(freqs) / scale).T * sgn[:, None]
    for nm, v in (("rotAq", A_q), ("rotBq", B_q), ("rotAk", A_k), ("rotBk", B_k)):
        c[nm] = np.ascontiguousarray(np.concatenate([v, v], 0).astype(BFNP))

    i = np.arange(128)[:, None]
    jd = np.arange(128)[None, :]
    keep = (jd <= i)
    c["dgmask"] = np.concatenate([
        np.where(keep, 3.0e38, -3.0e38),
        np.where(keep, 3.0e38, -1e-6),
    ], axis=1).astype(np.float32)
    for qc in (0, 1):
        i = np.arange(128)[:, None]
        j = np.arange(384)[None, :]
        dlt = 256 + i - j
        c[f"lmask{qc}"] = np.where((dlt >= 0) & (dlt <= 256), 0.0, NEGBIG).astype(BFNP)
        kjg = j - 256 + qc * 128
        c[f"lmask0_{qc}"] = np.where((dlt >= 0) & (dlt <= 256) & (kjg >= 0), 0.0, NEGBIG).astype(BFNP)

    selq = np.zeros((NCH, 128, HEADS), dtype=BFNP)
    selr = np.zeros((NCH, HEADS, 128), dtype=BFNP)
    for ci in range(NCH):
        selq[ci, 0:64, 2 * ci] = 1
        selq[ci, 64:128, 2 * ci + 1] = 1
        selr[ci, 2 * ci, 0:64] = 1
        selr[ci, 2 * ci + 1, 64:128] = 1
    c["selq"] = np.ascontiguousarray(selq.transpose(1, 0, 2).reshape(128, NCH * HEADS))
    c["selr"] = np.ascontiguousarray(selr.transpose(1, 0, 2).reshape(HEADS, NCH * 128))
    return c


def host_layout(inp):
    d = {}
    d["r0"] = inp["L0_ffn_router_w0"]
    d["r1"] = inp["L0_ffn_router_w1"]
    for g in (0, 1):
        d[f"wg{g}"] = inp[f"L0_ffn_wg{g}"]
        d[f"wu{g}"] = inp[f"L0_ffn_wu{g}"]
        d[f"wd{g}"] = inp[f"L0_ffn_wd{g}"]
    for nm in ("Wq", "Wk", "Wv", "Wo"):
        d[nm + "T"] = np.ascontiguousarray(inp[f"L0_attn_{nm}"].T)
    for nm in ("bq", "bk", "bv", "bo"):
        d[nm + "r"] = np.ascontiguousarray(inp[f"L0_attn_{nm}"].reshape(NCH, 128).T)
    d["wqkvT"] = np.ascontiguousarray(inp["L1_attn_w_qkv"].T)
    d["wkyT"] = np.ascontiguousarray(inp["L1_attn_w_ky"].T)
    d["wprojT"] = np.ascontiguousarray(inp["L1_attn_w_proj"].T)

    d["f_wqkv"] = inp["L1_ffn_Wqkv"]
    d["f_wo"] = inp["L1_ffn_Wo"]
    d["f_qs"] = np.ascontiguousarray(np.tile(inp["L1_ffn_q_scale"], (1, 2)).T)
    d["f_ks"] = np.ascontiguousarray(np.tile(inp["L1_ffn_k_scale"], (1, 2)).T)
    d["f_lng"] = np.ascontiguousarray(
        inp["L1_ffn_ln_g"].reshape(6, NCH, 128).transpose(2, 0, 1).reshape(128, 48))
    d["f_lnb"] = np.ascontiguousarray(
        inp["L1_ffn_ln_b"].reshape(6, NCH, 128).transpose(2, 0, 1).reshape(128, 48))
    d["f_b2"] = np.ascontiguousarray(
        inp["L1_ffn_b2"].reshape(6, NCH, 128).transpose(2, 0, 1).reshape(128, 48))

    W1 = inp["L1_ffn_W1"]
    b1 = inp["L1_ffn_b1"]
    W1a = np.zeros((6, D, FJ * 128), dtype=np.float32)
    W1g = np.zeros((6, D, FJ * 128), dtype=np.float32)
    W1a[:, :, :DI] = W1[:, :, :DI]
    W1g[:, :, :DI] = W1[:, :, DI:]
    b1a = np.zeros((6, FJ * 128), dtype=np.float32)
    b1g = np.zeros((6, FJ * 128), dtype=np.float32)
    b1a[:, :DI] = b1[:, :DI]
    b1g[:, :DI] = b1[:, DI:]
    d["f_w1a"] = np.ascontiguousarray(W1a.reshape(6, NCH, 128, FJ, 128).transpose(0, 2, 1, 3, 4))
    d["f_w1g"] = np.ascontiguousarray(W1g.reshape(6, NCH, 128, FJ, 128).transpose(0, 2, 1, 3, 4))
    d["f_b1a"] = np.ascontiguousarray(b1a.reshape(6, FJ, 128).transpose(2, 0, 1).reshape(128, 132))
    d["f_b1g"] = np.ascontiguousarray(b1g.reshape(6, FJ, 128).transpose(2, 0, 1).reshape(128, 132))
    W2p = np.zeros((6, FJ * 128, D), dtype=np.float32)
    W2p[:, :DI, :] = inp["L1_ffn_W2"]
    d["f_w2"] = np.ascontiguousarray(W2p.reshape(6, FJ, 128, D).transpose(0, 2, 1, 3))
    return d


class Sem:
    def __init__(self, nc, name):
        self.h = nc.alloc_semaphore(name)
        self.v = 0

    def inc(self, ins, n=1):
        ins.then_inc(self.h, n)
        self.v += n
        return self.v


class Prog:
    def __init__(self, nc, stages):
        self.nc = nc
        self.stages = stages

        def din(name, shape, dtype=F32):
            return nc.dram_tensor(name, list(shape), dtype, kind="ExternalInput").ap()

        self.x = din("x", (D, T))
        self.r_w = [din("r0", (D, NE)), din("r1", (D, NE))]
        self.wg = [din("wg0", (NE, D, MOE_H)), din("wg1", (NE, D, MOE_H))]
        self.wu = [din("wu0", (NE, D, MOE_H)), din("wu1", (NE, D, MOE_H))]
        self.wd = [din("wd0", (NE, MOE_H, D)), din("wd1", (NE, MOE_H, D))]
        self.WT = {nm: din(nm + "T", (D, D)) for nm in ("Wq", "Wk", "Wv", "Wo")}
        self.brd = {nm: din(nm + "r", (128, NCH)) for nm in ("bq", "bk", "bv", "bo")}
        self.wqkvT = din("wqkvT", (D, 3 * D))
        self.wkyT = din("wkyT", (D, D))
        self.wprojT = din("wprojT", (D, D))
        self.f_wqkv = din("f_wqkv", (6, D, 3 * D))
        self.f_wo = din("f_wo", (6, D, D))
        self.f_qs = din("f_qs", (128, 6))
        self.f_ks = din("f_ks", (128, 6))
        self.f_lng = din("f_lng", (128, 48))
        self.f_lnb = din("f_lnb", (128, 48))
        self.f_b2 = din("f_b2", (128, 48))
        self.f_w1a = din("f_w1a", (6, 128, NCH, FJ, 128))
        self.f_w1g = din("f_w1g", (6, 128, NCH, FJ, 128))
        self.f_b1a = din("f_b1a", (128, 132))
        self.f_b1g = din("f_b1g", (128, 132))
        self.f_w2 = din("f_w2", (6, 128, FJ, D))

        self.c_ident = din("c_ident", (128, 128), BF16)
        self.c_identf = din("c_identf", (128, 128), F32)
        self.c_ones = din("c_ones", (128, 128), BF16)
        self.c_pswap = din("c_pswap", (128, 128), BF16)
        self.c_rot = {nm: din("c_" + nm, (128, T), BF16)
                      for nm in ("rotAq", "rotBq", "rotAk", "rotBk")}
        self.c_lmask = [din(f"c_lmask{qc}", (128, 384), BF16) for qc in (0, 1)]
        self.c_dgmask = din("c_dgmask", (128, 256), F32)
        self.c_lmask0 = [din(f"c_lmask0_{qc}", (128, 384), BF16) for qc in (0, 1)]
        self.c_selq = din("c_selq", (128, NCH * HEADS), BF16)
        self.c_selr = din("c_selr", (HEADS, NCH * 128), BF16)

        self.out = nc.dram_tensor("out", [D, T], F32, kind="ExternalOutput").ap()

        self.gateT_dram = nc.dram_tensor("s_gateT", [2, NE, T], F32, kind="Internal").ap()
        self.ab_dram = nc.dram_tensor("s_ab", [1, 2 * T], F32, kind="Internal").ap()
        self.rspill = nc.dram_tensor("s_rspill", [128, NCH * T], F32, kind="Internal").ap()

        self.glob = ExitStack()
        sb = lambda n, s, d: self.glob.enter_context(nc.sbuf_tensor(n, s, d))
        self.rr = sb("rr", [128, NCH * T], F32)
        self.ubf = sb("ubf", [128, NCH * T], BF16)
        self.identb = sb("identb", [128, 128], BF16)
        self.onesb = sb("onesb", [128, 128], BF16)
        self.pswapb = sb("pswapb", [128, 128], BF16)
        self.arow = sb("arow", [128, T], F32)
        self.brow = sb("brow", [128, T], F32)
        self.fta = sb("fta", [128, T], F32)
        self.cstb = sb("cstb", [128, 8], F32)
        self.ftb = sb("ftb", [128, T], F32)

        self.s = {nm: Sem(nc, nm) for nm in
                  ("ld", "wq", "pe", "ve", "se", "ge", "sp", "rep", "st")}
        self.ffn_inv = 0

    def uname(self, n):
        self._uid = getattr(self, "_uid", 0) + 1
        return f"{n}_{self._uid}"

    def ch(self, buf, c, w=T):
        return buf[:, c * w:(c + 1) * w]

    class EvProg:

        def __init__(self, prog):
            self.p = prog
            self.evs = []
            self.pl = {k: v.v for k, v in prog.s.items()}
            self.joins = [(k, v) for k, v in self.pl.items()
                          if k in ("pe", "ve", "se", "ge")]

        def add(self, eng, emit, waits=(), inc=None, n=1):
            self.evs.append((eng, emit, list(waits), inc, n))
            if inc is not None:
                self.pl[inc] += n
                return self.pl[inc]
            return None

        def val(self, sem):
            return self.pl[sem]

        def emit(self, block):
            p = self.p
            for eng_name in ("sync", "gpsimd", "tensor", "scalar", "vector"):
                evs = [e for e in self.evs if e[0] == eng_name]
                if not evs:
                    continue
                def body(eng, evs=evs):
                    for sem, val in self.joins:
                        eng.wait_ge(p.s[sem].h, val)
                    for _, emit, waits, inc, n in evs:
                        for sem, val in waits:
                            eng.wait_ge(p.s[sem].h, val)
                        ins = emit(eng)
                        if inc is not None:
                            p.s[inc].inc(ins, n)
                getattr(block, eng_name)(body)
            for k, v in self.pl.items():
                assert p.s[k].v == v, (k, p.s[k].v, v)

    def edge(self, eng, ins, sem="ve"):
        s = self.s[sem]
        s.inc(ins, 1)
        eng.wait_ge(s.h, s.v)

    def emit_preload(self):
        nc, s = self.nc, self.s
        with nc.Block("preload") as block:
            @block.sync
            def _(sy):
                for o, i in (
                    (self.rr.ap().rearrange("p (c n) -> p c n", c=NCH),
                     self.x.rearrange("(c p) n -> p c n", p=128)),
                    (self.identb[:, :], self.c_ident),
                    (self.onesb[:, :], self.c_ones),
                    (self.pswapb[:, :], self.c_pswap),
                ):
                    s["ld"].inc(sy.dma_start(out=o, in_=i), 16)
                sy.wait_ge(s["ld"].h, s["ld"].v)

            @block.gpsimd
            def _(g):
                for col, val in ((0, 1e-5), (1, 2e-6), (2, 1e-6), (3, 1e-12), (4, 0.0), (5, 3e-6)):
                    g.memset(self.cstb[:, col:col + 1], val)
                g.engine_nop().then_inc(s["ge"].h, 1)
                s["ge"].v += 1

    def emit_ln(self, block, gain_ap=None, bias_ap=None, shift=False,
                fp32_inplace=False, pstat_ext=None, wait_ve=None, wait_se=None,
                wait_pe=None):
        nc, s = self.nc, self.s
        ps = ExitStack()
        if pstat_ext is None:
            pstat = ps.enter_context(nc.psum_tensor(self.uname("pstat"), [1, 1024], F32)).ap()
        else:
            pstat = pstat_ext
        sqb = self.fta.ap().bitcast(BF16)

        se0 = s["se"].v

        @block.scalar
        def _(sc):
            if wait_ve is not None:
                sc.wait_ge(s["ve"].h, wait_ve)
            if wait_se is not None:
                sc.wait_ge(s["se"].h, wait_se)
            if wait_pe is not None:
                sc.wait_ge(s["pe"].h, wait_pe)
            for c in range(NCH):
                if shift and c >= 4:
                    sc.copy(self.ch(self.ubf, c)[:, 0:1], self.cstb[:, 4:5])
                    ins = sc.copy(self.ch(self.ubf, c)[:, 1:T], self.ch(self.rr, c)[:, 0:T - 1])
                else:
                    ins = sc.copy(self.ch(self.ubf, c), self.ch(self.rr, c))
                s["se"].inc(ins, 1)

        ve0 = s["ve"].v
        pe0 = s["pe"].v
        se_evac0 = se0 + NCH

        @block.vector
        def _(v):
            for f in range(4):
                for c in range(NCH):
                    g = f * NCH + c
                    v.wait_ge(s["se"].h, se0 + c + 1)
                    if g >= 2:
                        v.wait_ge(s["pe"].h, pe0 + g - 1)
                    half = sqb[:, (g % 2) * T:(g % 2) * T + 512]
                    ins = v.tensor_mul(half, self.ch(self.ubf, c)[:, f * 512:(f + 1) * 512],
                                       self.ch(self.ubf, c)[:, f * 512:(f + 1) * 512])
                    s["ve"].inc(ins, 1)

        @block.tensor
        def _(t):
            for f in range(4):
                if f >= 1:
                    t.wait_ge(s["se"].h, se_evac0 + 2 * f)
                for c in range(NCH):
                    g = f * NCH + c
                    t.wait_ge(s["ve"].h, ve0 + g + 1)
                    half = sqb[:, (g % 2) * T:(g % 2) * T + 512]
                    fs = slice(f * 512, (f + 1) * 512)
                    t.matmul(pstat[0:1, 0:512], self.onesb[:, 0:1],
                             self.ch(self.ubf, c)[:, fs],
                             start=(c == 0), stop=(c == NCH - 1))
                    ins = t.matmul(pstat[0:1, 512:1024], self.onesb[:, 1:2], half,
                                   start=(c == 0), stop=(c == NCH - 1))
                    s["pe"].inc(ins, 1)

        @block.scalar
        def _(sc):
            for f in range(4):
                fs = slice(f * 512, (f + 1) * 512)
                sc.wait_ge(s["pe"].h, pe0 + NCH * (f + 1))
                ins = sc.copy(self.brow[0:1, fs], pstat[0:1, 0:512])
                s["se"].inc(ins, 1)
                ins = sc.copy(self.arow[0:1, fs], pstat[0:1, 512:1024])
                s["se"].inc(ins, 1)

        ve1 = s["ve"].v

        @block.vector
        def _(v):
            v.wait_ge(s["se"].h, se_evac0 + 8)
            ar, br = self.arow[0:1, 0:T], self.brow[0:1, 0:T]
            tmp = self.fta[0:1, 0:T]
            ins = v.tensor_scalar_mul(br, br, 1.0 / D)
            s["ve"].inc(ins, 1)
            v.wait_ge(s["ve"].h, s["ve"].v)
            ins = v.tensor_scalar_mul(ar, ar, 1.0 / D)
            s["ve"].inc(ins, 1)
            v.wait_ge(s["ve"].h, s["ve"].v)
            ins = v.tensor_mul(tmp, br, br)
            s["ve"].inc(ins, 1)
            v.wait_ge(s["ve"].h, s["ve"].v)
            ins = v.tensor_sub(ar, ar, tmp)
            s["ve"].inc(ins, 1)

        se1 = s["se"].v

        @block.scalar
        def _(sc):
            sc.wait_ge(s["ve"].h, ve1 + 4)
            ins = sc.activation(self.arow[0:1, 0:T], self.arow[0:1, 0:T], AF.Sqrt,
                                bias=self.cstb[0:1, 0:1])
            s["se"].inc(ins, 1)

        @block.vector
        def _(v):
            v.wait_ge(s["se"].h, se1 + 1)
            ar, br = self.arow[0:1, 0:T], self.brow[0:1, 0:T]
            ins = v.reciprocal(ar, ar)
            s["ve"].inc(ins, 1)
            v.wait_ge(s["ve"].h, s["ve"].v)
            ins = v.tensor_mul(br, br, ar)
            s["ve"].inc(ins, 1)
            v.wait_ge(s["ve"].h, s["ve"].v)
            ins = v.tensor_scalar_mul(br, br, -1.0)
            s["ve"].inc(ins, 1)

        sp0 = s["sp"].v
        ve2b = s["ve"].v

        @block.sync
        def _(sy):
            sy.wait_ge(s["ve"].h, ve2b)
            s["sp"].inc(sy.dma_start(out=self.ab_dram[0:1, T:2 * T], in_=self.arow[0:1, 0:T]), 16)
            s["sp"].inc(sy.dma_start(out=self.ab_dram[0:1, 0:T], in_=self.brow[0:1, 0:T]), 16)
            sy.wait_ge(s["sp"].h, sp0 + 32)
            s["sp"].inc(sy.dma_start(out=self.arow[:, :],
                                     in_=self.ab_dram[0:1, T:2 * T].partition_broadcast(128)), 16)
            s["sp"].inc(sy.dma_start(out=self.brow[:, :],
                                     in_=self.ab_dram[0:1, 0:T].partition_broadcast(128)), 16)

        sp1 = s["sp"].v
        ve3 = s["ve"].v
        se_aff = s["se"].v
        affine = gain_ap is not None

        @block.vector
        def _(v):
            v.wait_ge(s["sp"].h, sp1)
            for c in range(NCH):
                dsttmp = self.fta[:, 0:T]
                if c >= 1:
                    v.wait_ge(s["ve"].h, s["ve"].v)
                if fp32_inplace:
                    self.edge(v, v.tensor_mul(dsttmp, self.ch(self.rr, c), self.arow[:, :]))
                    ins = v.tensor_add(self.ch(self.rr, c), dsttmp, self.brow[:, :])
                elif affine:
                    self.edge(v, v.tensor_mul(dsttmp, self.ch(self.ubf, c), self.arow[:, :]))
                    if c >= 1:
                        v.wait_ge(s["se"].h, se_aff + c)
                    ins = v.tensor_add(self.ftb[:, 0:T], dsttmp, self.brow[:, :])
                else:
                    self.edge(v, v.tensor_mul(dsttmp, self.ch(self.ubf, c), self.arow[:, :]))
                    ins = v.tensor_add(self.ch(self.ubf, c), dsttmp, self.brow[:, :])
                s["ve"].inc(ins, 1)

        @block.scalar
        def _(sc):
            if affine:
                for c in range(NCH):
                    sc.wait_ge(s["ve"].h, ve3 + 2 * (c + 1))
                    ins = sc.activation(self.ch(self.ubf, c), self.ftb[:, 0:T],
                                        AF.Identity, bias=bias_ap(c), scale=gain_ap(c))
                    s["se"].inc(ins, 1)
        ps.close()

    def _moe_tail(self, block):
        pass

    def emit_gatetest(self):
        self.emit_moe(do_experts=False, gate_debug=True)

    def moe_debug_dump(self, block, bufs):
        nc, s = self.nc, self.s
        scr, scr2, m1, gtile, gT = bufs

        @block.sync
        def _(sy):
            sy.wait_ge(s["ve"].h, s["ve"].v)
            sy.wait_ge(s["se"].h, s["se"].v)
            sy.wait_ge(s["pe"].h, s["pe"].v)
            for i, (buf, rows, cols) in enumerate((
                    (scr, 128, NE), (scr2, 128, NE), (m1, 128, 4),
                    (gtile, 128, 2 * NE))):
                dst = self.rr[0:rows, (4 + i) * T:(4 + i) * T + cols]
                if buf.dtype == BF16:
                    dst = dst.bitcast(BF16)[:, 0:cols]
                ins = sy.dma_start(out=dst, in_=buf[0:rows, 0:cols])
                s["sp"].inc(ins, 16)
            sy.wait_ge(s["sp"].h, s["sp"].v)

    def emit_moe(self, do_experts=True, gate_debug=False):
        nc, s = self.nc, self.s
        with nc.Block(self.uname("moe")) as block:
            self.emit_ln(block)
            ln_ve = s["ve"].v
            ln_se = s["se"].v
            loc = ExitStack()
            sb = lambda n, sh, d: loc.enter_context(nc.sbuf_tensor(self.uname(n), sh, d))
            rb = sb("m_rb", [128, NCH * NE], BF16)
            scr = sb("m_scr", [128, NE], F32)
            scr2 = sb("m_scr2", [128, NE], F32)
            gtile = sb("m_gtile", [128, 2 * NE], BF16)
            m1 = sb("m_m1", [128, 4], F32)
            gT = sb("m_gT", [NE, T], F32)
            wgb = sb("m_wgb", [128, 2 * NCH * 512], BF16)
            wub = sb("m_wub", [128, 2 * NCH * 512], BF16)
            wdb = sb("m_wdb", [128, 2 * 4 * D], BF16)
            hb = sb("m_hb", [128, 4 * 512], BF16)
            sigb = sb("m_sigb", [128, 512], F32)
            repb = sb("m_repb", [128, 2 * 512], F32)

            psg = ExitStack()
            plg = psg.enter_context(nc.psum_tensor(self.uname("m_plg"), [128, 1024], F32))
            ptr = psg.enter_context(nc.psum_tensor(self.uname("m_ptr"), [NE, 2048], BF16))

            wgv = wgb.ap().rearrange("p (b kc n) -> p b kc n", b=2, kc=NCH)
            wuv = wub.ap().rearrange("p (b kc n) -> p b kc n", b=2, kc=NCH)
            wdv = wdb.ap().rearrange("p (b kc n) -> p b kc n", b=2, kc=4)

            WQg = s["wq"].v
            PEg = s["pe"].v
            VEg = s["ve"].v
            SEg = s["se"].v
            SPg = s["sp"].v

            @block.gpsimd
            def _(g):
                s["wq"].inc(g.dma_start(out=rb.ap().rearrange("p (kc e) -> p kc e", kc=NCH),
                                        in_=self.r_w[0].rearrange("(kc p) e -> p kc e", p=128)), 16)
                g.wait_ge(s["pe"].h, PEg + 32)
                s["wq"].inc(g.dma_start(out=rb.ap().rearrange("p (kc e) -> p kc e", kc=NCH),
                                        in_=self.r_w[1].rearrange("(kc p) e -> p kc e", p=128)), 16)

            @block.tensor
            def _(t):
                t.wait_ge(s["ve"].h, ln_ve)
                t.wait_ge(s["se"].h, ln_se)
                for gi in (0, 1):
                    t.wait_ge(s["wq"].h, WQg + 16 * (gi + 1))
                    for tc in range(16):
                        u = gi * 16 + tc
                        par = u % 2
                        if u >= 2:
                            t.wait_ge(s["se"].h, SEg + 2 * (u - 2) + 1)
                        for kc in range(NCH):
                            ins = t.matmul(plg[:, par * 512:par * 512 + NE],
                                           self.ch(self.ubf, kc)[:, tc * 128:(tc + 1) * 128],
                                           rb[:, kc * NE:(kc + 1) * NE],
                                           start=(kc == 0), stop=(kc == NCH - 1))
                        s["pe"].inc(ins, 1)
                        t.wait_ge(s["ve"].h, VEg + 9 * u + 9)
                        if u >= 2:
                            t.wait_ge(s["se"].h, SEg + 2 * (u - 2) + 2)
                        ins = t.transpose(ptr[:, par * 1024:par * 1024 + 128],
                                          gtile[:, par * NE:(par + 1) * NE], self.identb[:, :])
                        s["pe"].inc(ins, 1)

            @block.vector
            def _(v):
                for u in range(32):
                    par = u % 2
                    v.wait_ge(s["pe"].h, PEg + 2 * u + 1)
                    ins = v.tensor_reduce(m1[:, 0:1], plg[:, par * 512:par * 512 + NE],
                                          AX.X, OP.max, negate=True)
                    s["ve"].inc(ins, 1)
                    if u >= 2:
                        v.wait_ge(s["pe"].h, PEg + 2 * (u - 2) + 2)
                    v.wait_ge(s["se"].h, SEg + 2 * u + 1)
                    self.edge(v, v.reciprocal(m1[:, 1:2], m1[:, 1:2]))
                    self.edge(v, v.tensor_scalar(scr[:, :], scr[:, :], m1[:, 1:2], None, OP.mult))
                    self.edge(v, v.tensor_reduce(m1[:, 2:3], scr[:, :], AX.X, OP.max))
                    self.edge(v, v.tensor_scalar(scr2[:, :], scr[:, :], m1[:, 2:3], None, OP.is_ge))
                    self.edge(v, v.scalar_tensor_tensor(scr2[:, :], scr2[:, :], -1e30, scr[:, :],
                                                        OP.mult, OP.add))
                    self.edge(v, v.tensor_reduce(m1[:, 3:4], scr2[:, :], AX.X, OP.max))
                    self.edge(v, v.tensor_scalar(scr2[:, :], scr[:, :], m1[:, 3:4], None, OP.is_ge))
                    ins = v.scalar_tensor_tensor(gtile[:, par * NE:(par + 1) * NE],
                                                 scr2[:, :], 0.25, scr[:, :], OP.mult, OP.mult)
                    s["ve"].inc(ins, 1)

            @block.scalar
            def _(sc):
                for u in range(32):
                    par = u % 2
                    sc.wait_ge(s["ve"].h, VEg + 9 * u + 1)
                    ins = sc.activation(scr[:, :], plg[:, par * 512:par * 512 + NE],
                                        AF.Exp, bias=m1[:, 0:1], accum_out=m1[:, 1:2])
                    s["se"].inc(ins, 1)
                    sc.wait_ge(s["pe"].h, PEg + 2 * u + 2)
                    gi, tc = u // 16, u % 16
                    if u == 16:
                        sc.wait_ge(s["sp"].h, SPg + 16)
                    ins = sc.copy(gT[:, tc * 128:(tc + 1) * 128], ptr[:, par * 1024:par * 1024 + 128])
                    s["se"].inc(ins, 1)

            @block.sync
            def _(sy):
                for gi in (0, 1):
                    sy.wait_ge(s["se"].h, SEg + 2 * (16 * (gi + 1) - 1) + 2)
                    ins = sy.dma_start(out=self.gateT_dram[gi], in_=gT[:, :])
                    s["sp"].inc(ins, 16)

            if True:
                pass
            if gate_debug:
                @block.sync
                def _(sy):
                    sy.wait_ge(s["sp"].h, s["sp"].v)
                    for gi in (0, 1):
                        ins = sy.dma_start(out=self.rr[0:32, gi * T:(gi + 1) * T],
                                           in_=self.gateT_dram[gi])
                        s["sp"].inc(ins, 16)
                    sy.wait_ge(s["sp"].h, s["sp"].v)

            psg.close()
            psl = ExitStack()
            gu = psl.enter_context(nc.psum_tensor(self.uname("m_gu"), [128, 2048], F32))
            yd = psl.enter_context(nc.psum_tensor(self.uname("m_yd"), [128, 1024], F32))
            SP_G = s["sp"].v
            PE0 = s["pe"].v
            VE0 = s["ve"].v
            SE0 = s["se"].v
            WQ0 = s["wq"].v
            REP0 = s["rep"].v
            NK = 64 * 4
            if not do_experts:
                self.moe_debug_dump(block, (scr, scr2, m1, gtile, gT))
                psl.close()
                loc.close()
                return

            @block.gpsimd
            def _(g):
                for i in range(64):
                    gi, e = i // 32, i % 32
                    if i >= 2:
                        g.wait_ge(s["pe"].h, PE0 + 48 * (i - 1))
                    par = i % 2
                    s["wq"].inc(g.dma_start(out=wgv[:, par],
                                            in_=self.wg[gi][e].rearrange("(kc p) n -> p kc n", p=128)), 16)
                    s["wq"].inc(g.dma_start(out=wuv[:, par],
                                            in_=self.wu[gi][e].rearrange("(kc p) n -> p kc n", p=128)), 16)
                    s["wq"].inc(g.dma_start(out=wdv[:, par],
                                            in_=self.wd[gi][e].rearrange("(kc p) n -> p kc n", p=128)), 16)

            @block.sync
            def _(sy):
                sy.wait_ge(s["sp"].h, SP_G)
                for k in range(NK):
                    i, t = k // 4, k % 4
                    gi, e = i // 32, i % 32
                    if k >= 2:
                        sy.wait_ge(s["ve"].h, VE0 + 16 * (k - 2) + 8)
                    ins = sy.dma_start(out=repb[:, (k % 2) * 512:(k % 2 + 1) * 512],
                                       in_=self.gateT_dram[gi, e:e + 1, t * 512:(t + 1) * 512]
                                       .partition_broadcast(128))
                    s["rep"].inc(ins, 16)

            @block.tensor
            def _(t):
                t.wait_ge(s["se"].h, SE0)
                t.wait_ge(s["ve"].h, VE0)
                for k in range(NK):
                    i, ti = k // 4, k % 4
                    par_w = i % 2
                    if ti == 0:
                        t.wait_ge(s["wq"].h, WQ0 + 48 * (i + 1))
                    for hc in range(4):
                        par = hc % 2
                        if k > 0 and hc < 2:
                            t.wait_ge(s["ve"].h, VE0 + 16 * (k - 1) + 2 * hc + 6)
                        elif hc >= 2:
                            t.wait_ge(s["ve"].h, VE0 + 16 * k + 2 * hc - 2)
                        for kc in range(NCH):
                            t.matmul(gu[:, par * 1024:par * 1024 + 512],
                                     wgv[:, par_w, kc, hc * 128:(hc + 1) * 128],
                                     self.ch(self.ubf, kc)[:, ti * 512:(ti + 1) * 512],
                                     start=(kc == 0), stop=(kc == NCH - 1))
                        for kc in range(NCH):
                            ins = t.matmul(gu[:, par * 1024 + 512:par * 1024 + 1024],
                                           wuv[:, par_w, kc, hc * 128:(hc + 1) * 128],
                                           self.ch(self.ubf, kc)[:, ti * 512:(ti + 1) * 512],
                                           start=(kc == 0), stop=(kc == NCH - 1))
                        s["pe"].inc(ins, 1)
                    for d in range(8):
                        pard = d % 2
                        if d < 2:
                            t.wait_ge(s["ve"].h, VE0 + 16 * k + 8)
                        else:
                            t.wait_ge(s["ve"].h, VE0 + 16 * k + 7 + d)
                        for hcc in range(4):
                            ins = t.matmul(yd[:, pard * 512:(pard + 1) * 512],
                                           wdv[:, par_w, hcc, d * 128:(d + 1) * 128],
                                           hb[:, hcc * 512:(hcc + 1) * 512],
                                           start=(hcc == 0), stop=(hcc == 3))
                        s["pe"].inc(ins, 1)

            @block.scalar
            def _(sc):
                for k in range(NK):
                    for hc in range(4):
                        par = hc % 2
                        sc.wait_ge(s["pe"].h, PE0 + 12 * k + hc + 1)
                        if hc >= 1:
                            sc.wait_ge(s["ve"].h, VE0 + 16 * k + 2 * hc)
                        elif k > 0:
                            sc.wait_ge(s["ve"].h, VE0 + 16 * (k - 1) + 8)
                        ins = sc.activation(sigb[:, :], gu[:, par * 1024:par * 1024 + 512], AF.Silu)
                        s["se"].inc(ins, 1)

            @block.vector
            def _(v):
                for k in range(NK):
                    i, ti = k // 4, k % 4
                    for hc in range(4):
                        par = hc % 2
                        v.wait_ge(s["se"].h, SE0 + 4 * k + hc + 1)
                        if k > 0:
                            v.wait_ge(s["pe"].h, PE0 + 12 * k)
                        self.edge(v, v.tensor_mul(sigb[:, :], sigb[:, :],
                                                  gu[:, par * 1024 + 512:par * 1024 + 1024]))
                        v.wait_ge(s["rep"].h, REP0 + 16 * (k + 1))
                        ins = v.tensor_mul(hb[:, hc * 512:(hc + 1) * 512], sigb[:, :],
                                           repb[:, (k % 2) * 512:(k % 2 + 1) * 512])
                        s["ve"].inc(ins, 1)
                    for d in range(8):
                        pard = d % 2
                        v.wait_ge(s["pe"].h, PE0 + 12 * k + 4 + d + 1)
                        rrsl = self.ch(self.rr, d)[:, ti * 512:(ti + 1) * 512]
                        ins = v.scalar_tensor_tensor(rrsl, yd[:, pard * 512:(pard + 1) * 512],
                                                     1.0, rrsl, OP.mult, OP.add)
                        s["ve"].inc(ins, 1)

            psl.close()
            loc.close()

    def emit_lntest(self):
        nc, s = self.nc, self.s
        with nc.Block("lntest") as block:
            self.emit_ln(block, fp32_inplace=True)

    def emit_l0attn(self):
        nc, s = self.nc, self.s
        with nc.Block("l0attn") as block:
            self.emit_ln(block)
            ln_ve = s["ve"].v
            ln_se = s["se"].v
            loc = ExitStack()
            sb = lambda n, sh, d: loc.enter_context(nc.sbuf_tensor(self.uname(n), sh, d))
            B = {}
            B["wb"] = sb("a_wb", [128, 2 * 3 * NCH * 128], BF16)
            B["wob"] = sb("a_wob", [128, 2 * NCH * 128], BF16)
            B["bias4"] = sb("a_bias4", [128, 4 * NCH], F32)
            B["qph"] = sb("a_qph", [128, T], BF16)
            B["kph"] = sb("a_kph", [128, T], BF16)
            B["vbf"] = sb("a_vbf", [128, T], BF16)
            B["ktm"] = sb("a_ktm", [128, 16 * 64], BF16)
            B["vtm"] = sb("a_vtm", [128, 16 * 64], BF16)
            B["kvb"] = sb("a_kvb", [128, 64], BF16)
            B["zf"] = sb("a_zf", [128, 8], F32)
            B["zb"] = sb("a_zb", [128, 8], BF16)
            B["denr"] = sb("a_denr", [1, 512], F32)
            B["denb"] = sb("a_denb", [1, 512], BF16)
            B["obf"] = sb("a_obf", [128, 512], BF16)
            B["ao"] = sb("a_ao", [128, NCH * T], BF16)
            psl = ExitStack()
            B["pA"] = psl.enter_context(nc.psum_tensor(self.uname("a_pA"), [128, 1024], F32))
            B["pB"] = psl.enter_context(nc.psum_tensor(self.uname("a_pB"), [128, 1536], F32))
            B["ptp"] = psl.enter_context(nc.psum_tensor(self.uname("a_ptp"), [128, 2048], BF16))
            B["prp"] = psl.enter_context(nc.psum_tensor(self.uname("a_prp"), [128, 512], F32))
            B["wbv"] = B["wb"].ap().rearrange("p (b m kc n) -> p b m kc n", b=2, m=3, kc=NCH)
            B["wov"] = B["wob"].ap().rearrange("p (b kc n) -> p b kc n", b=2, kc=NCH)

            @block.sync
            def _(sy):
                for bi, nm in enumerate(("bq", "bk", "bv", "bo")):
                    ins = sy.dma_start(out=B["bias4"][:, bi * NCH:(bi + 1) * NCH], in_=self.brd[nm])
                    s["ld"].inc(ins, 16)
            B["bias_ld"] = s["ld"].v
            B["marks"] = {}

            @block.gpsimd
            def _(g):
                for c in (0, 1):
                    self._l0_load_qkv(g, c, B)

            st = {"first": True, "ln_ve": ln_ve, "ln_se": ln_se}
            for c in range(NCH):
                self._l0_chunk(block, c, B, st)

            B["womarks"] = {}

            @block.gpsimd
            def _(g):
                for c2 in (0, 1):
                    self._l0_load_wo(g, c2, B)
            wo_ve0 = s["ve"].v
            wo_se0 = s["se"].v
            for c2 in range(NCH):
                self._l0_wo_chunk(block, c2, B, wo_ve0, wo_se0)
            psl.close()
            loc.close()

    def _l0_load_qkv(self, g, c, B):
        s = self.s
        par = c % 2
        for mi, w in enumerate((self.WT["Wq"], self.WT["Wk"], self.WT["Wv"])):
            ins = g.dma_start(
                out=B["wbv"][:, par, mi],
                in_=w[:, c * 128:(c + 1) * 128].rearrange("(kc p) n -> p kc n", p=128))
            s["wq"].inc(ins, 16)
        B["marks"][c] = s["wq"].v

    def _l0_load_wo(self, g, c2, B):
        s = self.s
        par = c2 % 2
        ins = g.dma_start(
            out=B["wov"][:, par],
            in_=self.WT["Wo"][:, c2 * 128:(c2 + 1) * 128].rearrange("(kc p) n -> p kc n", p=128))
        s["wq"].inc(ins, 16)
        B["womarks"][c2] = s["wq"].v

    def _l0_chunk(self, block, c, B, st):
        nc, s = self.nc, self.s
        par_w = c % 2
        PE0 = s["pe"].v
        VE0 = s["ve"].v
        SE0 = s["se"].v
        bias4 = B["bias4"]

        @block.tensor
        def _(t):
            if st["first"]:
                t.wait_ge(s["ve"].h, st["ln_ve"])
                t.wait_ge(s["se"].h, st["ln_se"])
                t.wait_ge(s["ld"].h, B["bias_ld"])
                st["first"] = False
            t.wait_ge(s["wq"].h, B["marks"][c])
            t.wait_ge(s["ve"].h, VE0)
            t.wait_ge(s["se"].h, SE0)
            for g in range(12):
                mi, f = divmod(g, 4)
                fs = slice(f * 512, (f + 1) * 512)
                par = g % 2
                if g >= 2:
                    gp = g - 2
                    if gp < 8:
                        t.wait_ge(s["ve"].h, VE0 + 3 * gp + 2)
                    else:
                        t.wait_ge(s["se"].h, SE0 + gp + 1)
                for kc in range(NCH):
                    ins = t.matmul(B["pA"][:, par * 512:(par + 1) * 512],
                                   B["wbv"][:, par_w, mi, kc, :],
                                   self.ch(self.ubf, kc)[:, fs],
                                   start=(kc == 0), stop=(kc == NCH - 1))
                s["pe"].inc(ins, 1)
            t.wait_ge(s["ve"].h, VE0 + 24)
            t.wait_ge(s["se"].h, SE0 + 12)
            for ho in range(2):
                b = ho * 64
                PH0 = PE0 + 12 + ho * 45
                VH0 = VE0 + 24 + ho * 50
                SH0 = SE0 + 12 + ho * 5
                for i in range(32):
                    src_buf = B["kph"] if i < 16 else B["vbf"]
                    ii = i % 16
                    parp = i % 2
                    if i >= 2:
                        t.wait_ge(s["ve"].h, VH0 + i - 1)
                    ins = t.transpose(B["ptp"][:, parp * 1024:parp * 1024 + 64],
                                      src_buf[b:b + 64, ii * 128:(ii + 1) * 128],
                                      self.identb[b:b + 64, b:b + 64])
                    s["pe"].inc(ins, 1)
                t.wait_ge(s["ve"].h, VH0 + 32)
                if ho == 1:
                    t.wait_ge(s["se"].h, SE0 + 13)
                for i in range(16):
                    ins = t.matmul(B["pB"][b:b + 64, 0:64],
                                   B["ktm"][:, i * 64:(i + 1) * 64],
                                   B["vtm"][:, i * 64:(i + 1) * 64],
                                   start=(i == 0), stop=(i == 15))
                s["pe"].inc(ins, 1)
                for f in range(4):
                    fs = slice(f * 512, (f + 1) * 512)
                    t.wait_ge(s["se"].h, SH0 + 1)
                    if f >= 1:
                        t.wait_ge(s["se"].h, SH0 + 1 + f)
                    ins = t.matmul(B["pB"][b:b + 64, 512:1024], B["kvb"][b:b + 64, :],
                                   B["qph"][b:b + 64, fs], start=True, stop=True)
                    s["pe"].inc(ins, 1)
                    t.wait_ge(s["ve"].h, VH0 + 34)
                    if f >= 1:
                        t.wait_ge(s["ve"].h, VH0 + 34 + 4 * (f - 1) + 1)
                    ins = t.matmul(B["pB"][0:1, 1024:1536], B["zb"][b:b + 64, 0:1],
                                   B["qph"][b:b + 64, fs], start=True, stop=True)
                    s["pe"].inc(ins, 1)
                    if f >= 1:
                        t.wait_ge(s["ve"].h, VH0 + 34 + 4 * (f - 1) + 4)
                    t.wait_ge(s["ve"].h, VH0 + 34 + 4 * f + 3)
                    ins = t.matmul(B["prp"][b:b + 64, :], self.onesb[0:1, 0:64],
                                   B["denb"][0:1, :], start=True, stop=True)
                    s["pe"].inc(ins, 1)

        @block.scalar
        def _(sc):
            for g in range(12):
                mi, f = divmod(g, 4)
                fs = slice(f * 512, (f + 1) * 512)
                par = g % 2
                sc.wait_ge(s["pe"].h, PE0 + g + 1)
                if mi == 2:
                    ins = sc.activation(B["vbf"][:, fs], B["pA"][:, par * 512:(par + 1) * 512],
                                        AF.Identity, bias=bias4[:, 2 * NCH + c:2 * NCH + c + 1])
                else:
                    sc.wait_ge(s["ve"].h, VE0 + 3 * g + 1)
                    ins = sc.activation(self.fta[:, 1024:1536], self.fta[:, 0:512], AF.Exp)
                s["se"].inc(ins, 1)
            for ho in range(2):
                PH0 = PE0 + 12 + ho * 45
                VH0 = VE0 + 24 + ho * 50
                SH0 = SE0 + 12 + ho * 5
                b = ho * 64
                sc.wait_ge(s["pe"].h, PH0 + 33)
                ins = sc.activation(B["kvb"][b:b + 64, :], B["pB"][b:b + 64, 0:64], AF.Copy)
                s["se"].inc(ins, 1)
                for f in range(4):
                    sc.wait_ge(s["pe"].h, PH0 + 33 + 3 * f + 1)
                    if f >= 1:
                        sc.wait_ge(s["ve"].h, VH0 + 34 + 4 * (f - 1) + 4)
                    ins = sc.activation(B["obf"][b:b + 64, :], B["pB"][b:b + 64, 512:1024], AF.Copy)
                    s["se"].inc(ins, 1)

        @block.vector
        def _(v):
            for g in range(8):
                mi, f = divmod(g, 4)
                fs = slice(f * 512, (f + 1) * 512)
                par = g % 2
                bcol = bias4[:, mi * NCH + c:mi * NCH + c + 1]
                v.wait_ge(s["pe"].h, PE0 + g + 1)
                if g >= 1:
                    v.wait_ge(s["se"].h, SE0 + g)
                ins = v.tensor_scalar(self.fta[:, 0:512], B["pA"][:, par * 512:(par + 1) * 512],
                                      bcol, 0.0, OP.add, OP.min)
                s["ve"].inc(ins, 1)
                v.wait_ge(s["ve"].h, s["ve"].v)
                ins = v.tensor_scalar(self.fta[:, 512:1024], B["pA"][:, par * 512:(par + 1) * 512],
                                      bcol, 0.0, OP.add, OP.max)
                s["ve"].inc(ins, 1)
                v.wait_ge(s["ve"].h, s["ve"].v)
                v.wait_ge(s["se"].h, SE0 + g + 1)
                dst = B["qph"] if mi == 0 else B["kph"]
                ins = v.tensor_add(dst[:, fs], self.fta[:, 1024:1536], self.fta[:, 512:1024])
                s["ve"].inc(ins, 1)
            for ho in range(2):
                b = ho * 64
                PH0 = PE0 + 12 + ho * 45
                VH0 = VE0 + 24 + ho * 50
                SH0 = SE0 + 12 + ho * 5
                for i in range(32):
                    parp = i % 2
                    dstb = B["ktm"] if i < 16 else B["vtm"]
                    ii = i % 16
                    v.wait_ge(s["pe"].h, PH0 + i + 1)
                    ins = v.tensor_copy(dstb[:, ii * 64:(ii + 1) * 64],
                                        B["ptp"][:, parp * 1024:parp * 1024 + 64])
                    s["ve"].inc(ins, 1)
                ins = v.tensor_reduce(B["zf"][b:b + 64, 0:1], B["kph"][b:b + 64, :], AX.X, OP.add)
                s["ve"].inc(ins, 1)
                v.wait_ge(s["ve"].h, s["ve"].v)
                ins = v.tensor_copy(B["zb"][b:b + 64, 0:1], B["zf"][b:b + 64, 0:1])
                s["ve"].inc(ins, 1)
                for f in range(4):
                    fs = slice(f * 512, (f + 1) * 512)
                    v.wait_ge(s["pe"].h, PH0 + 33 + 3 * f + 2)
                    ins = v.tensor_scalar(B["denr"][:, :], B["pB"][0:1, 1024:1536],
                                          1e-6, None, OP.add)
                    s["ve"].inc(ins, 1)
                    v.wait_ge(s["ve"].h, s["ve"].v)
                    ins = v.reciprocal(B["denr"][:, :], B["denr"][:, :])
                    s["ve"].inc(ins, 1)
                    v.wait_ge(s["ve"].h, s["ve"].v)
                    ins = v.tensor_copy(B["denb"][:, :], B["denr"][:, :])
                    s["ve"].inc(ins, 1)
                    v.wait_ge(s["ve"].h, s["ve"].v)
                    v.wait_ge(s["pe"].h, PH0 + 33 + 3 * f + 3)
                    v.wait_ge(s["se"].h, SH0 + 1 + f + 1)
                    ins = v.tensor_mul(B["ao"][b:b + 64, c * T + f * 512:c * T + (f + 1) * 512],
                                       B["obf"][b:b + 64, :], B["prp"][b:b + 64, :])
                    s["ve"].inc(ins, 1)

        if c + 2 < NCH:
            @block.gpsimd
            def _(g):
                g.wait_ge(s["pe"].h, PE0 + 12)
                self._l0_load_qkv(g, c + 2, B)

    def _l0_wo_chunk(self, block, c2, B, VEW, SEW):
        nc, s = self.nc, self.s
        par_w = c2 % 2
        PE0 = s["pe"].v
        VE0 = s["ve"].v
        SE0 = s["se"].v

        @block.tensor
        def _(t):
            t.wait_ge(s["wq"].h, B["womarks"][c2])
            if c2 == 0:
                t.wait_ge(s["ve"].h, VEW)
                t.wait_ge(s["se"].h, SEW)
            for f in range(4):
                fs = slice(f * 512, (f + 1) * 512)
                par = f % 2
                if 4 * c2 + f >= 2:
                    t.wait_ge(s["se"].h, SE0 + f - 1)
                for kc in range(NCH):
                    ins = t.matmul(B["pA"][:, par * 512:(par + 1) * 512],
                                   B["wov"][:, par_w, kc, :],
                                   B["ao"][:, kc * T + f * 512:kc * T + (f + 1) * 512],
                                   start=(kc == 0), stop=(kc == NCH - 1))
                s["pe"].inc(ins, 1)

        @block.scalar
        def _(sc):
            for f in range(4):
                par = f % 2
                sc.wait_ge(s["pe"].h, PE0 + f + 1)
                if 4 * c2 + f >= 2:
                    sc.wait_ge(s["ve"].h, VE0 + f - 1)
                ins = sc.activation(self.fta[:, par * 512:(par + 1) * 512],
                                    B["pA"][:, par * 512:(par + 1) * 512],
                                    AF.Identity, bias=B["bias4"][:, 3 * NCH + c2:3 * NCH + c2 + 1])
                s["se"].inc(ins, 1)

        @block.vector
        def _(v):
            for f in range(4):
                par = f % 2
                fs = slice(f * 512, (f + 1) * 512)
                v.wait_ge(s["se"].h, SE0 + f + 1)
                rrsl = self.ch(self.rr, c2)[:, fs]
                ins = v.tensor_add(rrsl, rrsl, self.fta[:, par * 512:(par + 1) * 512])
                s["ve"].inc(ins, 1)

        if c2 + 2 < NCH:
            @block.gpsimd
            def _(g):
                g.wait_ge(s["pe"].h, PE0 + 4)
                self._l0_load_wo(g, c2 + 2, B)

    def emit_l1attn(self):
        nc, s = self.nc, self.s
        with nc.Block("l1attn") as block:
            self.emit_ln(block)
            ln_ve = s["ve"].v
            ln_se = s["se"].v
            loc = ExitStack()
            sb = lambda n, sh, d: loc.enter_context(nc.sbuf_tensor(self.uname(n), sh, d))
            wh = sb("e_wh", [128, 2 * 4 * NCH * 64], BF16)
            qhb = sb("e_qhb", [64, T], BF16)
            khb = sb("e_khb", [64, T], BF16)
            kyb = sb("e_kyb", [64, T], BF16)
            vhb = sb("e_vhb", [64, T], BF16)
            vtm = sb("e_vtm", [128, 16 * 64], BF16)
            Lrow = sb("e_Lrow", [128, T], F32)
            Eyrow = sb("e_Eyrow", [128, T], F32)
            Abf = self.fta.ap().bitcast(BF16)[:, 0:T]
            atb = sb("e_atb", [128, 2 * 128], BF16)
            mb = sb("e_mb", [128, 48], F32)
            mdg = sb("e_mdg", [128, 256], F32)
            ao = sb("e_ao", [128, NCH * T], BF16)
            psl = ExitStack()
            pL = psl.enter_context(nc.psum_tensor(self.uname("e_pL"), [128, 1024], F32))
            pLy = psl.enter_context(nc.psum_tensor(self.uname("e_pLy"), [128, 1024], F32))
            pt = psl.enter_context(nc.psum_tensor(self.uname("e_pt"), [128, 2048], BF16))
            pav = psl.enter_context(nc.psum_tensor(self.uname("e_pav"), [128, 1024], F32))
            whv = wh.ap().rearrange("p (b m kc n) -> p b m kc n", b=2, m=4, kc=NCH)
            wpv = wh.ap()[:, 0:2 * NCH * 128].rearrange("p (b kc n) -> p b kc n", b=2, kc=NCH)

            ev = self.EvProg(self)
            whm = {}
            wsrc = (self.wqkvT, self.wqkvT, self.wkyT, self.wqkvT)


            def em_mdg(sy):
                return sy.dma_start(out=mdg[:, :], in_=self.c_dgmask)
            mdg_ld = ev.add("sync", em_mdg, (), "ld", 16)

            def load_head(h):
                par = h % 2
                for mi in range(4):
                    def em1(g, h=h, par=par, mi=mi):
                        w = (self.wqkvT, self.wqkvT, self.wkyT, self.wqkvT)[mi]
                        col0 = (0, D, 0, 2 * D)[mi] + h * 64
                        return g.dma_start(out=whv[:, par, mi],
                                           in_=w[:, col0:col0 + 64].rearrange("(kc p) n -> p kc n", p=128))
                    waits = []
                    if h >= 2 and mi == 0:
                        waits.append(("pe", whpe[h - 2]))
                    ev.add("gpsimd", em1, waits, "wq", 16)
                whm[h] = ev.val("wq")

            whpe = {}
            evac_se = {}
            first = [True]
            avse = {0: None, 1: None}
            ao_hist = []
            Eyrow2 = self.ftb.ap()[:, 0:T]
            Abf2 = self.fta.ap().bitcast(BF16)[:, T:2 * T]
            last_pt_cp = None
            load_head(0)
            load_head(1)

            for h in range(HEADS):
                par_h = h % 2
                ch, bh = h // 2, (h % 2) * 64
                dstmap = (qhb, khb, kyb, vhb)
                scalemap = (1.0, 0.125, 0.125, 1.0)
                for g in range(16):
                    mi, f = divmod(g, 4)
                    fs = slice(f * 512, (f + 1) * 512)
                    par = g % 2
                    waits = [("wq", whm[h])]
                    if first[0]:
                        waits += [("ve", ln_ve), ("se", ln_se)]
                        first[0] = False
                    key = (h, g - 2) if g >= 2 else (h - 1, 14 + g)
                    if key in evac_se:
                        waits.append(("se", evac_se[key]))

                    def em(t, par_h=par_h, mi=mi, fs=fs, par=par):
                        for kc in range(NCH):
                            ins = t.matmul(pL[0:64, par * 512:(par + 1) * 512],
                                           whv[:, par_h, mi, kc, :],
                                           self.ch(self.ubf, kc)[:, fs],
                                           start=(kc == 0), stop=(kc == NCH - 1))
                        return ins
                    pm = ev.add("tensor", em, waits, "pe", 1)

                    def eme(sc, mi=mi, fs=fs, par=par, dst=dstmap[mi], scl=scalemap[mi]):
                        return sc.activation(dst[:, fs], pL[0:64, par * 512:(par + 1) * 512],
                                             AF.Identity, scale=scl)
                    evac_se[(h, g)] = ev.add("scalar", eme, [("pe", pm)], "se", 1)

                cp_ve = {}
                for i in range(16):
                    par = i % 2
                    waits = [("se", evac_se[(h, 12 + i // 4)])]
                    if i >= 2:
                        waits.append(("ve", cp_ve[i - 2]))
                    elif last_pt_cp is not None:
                        waits.append(("ve", last_pt_cp))

                    def em(t, i=i, par=par):
                        return t.transpose(pt[:, par * 1024:par * 1024 + 64],
                                           vhb[:, i * 128:(i + 1) * 128],
                                           self.identb[0:64, 0:64])
                    pm = ev.add("tensor", em, waits, "pe", 1)

                    def emc(v, i=i, par=par):
                        return v.tensor_copy(vtm[:, i * 64:(i + 1) * 64],
                                             pt[:, par * 1024:par * 1024 + 64])
                    cp_ve[i] = ev.add("vector", emc, [("pe", pm)], "ve", 1)
                last_cp_ve = {h: cp_ve[15]}
                last_pt_cp = cp_ve[15]

                prev_av_se = avse[h % 2]
                for qi in range(16):
                    kjend = (qi + 1) * 128
                    nt = (kjend + 511) // 512
                    qp = qi % 2
                    EY = Eyrow if qp == 0 else Eyrow2
                    AB = Abf if qp == 0 else Abf2
                    mo = qp
                    abs_se = {}
                    clip_ve = {}
                    for kt in range(nt):
                        c0 = kt * 512
                        c1 = min(kjend, c0 + 512)
                        par = kt % 2
                        wL = []
                        wLy = []
                        if kt >= 2:
                            wL.append(("se", abs_se[kt - 2]))
                            wLy.append(("ve", clip_ve[kt - 2]))
                        elif qi > 0:
                            wL.append(("se", last_abs_se))
                            wLy.append(("ve", last_clip_ve))
                        else:
                            wL.append(("se", evac_se[(h, 15)]))
                            if h > 0:
                                wLy.append(("ve", last_clip_ve))

                        def emL(t, qi=qi, c0=c0, c1=c1, par=par):
                            return t.matmul(pL[0:128, par * 512:par * 512 + (c1 - c0)],
                                            qhb[:, qi * 128:(qi + 1) * 128],
                                            khb[:, c0:c1], start=True, stop=True)
                        pmL = ev.add("tensor", emL, wL, "pe", 1)

                        def emA(sc, c0=c0, c1=c1, par=par):
                            return sc.activation(Lrow[:, c0:c1],
                                                 pL[0:128, par * 512:par * 512 + (c1 - c0)], AF.Abs)
                        abs_se[kt] = ev.add("scalar", emA, [("pe", pmL)], "se", 1)

                        def emLy(t, qi=qi, c0=c0, c1=c1, par=par):
                            return t.matmul(pLy[0:128, par * 512:par * 512 + (c1 - c0)],
                                            qhb[:, qi * 128:(qi + 1) * 128],
                                            kyb[:, c0:c1], start=True, stop=True)
                        pmLy = ev.add("tensor", emLy, wLy, "pe", 1)

                        cw = [("pe", pmLy)]
                        if kt == 0 and len(ao_hist) >= 1:
                            cw.append(("se", ao_hist[-1]))
                        def emC(v, c0=c0, c1=c1, par=par, EY=EY):
                            return v.tensor_scalar(EY[:, c0:c1],
                                                   pLy[0:128, par * 512:par * 512 + (c1 - c0)],
                                                   -15.0, 15.0, OP.max, OP.min)
                        clip_ve[kt] = ev.add("vector", emC, cw, "ve", 1)
                    last_abs_se = abs_se[nt - 1]
                    last_clip_ve = clip_ve[nt - 1]

                    dg0 = qi * 128
                    mw = [("ve", last_clip_ve)]
                    if h == 0 and qi == 0:
                        mw.append(("ld", mdg_ld))

                    def emFE(v, dg0=dg0, EY=EY):
                        return v.tensor_tensor(EY[:, dg0:dg0 + 128], EY[:, dg0:dg0 + 128],
                                               mdg[:, 0:128], OP.min)
                    veE = ev.add("vector", emFE, mw, "ve", 1)

                    def emFL(v, dg0=dg0):
                        return v.tensor_tensor(Lrow[:, dg0:dg0 + 128], Lrow[:, dg0:dg0 + 128],
                                               mdg[:, 128:256], OP.min)
                    veL = ev.add("vector", emFL, [("se", last_abs_se), ("ve", veE)], "ve", 1)

                    def emExp(sc, kjend=kjend, EY=EY, mo=mo):
                        return sc.activation(EY[:, 0:kjend], EY[:, 0:kjend], AF.Exp,
                                             accum_out=mb[:, 0 + mo:1 + mo])
                    seE = ev.add("scalar", emExp, [("ve", veE)], "se", 1)

                    def emBn(v, mo=mo):
                        return v.tensor_scalar(mb[:, 8 + mo:9 + mo], mb[:, 0 + mo:1 + mo],
                                               3e-6, None, OP.add)
                    veB = ev.add("vector", emBn, [("se", seE)], "ve", 1)

                    def emLd(sc, mo=mo):
                        return sc.activation(mb[:, 16 + mo:17 + mo], mb[:, 0 + mo:1 + mo], AF.Ln,
                                             bias=self.cstb[:, 1:2])
                    ev.add("scalar", emLd, [("se", seE), ("ve", veB)], "se", 1)

                    def emW(sc, kjend=kjend, EY=EY, mo=mo):
                        return sc.activation(EY[:, 0:kjend], Lrow[:, 0:kjend], AF.Ln,
                                             bias=mb[:, 8 + mo:9 + mo])
                    seW = ev.add("scalar", emW, [("ve", veB), ("ve", veL)], "se", 1)

                    def emWs(v, kjend=kjend, EY=EY, mo=mo):
                        return v.tensor_scalar(EY[:, 0:kjend], EY[:, 0:kjend],
                                               mb[:, 16 + mo:17 + mo], 0.0, OP.subtract, OP.add,
                                               accum_out=mb[:, 24 + mo:25 + mo])
                    veS = ev.add("vector", emWs, [("se", seW)], "ve", 1)

                    def emEps(v, mo=mo):
                        return v.tensor_scalar(mb[:, 32 + mo:33 + mo], mb[:, 24 + mo:25 + mo],
                                               1e-6, None, OP.add)
                    veP = ev.add("vector", emEps, [("ve", veS)], "ve", 1)

                    def emRc(v, mo=mo):
                        return v.reciprocal(mb[:, 40 + mo:41 + mo], mb[:, 32 + mo:33 + mo])
                    veR = ev.add("vector", emRc, [("ve", veP)], "ve", 1)

                    def emAmul(v, kjend=kjend, EY=EY, AB=AB, mo=mo):
                        return v.tensor_scalar(AB[:, 0:kjend], EY[:, 0:kjend],
                                               mb[:, 40 + mo:41 + mo], None, OP.mult)
                    veA = ev.add("vector", emAmul, [("ve", veR)], "ve", 1)

                    cpv = {}
                    for kc in range(qi + 1):
                        par = kc % 2
                        waits = [("ve", veA)]
                        if kc >= 2:
                            waits.append(("ve", cpv[kc - 2]))
                        elif last_pt_cp is not None:
                            waits.append(("ve", last_pt_cp))

                        def emT(t, kc=kc, par=par, AB=AB):
                            return t.transpose(pt[:, par * 1024:par * 1024 + 128],
                                               AB[:, kc * 128:(kc + 1) * 128],
                                               self.identb[:, :])
                        pmT = ev.add("tensor", emT, waits, "pe", 1)

                        def emCp(v, kc=kc, par=par):
                            return v.tensor_copy(atb[:, par * 128:(par + 1) * 128],
                                                 pt[:, par * 1024:par * 1024 + 128])
                        cpv[kc] = ev.add("vector", emCp, [("pe", pmT)], "ve", 1)

                        avw = [("ve", cpv[kc])]
                        if kc == 0 and avse[h % 2] is not None:
                            avw.append(("se", avse[h % 2]))

                        def emAV(t, kc=kc, par=par, qi=qi, bh=bh, hp=h % 2):
                            return t.matmul(pav[bh:bh + 64, hp * 512:hp * 512 + 128],
                                            vtm[:, kc * 64:(kc + 1) * 64],
                                            atb[:, par * 128:(par + 1) * 128],
                                            start=(kc == 0), stop=(kc == qi))
                        pmAV = ev.add("tensor", emAV, avw, "pe", 1)
                    last_pt_cp = cpv[qi]

                    def emEv(sc, bh=bh, ch=ch, qi=qi, hp=h % 2):
                        return sc.activation(ao[bh:bh + 64, ch * T + qi * 128:ch * T + (qi + 1) * 128],
                                             pav[bh:bh + 64, hp * 512:hp * 512 + 128], AF.Copy)
                    prev_av_se = ev.add("scalar", emEv, [("pe", pmAV)], "se", 1)
                    avse[h % 2] = prev_av_se
                    ao_hist.append(prev_av_se)

                if h + 2 < HEADS:
                    whpe[h] = ev.val("pe")
                    load_head(h + 2)
                else:
                    whpe[h] = ev.val("pe")

            wpm = {}

            def load_wp(c2):
                par = c2 % 2
                def em(g, c2=c2, par=par):
                    return g.dma_start(
                        out=wpv[:, par],
                        in_=self.wprojT[:, c2 * 128:(c2 + 1) * 128].rearrange("(kc p) n -> p kc n", p=128))
                waits = []
                if c2 >= 2:
                    waits.append(("pe", wp_pe[c2 - 2]))
                else:
                    waits.append(("pe", whpe[HEADS - 1]))
                ev.add("gpsimd", em, waits, "wq", 16)
                wpm[c2] = ev.val("wq")

            wp_pe = {}
            ao_ve = ev.val("ve")
            ao_se = ev.val("se")
            load_wp(0)
            load_wp(1)
            gse = {}
            gve = {}
            for c2 in range(NCH):
                par_w = c2 % 2
                for f in range(4):
                    fs = slice(f * 512, (f + 1) * 512)
                    par = f % 2
                    gg = 4 * c2 + f
                    waits = [("wq", wpm[c2])]
                    if gg == 0:
                        waits += [("ve", ao_ve), ("se", ao_se)]
                    if gg >= 2:
                        waits.append(("se", gse[gg - 2]))

                    def em(t, par_w=par_w, fs=fs, par=par):
                        for kc in range(NCH):
                            ins = t.matmul(pL[:, par * 512:(par + 1) * 512],
                                           wpv[:, par_w, kc, :],
                                           ao[:, kc * T + fs.start:kc * T + fs.stop],
                                           start=(kc == 0), stop=(kc == NCH - 1))
                        return ins
                    pm = ev.add("tensor", em, waits, "pe", 1)

                    waits = [("pe", pm)]
                    if gg >= 2:
                        waits.append(("ve", gve[gg - 2]))

                    def emc(sc, par=par):
                        return sc.activation(self.fta[:, par * 512:(par + 1) * 512],
                                             pL[:, par * 512:(par + 1) * 512], AF.Copy)
                    gse[gg] = ev.add("scalar", emc, waits, "se", 1)

                    def emv(v, c2=c2, fs=fs, par=par):
                        rrsl = self.ch(self.rr, c2)[:, fs]
                        return v.tensor_add(rrsl, rrsl, self.fta[:, par * 512:(par + 1) * 512])
                    gve[gg] = ev.add("vector", emv, [("se", gse[gg])], "ve", 1)
                wp_pe[c2] = ev.val("pe")
                if c2 + 2 < NCH:
                    load_wp(c2 + 2)

            ev.emit(block)
            psl.close()
            loc.close()

    def emit_ffn(self):
        nc, s = self.nc, self.s
        with nc.Block(self.uname("ffn")) as block:
            sp0 = s["sp"].v

            @block.sync
            def _(sy):
                ins = sy.dma_start(out=self.rspill, in_=self.rr.ap()[:, :])
                s["sp"].inc(ins, 16)

            loc = ExitStack()
            sb = lambda n, sh, d: loc.enter_context(nc.sbuf_tensor(self.uname(n), sh, d))
            AR = sb("f_arena", [128, 36352], BF16)
            arA = AR.ap()
            kroll = arA[:, 0:6144].rearrange("p (kc n) -> p kc n", kc=NCH)
            vtm = arA[:, 6144:12288].rearrange("p (kj h n) -> p kj h n", kj=6, h=HEADS)
            qbf = arA[:, 12288:16384].rearrange("p (c n) -> p c n", c=NCH)
            AOt = arA[:, 16384:20480].rearrange("p (c n) -> p c n", c=NCH)
            wqkvb = arA[:, 20480:26624].rearrange("p (b m kc n) -> p b m kc n", b=2, m=3, kc=NCH)
            wob = arA[:, 26624:28672].rearrange("p (b kc n) -> p b kc n", b=2, kc=NCH)
            vtmp = arA[:, 28672:29184]
            Eb = arA[:, 31232:32256]
            tabb = arA[:, 29184:31232].rearrange("p (m n) -> p m n", m=4)
            nrmv = arA[:, 31232:35328].bitcast(F32)
            recb = arA[0:16, 35328:36352]
            hbuf = arA[:, 0:22528].rearrange("p (j n) -> p j n", j=FJ)
            w1b = arA[:, 22528:26624].rearrange("p (b m kc n) -> p b m kc n", b=2, m=2, kc=NCH)
            w2b = arA[:, 26624:32256].rearrange("p (b kc n) -> p b kc n", b=2, kc=FJ)

            mbs = sb("f_mb", [128, 24], F32)
            zcolb = sb("f_zcol", [128, 16], F32)
            lmb = sb("f_lmb", [128, 1536], BF16)
            lngb = sb("f_lngb", [128, 48], F32)
            lnbb = sb("f_lnbb", [128, 48], F32)
            b2b = sb("f_b2b", [128, 48], F32)
            qsb = sb("f_qsb", [128, 8], F32)
            ksb = sb("f_ksb", [128, 8], F32)
            b1ab = sb("f_b1ab", [128, 132], F32)
            b1gb = sb("f_b1gb", [128, 132], F32)
            selqb = sb("f_selqb", [128, 128], BF16)
            selrb = sb("f_selrb", [16, 1024], BF16)

            psl = ExitStack()
            PS1 = psl.enter_context(nc.psum_tensor(self.uname("f_PS1"), [128, 2048], F32))
            PS2 = psl.enter_context(nc.psum_tensor(self.uname("f_PS2"), [128, 1024], F32))
            PSb = psl.enter_context(nc.psum_tensor(self.uname("f_PSb"), [128, 2048], BF16))

            ld0 = s["ld"].v

            @block.sync
            def _(sy):
                for dst, src in ((lngb[:, :], self.f_lng),
                                 (lnbb[:, :], self.f_lnb),
                                 (b2b[:, :], self.f_b2),
                                 (qsb[:, 0:6], self.f_qs),
                                 (ksb[:, 0:6], self.f_ks),
                                 (b1ab[:, :], self.f_b1a),
                                 (b1gb[:, :], self.f_b1g),
                                 (lmb[:, 0:384], self.c_lmask[0]),
                                 (lmb[:, 384:768], self.c_lmask[1]),
                                 (lmb[:, 768:1152], self.c_lmask0[0]),
                                 (lmb[:, 1152:1536], self.c_lmask0[1]),
                                 (selqb[:, :], self.c_selq),
                                 (selrb[:, :], self.c_selr),
                                 ):
                    ins = sy.dma_start(out=dst, in_=src)
                    s["ld"].inc(ins, 16)

            self.emit_ln(block, fp32_inplace=True,
                         pstat_ext=PS2.ap()[0:1, 0:1024])

            self._st4 = {"tab_rd": None, "cast_last": None, "wo_last_ve": None,
                         "sim_last_pe": None, "av_last_pe": None, "vpe": {},
                         "q_rgn_rd": {0: None, 1: None},
                         "kv_rgn_rd": {0: None, 1: None},
                         "vtmp_rd_pe": None, "psqr": None, "psb_rd": None,
                         "u": 0, "exp_se": {}, "rmax_ve": {}, "msk_ve": {}, "amul_ve": {}, "tr_last_pe": {},
                         "psb_cp": {}, "av_pe": {}, "ao_se": {}, "eb_rd": None}
            for i in range(DBG_FFN_ITERS if DBG_FFN_ITERS else 6):
                self._ffn_attn(block, i, dict(
                    kroll=kroll, vtm=vtm, qbf=qbf, AOt=AOt, wqkvb=wqkvb, wob=wob,
                    vtmp=vtmp, tabb=tabb, nrmv=nrmv, recb=recb, Eb=Eb,
                    mbs=mbs, zcolb=zcolb, lmb=lmb, qsb=qsb, ksb=ksb,
                    selqb=selqb, selrb=selrb, PS1=PS1, PS2=PS2, PSb=PSb, ld0=ld0))
                if DBG_FFN_ITERS and i == DBG_FFN_ITERS - 1:
                    break
                self.emit_ln(block, shift=True,
                             gain_ap=lambda c, i=i: lngb[:, i * 8 + c:i * 8 + c + 1],
                             bias_ap=lambda c, i=i: lnbb[:, i * 8 + c:i * 8 + c + 1],
                             pstat_ext=PS2.ap()[0:1, 0:1024],
                             wait_ve=s["ve"].v, wait_pe=s["pe"].v)
                self._ffn_mlp(block, i, dict(
                    hbuf=hbuf, w1b=w1b, w2b=w2b, b1ab=b1ab, b1gb=b1gb, b2b=b2b,
                    PS1=PS1, PS2=PS2))

            if DBG_FFN_ITERS:
                psl.close()
                loc.close()
                return
            ve_end = s["ve"].v
            pe_end = s["pe"].v
            se_end = s["se"].v
            sp_base = s["sp"].v

            @block.sync
            def _(sy):
                sy.wait_ge(s["ve"].h, ve_end)
                sy.wait_ge(s["pe"].h, pe_end)
                sy.wait_ge(s["se"].h, se_end)
                for c in range(NCH):
                    buf = self.fta if c % 2 == 0 else self.ftb
                    if c >= 2:
                        sy.wait_ge(s["ve"].h, ve_end + c - 1)
                    ins = sy.dma_start(out=buf[:, 0:T],
                                       in_=self.rspill[:, c * T:(c + 1) * T])
                    s["sp"].inc(ins, 16)

            sp_now = sp_base

            @block.vector
            def _(v):
                spv = sp_base
                for c in range(NCH):
                    buf = self.fta if c % 2 == 0 else self.ftb
                    spv += 16
                    v.wait_ge(s["sp"].h, spv)
                    ins = v.scalar_tensor_tensor(self.ch(self.rr, c), self.ch(self.rr, c),
                                                 0.5, buf[:, 0:T], OP.mult, OP.add)
                    s["ve"].inc(ins, 1)

            psl.close()
            loc.close()

    def _ffn_attn(self, block, it, B):
        nc, s = self.nc, self.s
        ev = self.EvProg(self)
        PS1, PS2, PSb = B["PS1"], B["PS2"], B["PSb"]
        kroll, vtm, qbf, AOt = B["kroll"], B["vtm"], B["qbf"], B["AOt"]
        wqkvb, wob, vtmp, tabb = B["wqkvb"], B["wob"], B["vtmp"], B["tabb"]
        nrmv, recb, mbs = B["nrmv"], B["recb"], B["mbs"]
        zcolb, lmb, qsb, ksb = B["zcolb"], B["lmb"], B["qsb"], B["ksb"]
        selqb, selrb, Eb = B["selqb"], B["selrb"], B["Eb"]
        atb = self.ftb.ap().bitcast(BF16)[:, 0:1024]
        if it == 0:
            ev.joins.append(("ld", B["ld0"]))

        st4 = self._st4
        psim_rgn = (PS1.ap()[:, 0:512], PS1.ap()[:, 512:1024],
                    PS2.ap()[:, 0:512], PS2.ap()[:, 512:1024])

        def emZzz(v):
            return v.memset(zcolb[:, 8:16], 0.0)
        ev.add("vector", emZzz, [], "ve", 1)
        if it == 0:
            def emZk(v):
                return v.memset(kroll[:, :, 0:256], 0.0)
            ev.add("vector", emZk, [], "ve", 1)

            def emZv(v):
                return v.memset(vtm[:, 0:2], 0.0)
            ev.add("vector", emZv, [], "ve", 1)

        for tt in range(4):
            ts, te = tt * 512, tt * 512 + 512

            tabw = []
            if st4["tab_rd"] is not None:
                tabw.append(("ve", st4["tab_rd"]))
            for mi, nm in enumerate(("rotAq", "rotBq", "rotAk", "rotBk")):
                def emTab(sy, mi=mi, nm=nm, ts=ts):
                    return sy.dma_start(out=tabb[:, mi, :], in_=self.c_rot[nm][:, ts:ts + 512])
                tab_ld = ev.add("sync", emTab, tabw if mi == 0 else [], "ld", 16)

            def emZc(v, tt=tt, te=te):
                last = None
                for c in range(4, NCH):
                    last = v.tensor_copy(zcolb[:, (tt % 2) * 8 + c:(tt % 2) * 8 + c + 1],
                                         self.ch(self.rr, c)[:, te - 1:te])
                return last
            zcw = []
            if st4["cast_last"] is not None:
                zcw.append(("se", st4["cast_last"]))
            zc_ve = ev.add("vector", emZc, zcw, "ve", 1)

            cast_se = {}
            for c in range(NCH):
                def emCast(sc, c=c, tt=tt, ts=ts, te=te):
                    if c < 4:
                        return sc.copy(self.ch(self.ubf, c)[:, ts:te],
                                       self.ch(self.rr, c)[:, ts:te])
                    sc.copy(self.ch(self.ubf, c)[:, ts:ts + 1],
                            zcolb[:, ((tt - 1) % 2) * 8 + c:((tt - 1) % 2) * 8 + c + 1])
                    return sc.copy(self.ch(self.ubf, c)[:, ts + 1:te],
                                   self.ch(self.rr, c)[:, ts:te - 1])
                w = []
                if c >= 4 and tt > 0:
                    w.append(("ve", zc_prev))
                if st4["wo_last_ve"] is not None:
                    w.append(("ve", st4["wo_last_ve"]))
                cast_se[c] = ev.add("scalar", emCast, w, "se", 1)
            zc_prev = zc_ve
            st4["cast_last"] = cast_se[NCH - 1]

            if tt > 0:
                def emRollK(v):
                    return v.tensor_copy(kroll[:, :, 0:256], kroll[:, :, 512:768])
                rollk_ve = ev.add("vector", emRollK, [("pe", st4["sim_last_pe"])], "ve", 1)

                def emRollV(v):
                    return v.tensor_copy(vtm[:, 0:2], vtm[:, 4:6])
                rollv_ve = ev.add("vector", emRollV, [("pe", st4["av_last_pe"])], "ve", 1)
            else:
                rollk_ve = rollv_ve = None

            evq_se = {}
            evkv_se = {}
            vtm_cp = {}
            psq_pe = psk_pe = None
            for c2 in range(NCH):
                par = c2 % 2
                ww = []
                key = c2 - 2 if c2 >= 2 else c2 + 6
                if key in st4["vpe"]:
                    ww.append(("pe", st4["vpe"][key]))
                for mi in range(3):
                    def emW(g, it=it, c2=c2, par=par, mi=mi):
                        return g.dma_start(
                            out=wqkvb[:, par, mi],
                            in_=self.f_wqkv[it, :, mi * D + c2 * 128:mi * D + (c2 + 1) * 128]
                            .rearrange("(kc p) n -> p kc n", p=128))
                    ev.add("gpsimd", emW, ww if mi == 0 else [], "wq", 16)
                wq_mark = ev.val("wq")

                qw = [("wq", wq_mark), ("se", cast_se[NCH - 1])]
                if (c2 - 2) in evq_se:
                    qw.append(("se", evq_se[c2 - 2]))
                elif st4["q_rgn_rd"][par] is not None:
                    qw.append(("ve", st4["q_rgn_rd"][par]))

                def emQ(t, c2=c2, par=par, ts=ts, te=te):
                    for kc in range(NCH):
                        ins = t.matmul(PS2[:, par * 512:(par + 1) * 512],
                                       wqkvb[:, c2 % 2, 0, kc, :],
                                       self.ch(self.ubf, kc)[:, ts:te],
                                       start=(kc == 0), stop=(kc == NCH - 1))
                    return ins
                qpe = ev.add("tensor", emQ, qw, "pe", 1)

                def emQe(sc, c2=c2, par=par):
                    return sc.activation(qbf[:, c2, :], PS2[:, par * 512:(par + 1) * 512], AF.Copy)
                evq_se[c2] = ev.add("scalar", emQe, [("pe", qpe)], "se", 1)

                vtmpw = [("se", evq_se[c2])]
                if st4["vtmp_rd_pe"] is not None:
                    vtmpw.append(("pe", st4["vtmp_rd_pe"]))
                    st4["vtmp_rd_pe"] = None

                def emQsq(v, c2=c2):
                    return v.tensor_mul(vtmp[:, :], qbf[:, c2, :], qbf[:, c2, :])
                sq_ve = ev.add("vector", emQsq, vtmpw, "ve", 1)

                def emQpsq(t, c2=c2):
                    return t.matmul(PS1[0:16, 1024:1536], selqb[:, c2 * 16:(c2 + 1) * 16],
                                    vtmp[:, :], start=(c2 == 0), stop=(c2 == NCH - 1))
                pqw = [("ve", sq_ve)]
                if c2 == 0 and st4["psqr"] is not None:
                    pqw.append(("se", st4["psqr"]))
                psq_pe = ev.add("tensor", emQpsq, pqw, "pe", 1)

                kw = [("wq", wq_mark)]
                if (c2 - 2) in evkv_se:
                    kw.append(("se", evkv_se[c2 - 2][1]))
                elif st4["kv_rgn_rd"][par] is not None:
                    kw.append(("ve", st4["kv_rgn_rd"][par]))
                if rollk_ve is not None and c2 == 0:
                    kw.append(("ve", rollk_ve))

                def emK(t, c2=c2, par=par, ts=ts, te=te):
                    for kc in range(NCH):
                        ins = t.matmul(PS1[:, par * 512:(par + 1) * 512],
                                       wqkvb[:, c2 % 2, 1, kc, :],
                                       self.ch(self.ubf, kc)[:, ts:te],
                                       start=(kc == 0), stop=(kc == NCH - 1))
                    return ins
                kpe = ev.add("tensor", emK, kw, "pe", 1)

                kew = [("pe", kpe)]
                if rollk_ve is not None:
                    kew.append(("ve", rollk_ve))

                def emKe(sc, c2=c2, par=par):
                    return sc.activation(kroll[:, c2, 256:768], PS1[:, par * 512:(par + 1) * 512], AF.Copy)
                ke_se = ev.add("scalar", emKe, kew, "se", 1)

                def emKsq(v, c2=c2):
                    return v.tensor_mul(vtmp[:, :], kroll[:, c2, 256:768], kroll[:, c2, 256:768])
                ksq_ve = ev.add("vector", emKsq, [("se", ke_se), ("pe", psq_pe)], "ve", 1)

                def emKpsq(t, c2=c2):
                    return t.matmul(PS1[0:16, 1536:2048], selqb[:, c2 * 16:(c2 + 1) * 16],
                                    vtmp[:, :], start=(c2 == 0), stop=(c2 == NCH - 1))
                psk_pe = ev.add("tensor", emKpsq, [("ve", ksq_ve)], "pe", 1)

                def emV(t, c2=c2, par=par, ts=ts, te=te):
                    for kc in range(NCH):
                        ins = t.matmul(PS1[:, par * 512:(par + 1) * 512],
                                       wqkvb[:, c2 % 2, 2, kc, :],
                                       self.ch(self.ubf, kc)[:, ts:te],
                                       start=(kc == 0), stop=(kc == NCH - 1))
                    return ins
                vpe = ev.add("tensor", emV, [("se", ke_se)], "pe", 1)

                def emVe(sc, c2=c2, par=par):
                    return sc.activation(vtmp[:, :], PS1[:, par * 512:(par + 1) * 512], AF.Copy)
                ve_se = ev.add("scalar", emVe, [("pe", vpe), ("pe", psk_pe)], "se", 1)
                evkv_se[c2] = (ke_se, ve_se)

                for ho in range(2):
                    h = 2 * c2 + ho
                    b = ho * 64
                    for tch in range(4):
                        ti = ho * 4 + tch
                        parp = (c2 * 8 + ti) % 2
                        tw = [("se", ve_se)]
                        gkey = c2 * 8 + ti - 2
                        if gkey >= 0 and gkey in vtm_cp:
                            tw.append(("se", vtm_cp[gkey]))
                        elif st4.get("psb_rd_se") is not None and gkey < 0:
                            tw.append(("se", st4["psb_rd_se"]))
                        if rollv_ve is not None and c2 == 0 and ti == 0:
                            tw.append(("ve", rollv_ve))

                        def emT(t, b=b, tch=tch, parp=parp):
                            return t.transpose(PSb[:, parp * 1024:parp * 1024 + 64],
                                               vtmp[b:b + 64, tch * 128:(tch + 1) * 128],
                                               self.identb[b:b + 64, b:b + 64])
                        tpe = ev.add("tensor", emT, tw, "pe", 1)

                        def emTc(sc, h=h, tch=tch, parp=parp):
                            return sc.copy(vtm[:, 2 + tch, h, :],
                                           PSb[:, parp * 1024:parp * 1024 + 64])
                        vtm_cp[c2 * 8 + ti] = ev.add("scalar", emTc, [("pe", tpe)], "se", 1)
                st4["vpe"][c2] = ev.val("pe")
            st4["vtmp_rd_pe"] = ev.val("pe")
            vtm_cp_last = vtm_cp[NCH * 8 - 1]

            def emSqQ(sc):
                return sc.activation(nrmv[0:16, 0:512], PS1[0:16, 1024:1536], AF.Sqrt)
            sqw = [("pe", psq_pe)]
            if st4.get("eb_rd") is not None:
                sqw.append(("ve", st4["eb_rd"]))
            sq_se = ev.add("scalar", emSqQ, sqw, "se", 1)

            def emSqK(sc):
                return sc.activation(nrmv[0:16, 512:1024], PS1[0:16, 1536:2048], AF.Sqrt)
            sk_se = ev.add("scalar", emSqK, [("pe", psk_pe)], "se", 1)

            def emMaxQ(v):
                return v.tensor_scalar_max(nrmv[0:16, 0:1024], nrmv[0:16, 0:1024], 1e-12)
            mx_ve = ev.add("vector", emMaxQ, [("se", sq_se), ("se", sk_se)], "ve", 1)

            def emRec(v):
                return v.reciprocal(nrmv[0:16, 1024:2048], nrmv[0:16, 0:1024])
            rc_ve = ev.add("vector", emRec, [("ve", mx_ve)], "ve", 1)

            def emRecB(v):
                return v.tensor_copy(recb[:, :], nrmv[0:16, 1024:2048])
            rb_ve = ev.add("vector", emRecB, [("ve", rc_ve)], "ve", 1)

            rot_done_ve = None
            prep_rd = {}
            for c2 in range(NCH):
                for which in (0, 1):
                    dst = qbf[:, c2, :] if which == 0 else kroll[:, c2, 256:768]
                    tabA = tabb[:, 0, :] if which == 0 else tabb[:, 2, :]
                    tabB = tabb[:, 1, :] if which == 0 else tabb[:, 3, :]
                    scl = qsb if which == 0 else ksb
                    roff = 0 if which == 0 else 512
                    u = c2 * 2 + which

                    pw = [("ve", rb_ve)]
                    if u >= 1:
                        pw.append(("ve", prep_rd[u - 1]))
                    elif True:
                        pw.append(("se", evkv_se[NCH - 1][1]))

                    def emRep(t, c2=c2, roff=roff):
                        return t.matmul(PS1[:, 0:512], selrb[:, c2 * 128:(c2 + 1) * 128],
                                        recb[:, roff:roff + 512], start=True, stop=True)
                    prep_pe = ev.add("tensor", emRep, pw, "pe", 1)

                    def emQn(v, dst=dst, scl=scl, it=it):
                        return v.scalar_tensor_tensor(dst, PS1[:, 0:512],
                                                      scl[:, it:it + 1], dst,
                                                      OP.mult, OP.mult)
                    qn_ve = ev.add("vector", emQn, [("pe", prep_pe), ("ld", tab_ld)], "ve", 1)
                    prep_rd[u] = qn_ve

                    sww = [("ve", qn_ve)]
                    if u >= 1:
                        sww.append(("ve", t2_rd_prev))

                    def emSw(t, dst=dst):
                        return t.matmul(PS1[:, 512:1024], self.pswapb[:, :], dst,
                                        start=True, stop=True)
                    pswp_pe = ev.add("tensor", emSw, sww, "pe", 1)

                    def emT1(v, dst=dst, tabA=tabA):
                        return v.tensor_mul(self.fta[:, 0:512], dst, tabA)
                    t1_ve = ev.add("vector", emT1, [("ve", qn_ve)], "ve", 1)

                    def emT2(v, tabB=tabB):
                        return v.tensor_mul(self.fta[:, 512:1024], PS1[:, 512:1024], tabB)
                    t2_ve = ev.add("vector", emT2, [("pe", pswp_pe), ("ve", t1_ve)], "ve", 1)
                    t2_rd_prev = t2_ve

                    def emRot(v, dst=dst):
                        return v.tensor_add(dst, self.fta[:, 0:512], self.fta[:, 512:1024])
                    rot_done_ve = ev.add("vector", emRot, [("ve", t2_ve), ("pe", pswp_pe)], "ve", 1)
                    prep_rd[u] = rot_done_ve
            st4["tab_rd"] = rot_done_ve

            for h in range(HEADS):
                ch2, bh = h // 2, (h % 2) * 64
                for wi in range(2):
                    w_glob = 2 * tt + wi
                    for qc in range(2):
                        wstart, wlen = wi * 256 + qc * 128, 384
                        kj0 = (wstart) // 128
                        g = st4["u"]
                        pu4 = g % 4
                        pu2 = g % 2
                        mslice = (lmb[:, 768 + qc * 384:1152 + qc * 384] if w_glob == 0
                                  else lmb[:, qc * 384:(qc + 1) * 384])
                        simw = [("ve", rot_done_ve)]
                        if g - 4 in st4["msk_ve"]:
                            simw.append(("ve", st4["msk_ve"][g - 4]))
                        if g - 1 in st4["rmax_ve"]:
                            simw.append(("ve", st4["rmax_ve"][g - 1]))

                        def emSim(t, bh=bh, ch2=ch2, wi=wi, qc=qc, wstart=wstart,
                                  wlen=wlen, pu4=pu4):
                            return t.matmul(psim_rgn[pu4][:, 0:wlen],
                                            qbf[bh:bh + 64, ch2, wi * 256 + qc * 128:wi * 256 + (qc + 1) * 128],
                                            kroll[bh:bh + 64, ch2, wstart:wstart + wlen],
                                            start=True, stop=True)
                        sim_pe = ev.add("tensor", emSim, simw, "pe", 1)
                        st4["sim_last_pe"] = sim_pe

                        mkw = [("pe", sim_pe)]
                        if g - 2 in st4["exp_se"]:
                            mkw.append(("se", st4["exp_se"][g - 2]))
                        def emMk(v, pu4=pu4, pu2=pu2, wlen=wlen, mslice=mslice):
                            return v.tensor_add(
                                self.fta[:, 1024 + pu2 * 512:1024 + pu2 * 512 + wlen],
                                psim_rgn[pu4][:, 0:wlen], mslice[:, 0:wlen])
                        st4["msk_ve"][g] = ev.add("vector", emMk, mkw, "ve", 1)

                        def emRm(v, pu2=pu2, wlen=wlen):
                            return v.tensor_reduce(mbs[:, pu2:pu2 + 1],
                                                   self.fta[:, 1024 + pu2 * 512:1024 + pu2 * 512 + wlen],
                                                   AX.X, OP.max, negate=True)
                        st4["rmax_ve"][g] = ev.add("vector", emRm,
                                                   [("ve", st4["msk_ve"][g])], "ve", 1)

                        expw = [("ve", st4["rmax_ve"][g])]
                        if g - 2 in st4["tr_last_pe"]:
                            expw.append(("pe", st4["tr_last_pe"][g - 2]))

                        def emExp(sc, pu2=pu2, wlen=wlen):
                            return sc.activation(Eb[:, pu2 * 512:pu2 * 512 + wlen],
                                                 self.fta[:, 1024 + pu2 * 512:1024 + pu2 * 512 + wlen],
                                                 AF.Exp, bias=mbs[:, pu2:pu2 + 1],
                                                 accum_out=mbs[:, 8 * (1 + pu2):8 * (1 + pu2) + 1])
                        st4["exp_se"][g] = ev.add("scalar", emExp, expw, "se", 1)

                        def emRc2(v, pu2=pu2):
                            return v.reciprocal(mbs[:, 4 + pu2:5 + pu2],
                                                mbs[:, 8 * (1 + pu2):8 * (1 + pu2) + 1])
                        rc2_ve = ev.add("vector", emRc2, [("se", st4["exp_se"][g])], "ve", 1)

                        def emAm(v, pu2=pu2, wlen=wlen):
                            return v.tensor_scalar(Eb[:, pu2 * 512:pu2 * 512 + wlen],
                                                   Eb[:, pu2 * 512:pu2 * 512 + wlen],
                                                   mbs[:, 4 + pu2:5 + pu2], None, OP.mult)
                        amul_ve = ev.add("vector", emAm, [("ve", rc2_ve)], "ve", 1)

                        ntile = 3
                        tlast = None
                        for j in range(ntile):
                            tw = [("ve", amul_ve)]
                            if j == 0 and g - 1 in st4["psb_cp"]:
                                tw.append(("se", st4["psb_cp"][g - 1]))
                            if j == 0 and h == 0 and wi == 0 and qc == 0:
                                tw.append(("se", vtm_cp_last))

                            def emTA(t, j=j, pu2=pu2):
                                return t.transpose(PSb[:, pu2 * 1024 + j * 128:pu2 * 1024 + (j + 1) * 128],
                                                   Eb[:, pu2 * 512 + j * 128:pu2 * 512 + (j + 1) * 128],
                                                   self.identb[:, :])
                            tlast = ev.add("tensor", emTA, tw, "pe", 1)
                        st4["tr_last_pe"][g] = tlast

                        cpw = [("pe", tlast)]
                        if g - 2 in st4["av_pe"]:
                            cpw.append(("pe", st4["av_pe"][g - 2]))

                        def emCpA(sc, pu2=pu2, wlen=wlen):
                            return sc.copy(atb[:, pu2 * 512:pu2 * 512 + wlen],
                                           PSb[:, pu2 * 1024:pu2 * 1024 + wlen])
                        st4["psb_cp"][g] = ev.add("scalar", emCpA, cpw, "se", 1)

                        avw = [("se", st4["psb_cp"][g])]
                        if g - 2 in st4["ao_se"]:
                            avw.append(("se", st4["ao_se"][g - 2]))

                        def emAV(t, kj0=kj0, h=h, bh=bh, pu2=pu2, ntile=ntile):
                            for j in range(ntile):
                                ins = t.matmul(PS1[bh:bh + 64, 1024 + pu2 * 512:1024 + pu2 * 512 + 128],
                                               vtm[:, kj0 + j, h, :],
                                               atb[:, pu2 * 512 + j * 128:pu2 * 512 + (j + 1) * 128],
                                               start=(j == 0), stop=(j == ntile - 1))
                            return ins
                        st4["av_pe"][g] = ev.add("tensor", emAV, avw, "pe", 1)
                        st4["av_last_pe"] = st4["av_pe"][g]

                        def emAOe(sc, bh=bh, ch2=ch2, wi=wi, qc=qc, pu2=pu2):
                            return sc.activation(AOt[bh:bh + 64, ch2, wi * 256 + qc * 128:wi * 256 + (qc + 1) * 128],
                                                 PS1[bh:bh + 64, 1024 + pu2 * 512:1024 + pu2 * 512 + 128],
                                                 AF.Copy)
                        st4["ao_se"][g] = ev.add("scalar", emAOe, [("pe", st4["av_pe"][g])], "se", 1)
                        st4["u"] += 1
            ao_all = ev.val("se")
            st4["eb_rd"] = ev.val("ve")
            st4["psqr"] = st4["ao_se"][st4["u"] - 1]
            st4["psb_rd"] = None
            st4["psb_rd_se"] = st4["psb_cp"][st4["u"] - 1]

            wo_pe = {}
            wo_ve = {}
            for c2 in range(NCH):
                par = c2 % 2

                def emWoL(g_, it=it, c2=c2, par=par):
                    return g_.dma_start(out=wob[:, par],
                                        in_=self.f_wo[it, :, c2 * 128:(c2 + 1) * 128]
                                        .rearrange("(kc p) n -> p kc n", p=128))
                ww = []
                if c2 >= 2:
                    ww.append(("pe", wo_pe[c2 - 2]))
                ev.add("gpsimd", emWoL, ww, "wq", 16)
                wo_mark = ev.val("wq")

                mmw = [("wq", wo_mark), ("se", ao_all)]
                if c2 >= 2:
                    mmw.append(("ve", wo_ve[c2 - 2]))
                else:
                    mmw.append(("ve", ev.val("ve")))
                def emWo(t, c2=c2, par=par):
                    for kc in range(NCH):
                        ins = t.matmul(PS1[:, par * 512:(par + 1) * 512],
                                       wob[:, par, kc, :], AOt[:, kc, :],
                                       start=(kc == 0), stop=(kc == NCH - 1))
                    return ins
                wo_pe[c2] = ev.add("tensor", emWo, mmw, "pe", 1)

                def emZu(v, c2=c2, par=par, ts=ts, te=te):
                    rrsl = self.ch(self.rr, c2)[:, ts:te]
                    return v.scalar_tensor_tensor(rrsl, PS1[:, par * 512:(par + 1) * 512],
                                                  1.0, rrsl, OP.mult, OP.add)
                wo_ve[c2] = ev.add("vector", emZu, [("pe", wo_pe[c2])], "ve", 1)
            st4["wo_last_ve"] = wo_ve[NCH - 1]
            st4["q_rgn_rd"] = {0: None, 1: None}
            st4["kv_rgn_rd"] = {0: wo_ve[NCH - 2], 1: wo_ve[NCH - 1]}
            st4["q_rgn_rd"][0] = ev.val("ve")
            st4["q_rgn_rd"][1] = ev.val("ve")

        ev.emit(block)

    def _ffn_mlp(self, block, it, B):
        nc, s = self.nc, self.s
        ev = self.EvProg(self)
        PS1, PS2 = B["PS1"], B["PS2"]
        hbuf, w1b, w2b = B["hbuf"], B["w1b"], B["w2b"]
        b1ab, b1gb, b2b = B["b1ab"], B["b1gb"], B["b2b"]

        for half in range(2):
            hs = half * 1024
            hv_ve = {}
            gl_se = {}
            hw_pe = {}
            for j in range(FJ):
                parw = j % 2
                ww = []
                if j >= 2:
                    ww.append(("pe", hw_pe[(j - 2, 1)]))

                def emW1a(g, it=it, j=j, parw=parw):
                    return g.dma_start(out=w1b[:, parw, 0], in_=self.f_w1a[it, :, :, j, :])

                def emW1g(g, it=it, j=j, parw=parw):
                    return g.dma_start(out=w1b[:, parw, 1], in_=self.f_w1g[it, :, :, j, :])
                ev.add("gpsimd", emW1a, ww, "wq", 16)
                ev.add("gpsimd", emW1g, [], "wq", 16)
                w1_mark = ev.val("wq")

                for fsub in range(2):
                    gidx = 2 * j + fsub
                    para = gidx % 2
                    mmw = [("wq", w1_mark)]
                    if (j - 1, fsub) in hv_ve:
                        mmw.append(("ve", hv_ve[(j - 1, fsub)]))

                    def emA(t, j=j, parw=parw, fsub=fsub, para=para, hs=hs):
                        for kc in range(NCH):
                            t.matmul(PS1[:, para * 512:(para + 1) * 512],
                                     w1b[:, parw, 0, kc, :],
                                     self.ch(self.ubf, kc)[:, hs + fsub * 512:hs + (fsub + 1) * 512],
                                     start=(kc == 0), stop=(kc == NCH - 1))
                        for kc in range(NCH):
                            ins = t.matmul(PS1[:, 1024 + para * 512:1024 + (para + 1) * 512],
                                           w1b[:, parw, 1, kc, :],
                                           self.ch(self.ubf, kc)[:, hs + fsub * 512:hs + (fsub + 1) * 512],
                                           start=(kc == 0), stop=(kc == NCH - 1))
                        return ins
                    hw_pe[(j, fsub)] = ev.add("tensor", emA, mmw, "pe", 1)

                    gw = [("pe", hw_pe[(j, fsub)])]
                    if (j - 1, fsub) in hv_ve:
                        gw.append(("ve", hv_ve[(j - 1, fsub)]))

                    def emG(sc, it=it, j=j, fsub=fsub, para=para):
                        return sc.activation(self.fta[:, fsub * 512:(fsub + 1) * 512],
                                             PS1[:, 1024 + para * 512:1024 + (para + 1) * 512],
                                             AF.Gelu,
                                             bias=b1gb[:, it * FJ + j:it * FJ + j + 1])
                    g_se = ev.add("scalar", emG, gw, "se", 1)
                    gl_se[(j, fsub)] = g_se

                    def emH(v, it=it, j=j, fsub=fsub, para=para):
                        return v.scalar_tensor_tensor(hbuf[:, j, fsub * 512:(fsub + 1) * 512],
                                                      PS1[:, para * 512:(para + 1) * 512],
                                                      b1ab[:, it * FJ + j:it * FJ + j + 1],
                                                      self.fta[:, fsub * 512:(fsub + 1) * 512],
                                                      OP.add, OP.mult)
                    hv_ve[(j, fsub)] = ev.add("vector", emH, [("se", g_se)], "ve", 1)
            h_all = ev.val("ve")

            yw_pe = {}
            y_se = {}
            y_ve = {}
            for dout in range(NCH):
                par = dout % 2
                ww = []
                if dout >= 2:
                    ww.append(("pe", yw_pe[dout - 2]))

                def emW2(g, it=it, dout=dout, par=par):
                    return g.dma_start(out=w2b[:, par],
                                       in_=self.f_w2[it, :, :, dout * 128:(dout + 1) * 128])
                ev.add("gpsimd", emW2, ww, "wq", 16)
                w2_mark = ev.val("wq")

                mmw = [("wq", w2_mark), ("ve", h_all)]
                if dout >= 2:
                    mmw.append(("se", y_se[dout - 2]))

                def emY(t, dout=dout, par=par):
                    for fsub in range(2):
                        for kc in range(FJ):
                            ins = t.matmul(PS1[:, par * 1024 + fsub * 512:par * 1024 + (fsub + 1) * 512],
                                           w2b[:, par, kc, :],
                                           hbuf[:, kc, fsub * 512:(fsub + 1) * 512],
                                           start=(kc == 0), stop=(kc == FJ - 1))
                    return ins
                yw_pe[dout] = ev.add("tensor", emY, mmw, "pe", 1)

                yew = [("pe", yw_pe[dout])]
                if dout >= 1:
                    yew.append(("ve", y_ve[dout - 1]))

                def emYe(sc, it=it, dout=dout, par=par):
                    return sc.activation(self.ftb[:, 0:1024], PS1[:, par * 1024:par * 1024 + 1024],
                                         AF.Identity, bias=b2b[:, it * 8 + dout:it * 8 + dout + 1])
                y_se[dout] = ev.add("scalar", emYe, yew, "se", 1)

                def emZa(v, dout=dout, hs=hs):
                    rrsl = self.ch(self.rr, dout)[:, hs:hs + 1024]
                    return v.tensor_add(rrsl, rrsl, self.ftb[:, 0:1024])
                y_ve[dout] = ev.add("vector", emZa, [("se", y_se[dout])], "ve", 1)

        ev.emit(block)

    def build(self):
        self.emit_preload()
        for idx, st in enumerate(self.stages):
            with self.nc.named_scope(f"{idx}_{st}"):
                getattr(self, "emit_" + st)()
        self.emit_store()
        self.glob.close()
        return self.nc

    def emit_store(self):
        nc, s = self.nc, self.s
        with nc.Block("store") as block:
            @block.sync
            def _(sy):
                ins = sy.dma_start(out=self.out.rearrange("(c p) n -> p c n", p=128),
                                   in_=self.rr.ap().rearrange("p (c n) -> p c n", c=NCH))
                s["st"].inc(ins, 16)
                sy.wait_ge(s["st"].h, s["st"].v)


def build_program(stages=ALL_STAGES, detect_races=True):
    nc = bass.Bass("TRN2", target_bir_lowering=False, detect_race_conditions=detect_races)
    p = Prog(nc, stages)
    p.build()
    return nc


def make_in_maps(inputs):
    consts = host_constants()
    lay = host_layout(inputs)
    xs = inputs["L0_attn_x"]
    in_maps = []
    for b in range(xs.shape[0]):
        m = {"x": np.ascontiguousarray(xs[b].T)}
        m.update(lay)
        for k, v in consts.items():
            m["c_" + k] = v
        in_maps.append(m)
    return in_maps


def kernel(**inputs):
    nc = build_program()
    in_maps = make_in_maps(inputs)
    res = run_bass_kernel_spmd(nc, in_maps, core_ids=list(range(8)))
    out = np.stack([np.asarray(r["out"]).T for r in res.results], 0)
    return np.ascontiguousarray(out.astype(np.float32))


# version 10
# speedup vs baseline: 1.1615x; 1.1615x over previous
import numpy as np
import ml_dtypes
from contextlib import ExitStack

import concourse.bass as bass
import concourse.mybir as mybir
from concourse.bass_utils import run_bass_kernel_spmd

F32 = mybir.dt.float32
BF16 = mybir.dt.bfloat16
AF = mybir.ActivationFunctionType
OP = mybir.AluOpType
AX = mybir.AxisListType
BFNP = ml_dtypes.bfloat16

T = 2048
D = 1024
NCH = 8
HEADS = 16
HD = 64
NE = 32
MOE_H = 512
FJ = 22
DI = 2730
NEGBIG = -3.0e38

ALL_STAGES = ("moe", "l0attn", "moe", "ffn", "l1attn", "ffn")
DBG_FFN_ITERS = 0


def host_constants():
    c = {}
    c["ident"] = np.eye(128, dtype=BFNP)
    c["identf"] = np.eye(128, dtype=np.float32)
    c["ones"] = np.ones((128, 128), dtype=BFNP)

    P = np.zeros((128, 128), dtype=BFNP)
    for r in range(128):
        g = (r // 64) * 64
        P[g + (r - g + 32) % 64, r] = 1
    c["pswap"] = P

    n = T
    half = np.arange(0, HD, 2, dtype=np.float64)
    inv_freq = 1.0 / 10000.0 ** (half / HD)
    t = np.arange(n, dtype=np.float64)
    freqs = np.concatenate([t[:, None] * inv_freq[None, :]] * 2, -1)
    base_scale = (half + 0.4 * HD) / (1.4 * HD)
    power = (t - n // 2) / (256 // 2)
    scale = np.concatenate([base_scale[None, :] ** power[:, None]] * 2, -1)
    sgn = np.where(np.arange(HD) < 32, -1.0, 1.0)
    A_q = (np.cos(freqs) * scale).T * 8.0
    B_q = (np.sin(freqs) * scale).T * sgn[:, None] * 8.0
    A_k = (np.cos(freqs) / scale).T
    B_k = (np.sin(freqs) / scale).T * sgn[:, None]
    for nm, v in (("rotAq", A_q), ("rotBq", B_q), ("rotAk", A_k), ("rotBk", B_k)):
        c[nm] = np.ascontiguousarray(np.concatenate([v, v], 0).astype(BFNP))

    i = np.arange(128)[:, None]
    jd = np.arange(128)[None, :]
    keep = (jd <= i)
    c["dgmask"] = np.concatenate([
        np.where(keep, 3.0e38, -3.0e38),
        np.where(keep, 3.0e38, -1e-6),
    ], axis=1).astype(np.float32)
    for qc in (0, 1):
        i = np.arange(128)[:, None]
        j = np.arange(384)[None, :]
        dlt = 256 + i - j
        c[f"lmask{qc}"] = np.where((dlt >= 0) & (dlt <= 256), 0.0, NEGBIG).astype(BFNP)
        kjg = j - 256 + qc * 128
        c[f"lmask0_{qc}"] = np.where((dlt >= 0) & (dlt <= 256) & (kjg >= 0), 0.0, NEGBIG).astype(BFNP)

    selq = np.zeros((NCH, 128, HEADS), dtype=BFNP)
    selr = np.zeros((NCH, HEADS, 128), dtype=BFNP)
    for ci in range(NCH):
        selq[ci, 0:64, 2 * ci] = 1
        selq[ci, 64:128, 2 * ci + 1] = 1
        selr[ci, 2 * ci, 0:64] = 1
        selr[ci, 2 * ci + 1, 64:128] = 1
    c["selq"] = np.ascontiguousarray(selq.transpose(1, 0, 2).reshape(128, NCH * HEADS))
    c["selr"] = np.ascontiguousarray(selr.transpose(1, 0, 2).reshape(HEADS, NCH * 128))
    return c


def host_layout(inp):
    d = {}
    d["r0"] = inp["L0_ffn_router_w0"]
    d["r1"] = inp["L0_ffn_router_w1"]
    for g in (0, 1):
        d[f"wg{g}"] = inp[f"L0_ffn_wg{g}"]
        d[f"wu{g}"] = inp[f"L0_ffn_wu{g}"]
        d[f"wd{g}"] = inp[f"L0_ffn_wd{g}"]
    for nm in ("Wq", "Wk", "Wv", "Wo"):
        d[nm + "T"] = np.ascontiguousarray(inp[f"L0_attn_{nm}"].T)
    for nm in ("bq", "bk", "bv", "bo"):
        d[nm + "r"] = np.ascontiguousarray(inp[f"L0_attn_{nm}"].reshape(NCH, 128).T)
    d["wqkvT"] = np.ascontiguousarray(inp["L1_attn_w_qkv"].T)
    d["wkyT"] = np.ascontiguousarray(inp["L1_attn_w_ky"].T)
    d["wprojT"] = np.ascontiguousarray(inp["L1_attn_w_proj"].T)

    d["f_wqkv"] = inp["L1_ffn_Wqkv"]
    d["f_wo"] = inp["L1_ffn_Wo"]
    d["f_qs"] = np.ascontiguousarray(np.tile(inp["L1_ffn_q_scale"], (1, 2)).T)
    d["f_ks"] = np.ascontiguousarray(np.tile(inp["L1_ffn_k_scale"], (1, 2)).T)
    d["f_lng"] = np.ascontiguousarray(
        inp["L1_ffn_ln_g"].reshape(6, NCH, 128).transpose(2, 0, 1).reshape(128, 48))
    d["f_lnb"] = np.ascontiguousarray(
        inp["L1_ffn_ln_b"].reshape(6, NCH, 128).transpose(2, 0, 1).reshape(128, 48))
    d["f_b2"] = np.ascontiguousarray(
        inp["L1_ffn_b2"].reshape(6, NCH, 128).transpose(2, 0, 1).reshape(128, 48))

    W1 = inp["L1_ffn_W1"]
    b1 = inp["L1_ffn_b1"]
    W1a = np.zeros((6, D, FJ * 128), dtype=np.float32)
    W1g = np.zeros((6, D, FJ * 128), dtype=np.float32)
    W1a[:, :, :DI] = W1[:, :, :DI]
    W1g[:, :, :DI] = W1[:, :, DI:]
    b1a = np.zeros((6, FJ * 128), dtype=np.float32)
    b1g = np.zeros((6, FJ * 128), dtype=np.float32)
    b1a[:, :DI] = b1[:, :DI]
    b1g[:, :DI] = b1[:, DI:]
    d["f_w1a"] = np.ascontiguousarray(W1a.reshape(6, NCH, 128, FJ, 128).transpose(0, 2, 1, 3, 4))
    d["f_w1g"] = np.ascontiguousarray(W1g.reshape(6, NCH, 128, FJ, 128).transpose(0, 2, 1, 3, 4))
    d["f_b1a"] = np.ascontiguousarray(b1a.reshape(6, FJ, 128).transpose(2, 0, 1).reshape(128, 132))
    d["f_b1g"] = np.ascontiguousarray(b1g.reshape(6, FJ, 128).transpose(2, 0, 1).reshape(128, 132))
    W2p = np.zeros((6, FJ * 128, D), dtype=np.float32)
    W2p[:, :DI, :] = inp["L1_ffn_W2"]
    d["f_w2"] = np.ascontiguousarray(W2p.reshape(6, FJ, 128, D).transpose(0, 2, 1, 3))
    return d


class Sem:
    def __init__(self, nc, name):
        self.h = nc.alloc_semaphore(name)
        self.v = 0

    def inc(self, ins, n=1):
        ins.then_inc(self.h, n)
        self.v += n
        return self.v


class Prog:
    def __init__(self, nc, stages):
        self.nc = nc
        self.stages = stages

        def din(name, shape, dtype=F32):
            return nc.dram_tensor(name, list(shape), dtype, kind="ExternalInput").ap()

        self.x = din("x", (D, T))
        self.r_w = [din("r0", (D, NE)), din("r1", (D, NE))]
        self.wg = [din("wg0", (NE, D, MOE_H)), din("wg1", (NE, D, MOE_H))]
        self.wu = [din("wu0", (NE, D, MOE_H)), din("wu1", (NE, D, MOE_H))]
        self.wd = [din("wd0", (NE, MOE_H, D)), din("wd1", (NE, MOE_H, D))]
        self.WT = {nm: din(nm + "T", (D, D)) for nm in ("Wq", "Wk", "Wv", "Wo")}
        self.brd = {nm: din(nm + "r", (128, NCH)) for nm in ("bq", "bk", "bv", "bo")}
        self.wqkvT = din("wqkvT", (D, 3 * D))
        self.wkyT = din("wkyT", (D, D))
        self.wprojT = din("wprojT", (D, D))
        self.f_wqkv = din("f_wqkv", (6, D, 3 * D))
        self.f_wo = din("f_wo", (6, D, D))
        self.f_qs = din("f_qs", (128, 6))
        self.f_ks = din("f_ks", (128, 6))
        self.f_lng = din("f_lng", (128, 48))
        self.f_lnb = din("f_lnb", (128, 48))
        self.f_b2 = din("f_b2", (128, 48))
        self.f_w1a = din("f_w1a", (6, 128, NCH, FJ, 128))
        self.f_w1g = din("f_w1g", (6, 128, NCH, FJ, 128))
        self.f_b1a = din("f_b1a", (128, 132))
        self.f_b1g = din("f_b1g", (128, 132))
        self.f_w2 = din("f_w2", (6, 128, FJ, D))

        self.c_ident = din("c_ident", (128, 128), BF16)
        self.c_identf = din("c_identf", (128, 128), F32)
        self.c_ones = din("c_ones", (128, 128), BF16)
        self.c_pswap = din("c_pswap", (128, 128), BF16)
        self.c_rot = {nm: din("c_" + nm, (128, T), BF16)
                      for nm in ("rotAq", "rotBq", "rotAk", "rotBk")}
        self.c_lmask = [din(f"c_lmask{qc}", (128, 384), BF16) for qc in (0, 1)]
        self.c_dgmask = din("c_dgmask", (128, 256), F32)
        self.c_lmask0 = [din(f"c_lmask0_{qc}", (128, 384), BF16) for qc in (0, 1)]
        self.c_selq = din("c_selq", (128, NCH * HEADS), BF16)
        self.c_selr = din("c_selr", (HEADS, NCH * 128), BF16)

        self.out = nc.dram_tensor("out", [D, T], F32, kind="ExternalOutput").ap()

        self.gateT_dram = nc.dram_tensor("s_gateT", [2, NE, T], F32, kind="Internal").ap()
        self.ab_dram = nc.dram_tensor("s_ab", [1, 2 * T], F32, kind="Internal").ap()
        self.rspill = nc.dram_tensor("s_rspill", [128, NCH * T], F32, kind="Internal").ap()

        self.glob = ExitStack()
        sb = lambda n, s, d: self.glob.enter_context(nc.sbuf_tensor(n, s, d))
        self.rr = sb("rr", [128, NCH * T], F32)
        self.ubf = sb("ubf", [128, NCH * T], BF16)
        self.identb = sb("identb", [128, 128], BF16)
        self.onesb = sb("onesb", [128, 128], BF16)
        self.pswapb = sb("pswapb", [128, 128], BF16)
        self.arow = sb("arow", [128, T], F32)
        self.brow = sb("brow", [128, T], F32)
        self.fta = sb("fta", [128, T], F32)
        self.cstb = sb("cstb", [128, 8], F32)
        self.ftb = sb("ftb", [128, T], F32)

        self.s = {nm: Sem(nc, nm) for nm in
                  ("ld", "wq", "pe", "ve", "se", "ge", "sp", "rep", "st")}
        self.ffn_inv = 0

    def uname(self, n):
        self._uid = getattr(self, "_uid", 0) + 1
        return f"{n}_{self._uid}"

    def ch(self, buf, c, w=T):
        return buf[:, c * w:(c + 1) * w]

    class EvProg:

        def __init__(self, prog):
            self.p = prog
            self.evs = []
            self.pl = {k: v.v for k, v in prog.s.items()}
            self.joins = [(k, v) for k, v in self.pl.items()
                          if k in ("pe", "ve", "se", "ge")]

        def add(self, eng, emit, waits=(), inc=None, n=1):
            self.evs.append((eng, emit, list(waits), inc, n))
            if inc is not None:
                self.pl[inc] += n
                return self.pl[inc]
            return None

        def val(self, sem):
            return self.pl[sem]

        def emit(self, block):
            p = self.p
            for eng_name in ("sync", "gpsimd", "tensor", "scalar", "vector"):
                evs = [e for e in self.evs if e[0] == eng_name]
                if not evs:
                    continue
                def body(eng, evs=evs):
                    for sem, val in self.joins:
                        eng.wait_ge(p.s[sem].h, val)
                    for _, emit, waits, inc, n in evs:
                        for sem, val in waits:
                            eng.wait_ge(p.s[sem].h, val)
                        ins = emit(eng)
                        if inc is not None:
                            p.s[inc].inc(ins, n)
                getattr(block, eng_name)(body)
            for k, v in self.pl.items():
                assert p.s[k].v == v, (k, p.s[k].v, v)

    def edge(self, eng, ins, sem="ve"):
        s = self.s[sem]
        s.inc(ins, 1)
        eng.wait_ge(s.h, s.v)

    def emit_preload(self):
        nc, s = self.nc, self.s
        with nc.Block("preload") as block:
            @block.sync
            def _(sy):
                for o, i in (
                    (self.rr.ap().rearrange("p (c n) -> p c n", c=NCH),
                     self.x.rearrange("(c p) n -> p c n", p=128)),
                    (self.identb[:, :], self.c_ident),
                    (self.onesb[:, :], self.c_ones),
                    (self.pswapb[:, :], self.c_pswap),
                ):
                    s["ld"].inc(sy.dma_start(out=o, in_=i), 16)
                sy.wait_ge(s["ld"].h, s["ld"].v)

            @block.gpsimd
            def _(g):
                for col, val in ((0, 1e-5), (1, 2e-6), (2, 1e-6), (3, 1e-12), (4, 0.0), (5, 3e-6)):
                    g.memset(self.cstb[:, col:col + 1], val)
                g.engine_nop().then_inc(s["ge"].h, 1)
                s["ge"].v += 1

    def emit_ln(self, block, gain_ap=None, bias_ap=None, shift=False,
                fp32_inplace=False, pstat_ext=None, wait_ve=None, wait_se=None,
                wait_pe=None):
        nc, s = self.nc, self.s
        ps = ExitStack()
        if pstat_ext is None:
            pstat = ps.enter_context(nc.psum_tensor(self.uname("pstat"), [1, 1024], F32)).ap()
        else:
            pstat = pstat_ext
        sqb = self.fta.ap().bitcast(BF16)

        se0 = s["se"].v

        @block.scalar
        def _(sc):
            if wait_ve is not None:
                sc.wait_ge(s["ve"].h, wait_ve)
            if wait_se is not None:
                sc.wait_ge(s["se"].h, wait_se)
            if wait_pe is not None:
                sc.wait_ge(s["pe"].h, wait_pe)
            for c in range(NCH):
                if shift and c >= 4:
                    sc.copy(self.ch(self.ubf, c)[:, 0:1], self.cstb[:, 4:5])
                    ins = sc.copy(self.ch(self.ubf, c)[:, 1:T], self.ch(self.rr, c)[:, 0:T - 1])
                else:
                    ins = sc.copy(self.ch(self.ubf, c), self.ch(self.rr, c))
                s["se"].inc(ins, 1)

        ve0 = s["ve"].v
        pe0 = s["pe"].v
        se_evac0 = se0 + NCH

        @block.vector
        def _(v):
            for f in range(4):
                for c in range(NCH):
                    g = f * NCH + c
                    v.wait_ge(s["se"].h, se0 + c + 1)
                    if g >= 2:
                        v.wait_ge(s["pe"].h, pe0 + g - 1)
                    half = sqb[:, (g % 2) * T:(g % 2) * T + 512]
                    ins = v.tensor_mul(half, self.ch(self.ubf, c)[:, f * 512:(f + 1) * 512],
                                       self.ch(self.ubf, c)[:, f * 512:(f + 1) * 512])
                    s["ve"].inc(ins, 1)

        @block.tensor
        def _(t):
            for f in range(4):
                if f >= 1:
                    t.wait_ge(s["se"].h, se_evac0 + 2 * f)
                for c in range(NCH):
                    g = f * NCH + c
                    t.wait_ge(s["ve"].h, ve0 + g + 1)
                    half = sqb[:, (g % 2) * T:(g % 2) * T + 512]
                    fs = slice(f * 512, (f + 1) * 512)
                    t.matmul(pstat[0:1, 0:512], self.onesb[:, 0:1],
                             self.ch(self.ubf, c)[:, fs],
                             start=(c == 0), stop=(c == NCH - 1))
                    ins = t.matmul(pstat[0:1, 512:1024], self.onesb[:, 1:2], half,
                                   start=(c == 0), stop=(c == NCH - 1))
                    s["pe"].inc(ins, 1)

        @block.scalar
        def _(sc):
            for f in range(4):
                fs = slice(f * 512, (f + 1) * 512)
                sc.wait_ge(s["pe"].h, pe0 + NCH * (f + 1))
                ins = sc.copy(self.brow[0:1, fs], pstat[0:1, 0:512])
                s["se"].inc(ins, 1)
                ins = sc.copy(self.arow[0:1, fs], pstat[0:1, 512:1024])
                s["se"].inc(ins, 1)

        ve1 = s["ve"].v

        @block.vector
        def _(v):
            v.wait_ge(s["se"].h, se_evac0 + 8)
            ar, br = self.arow[0:1, 0:T], self.brow[0:1, 0:T]
            tmp = self.fta[0:1, 0:T]
            ins = v.tensor_scalar_mul(br, br, 1.0 / D)
            s["ve"].inc(ins, 1)
            v.wait_ge(s["ve"].h, s["ve"].v)
            ins = v.tensor_scalar_mul(ar, ar, 1.0 / D)
            s["ve"].inc(ins, 1)
            v.wait_ge(s["ve"].h, s["ve"].v)
            ins = v.tensor_mul(tmp, br, br)
            s["ve"].inc(ins, 1)
            v.wait_ge(s["ve"].h, s["ve"].v)
            ins = v.tensor_sub(ar, ar, tmp)
            s["ve"].inc(ins, 1)

        se1 = s["se"].v

        @block.scalar
        def _(sc):
            sc.wait_ge(s["ve"].h, ve1 + 4)
            ins = sc.activation(self.arow[0:1, 0:T], self.arow[0:1, 0:T], AF.Sqrt,
                                bias=self.cstb[0:1, 0:1])
            s["se"].inc(ins, 1)

        @block.vector
        def _(v):
            v.wait_ge(s["se"].h, se1 + 1)
            ar, br = self.arow[0:1, 0:T], self.brow[0:1, 0:T]
            ins = v.reciprocal(ar, ar)
            s["ve"].inc(ins, 1)
            v.wait_ge(s["ve"].h, s["ve"].v)
            ins = v.tensor_mul(br, br, ar)
            s["ve"].inc(ins, 1)
            v.wait_ge(s["ve"].h, s["ve"].v)
            ins = v.tensor_scalar_mul(br, br, -1.0)
            s["ve"].inc(ins, 1)

        sp0 = s["sp"].v
        ve2b = s["ve"].v

        @block.sync
        def _(sy):
            sy.wait_ge(s["ve"].h, ve2b)
            s["sp"].inc(sy.dma_start(out=self.ab_dram[0:1, T:2 * T], in_=self.arow[0:1, 0:T]), 16)
            s["sp"].inc(sy.dma_start(out=self.ab_dram[0:1, 0:T], in_=self.brow[0:1, 0:T]), 16)
            sy.wait_ge(s["sp"].h, sp0 + 32)
            s["sp"].inc(sy.dma_start(out=self.arow[:, :],
                                     in_=self.ab_dram[0:1, T:2 * T].partition_broadcast(128)), 16)
            s["sp"].inc(sy.dma_start(out=self.brow[:, :],
                                     in_=self.ab_dram[0:1, 0:T].partition_broadcast(128)), 16)

        sp1 = s["sp"].v
        ve3 = s["ve"].v
        se_aff = s["se"].v
        affine = gain_ap is not None

        @block.vector
        def _(v):
            v.wait_ge(s["sp"].h, sp1)
            for c in range(NCH):
                dsttmp = self.fta[:, 0:T]
                if c >= 1:
                    v.wait_ge(s["ve"].h, s["ve"].v)
                if fp32_inplace:
                    self.edge(v, v.tensor_mul(dsttmp, self.ch(self.rr, c), self.arow[:, :]))
                    ins = v.tensor_add(self.ch(self.rr, c), dsttmp, self.brow[:, :])
                elif affine:
                    self.edge(v, v.tensor_mul(dsttmp, self.ch(self.ubf, c), self.arow[:, :]))
                    if c >= 1:
                        v.wait_ge(s["se"].h, se_aff + c)
                    ins = v.tensor_add(self.ftb[:, 0:T], dsttmp, self.brow[:, :])
                else:
                    self.edge(v, v.tensor_mul(dsttmp, self.ch(self.ubf, c), self.arow[:, :]))
                    ins = v.tensor_add(self.ch(self.ubf, c), dsttmp, self.brow[:, :])
                s["ve"].inc(ins, 1)

        @block.scalar
        def _(sc):
            if affine:
                for c in range(NCH):
                    sc.wait_ge(s["ve"].h, ve3 + 2 * (c + 1))
                    ins = sc.activation(self.ch(self.ubf, c), self.ftb[:, 0:T],
                                        AF.Identity, bias=bias_ap(c), scale=gain_ap(c))
                    s["se"].inc(ins, 1)
        ps.close()

    def _moe_tail(self, block):
        pass

    def emit_gatetest(self):
        self.emit_moe(do_experts=False, gate_debug=True)

    def moe_debug_dump(self, block, bufs):
        nc, s = self.nc, self.s
        scr, scr2, m1, gtile, gT = bufs

        @block.sync
        def _(sy):
            sy.wait_ge(s["ve"].h, s["ve"].v)
            sy.wait_ge(s["se"].h, s["se"].v)
            sy.wait_ge(s["pe"].h, s["pe"].v)
            for i, (buf, rows, cols) in enumerate((
                    (scr, 128, NE), (scr2, 128, NE), (m1, 128, 4),
                    (gtile, 128, 2 * NE))):
                dst = self.rr[0:rows, (4 + i) * T:(4 + i) * T + cols]
                if buf.dtype == BF16:
                    dst = dst.bitcast(BF16)[:, 0:cols]
                ins = sy.dma_start(out=dst, in_=buf[0:rows, 0:cols])
                s["sp"].inc(ins, 16)
            sy.wait_ge(s["sp"].h, s["sp"].v)

    def emit_moe(self, do_experts=True, gate_debug=False):
        nc, s = self.nc, self.s
        with nc.Block(self.uname("moe")) as block:
            self.emit_ln(block)
            ln_ve = s["ve"].v
            ln_se = s["se"].v
            loc = ExitStack()
            sb = lambda n, sh, d: loc.enter_context(nc.sbuf_tensor(self.uname(n), sh, d))
            rb = sb("m_rb", [128, NCH * NE], BF16)
            scr = sb("m_scr", [128, NE], F32)
            scr2 = sb("m_scr2", [128, NE], F32)
            gtile = sb("m_gtile", [128, 2 * NE], BF16)
            m1 = sb("m_m1", [128, 4], F32)
            gT = sb("m_gT", [NE, T], F32)
            wgb = sb("m_wgb", [128, 2 * NCH * 512], BF16)
            wub = sb("m_wub", [128, 2 * NCH * 512], BF16)
            wdb = sb("m_wdb", [128, 2 * 4 * D], BF16)
            hb = sb("m_hb", [128, 4 * 512], BF16)
            sigb = sb("m_sigb", [128, 512], F32)
            repb = sb("m_repb", [128, 2 * 512], F32)

            psg = ExitStack()
            plg = psg.enter_context(nc.psum_tensor(self.uname("m_plg"), [128, 1024], F32))
            ptr = psg.enter_context(nc.psum_tensor(self.uname("m_ptr"), [NE, 2048], BF16))

            wgv = wgb.ap().rearrange("p (b kc n) -> p b kc n", b=2, kc=NCH)
            wuv = wub.ap().rearrange("p (b kc n) -> p b kc n", b=2, kc=NCH)
            wdv = wdb.ap().rearrange("p (b kc n) -> p b kc n", b=2, kc=4)

            WQg = s["wq"].v
            PEg = s["pe"].v
            VEg = s["ve"].v
            SEg = s["se"].v
            SPg = s["sp"].v

            @block.gpsimd
            def _(g):
                s["wq"].inc(g.dma_start(out=rb.ap().rearrange("p (kc e) -> p kc e", kc=NCH),
                                        in_=self.r_w[0].rearrange("(kc p) e -> p kc e", p=128)), 16)
                g.wait_ge(s["pe"].h, PEg + 32)
                s["wq"].inc(g.dma_start(out=rb.ap().rearrange("p (kc e) -> p kc e", kc=NCH),
                                        in_=self.r_w[1].rearrange("(kc p) e -> p kc e", p=128)), 16)

            @block.tensor
            def _(t):
                t.wait_ge(s["ve"].h, ln_ve)
                t.wait_ge(s["se"].h, ln_se)
                for gi in (0, 1):
                    t.wait_ge(s["wq"].h, WQg + 16 * (gi + 1))
                    for tc in range(16):
                        u = gi * 16 + tc
                        par = u % 2
                        if u >= 2:
                            t.wait_ge(s["se"].h, SEg + 2 * (u - 2) + 1)
                        for kc in range(NCH):
                            ins = t.matmul(plg[:, par * 512:par * 512 + NE],
                                           self.ch(self.ubf, kc)[:, tc * 128:(tc + 1) * 128],
                                           rb[:, kc * NE:(kc + 1) * NE],
                                           start=(kc == 0), stop=(kc == NCH - 1))
                        s["pe"].inc(ins, 1)
                        t.wait_ge(s["ve"].h, VEg + 9 * u + 9)
                        if u >= 2:
                            t.wait_ge(s["se"].h, SEg + 2 * (u - 2) + 2)
                        ins = t.transpose(ptr[:, par * 1024:par * 1024 + 128],
                                          gtile[:, par * NE:(par + 1) * NE], self.identb[:, :])
                        s["pe"].inc(ins, 1)

            @block.vector
            def _(v):
                for u in range(32):
                    par = u % 2
                    v.wait_ge(s["pe"].h, PEg + 2 * u + 1)
                    ins = v.tensor_reduce(m1[:, 0:1], plg[:, par * 512:par * 512 + NE],
                                          AX.X, OP.max, negate=True)
                    s["ve"].inc(ins, 1)
                    if u >= 2:
                        v.wait_ge(s["pe"].h, PEg + 2 * (u - 2) + 2)
                    v.wait_ge(s["se"].h, SEg + 2 * u + 1)
                    self.edge(v, v.reciprocal(m1[:, 1:2], m1[:, 1:2]))
                    self.edge(v, v.tensor_scalar(scr[:, :], scr[:, :], m1[:, 1:2], None, OP.mult))
                    self.edge(v, v.tensor_reduce(m1[:, 2:3], scr[:, :], AX.X, OP.max))
                    self.edge(v, v.tensor_scalar(scr2[:, :], scr[:, :], m1[:, 2:3], None, OP.is_ge))
                    self.edge(v, v.scalar_tensor_tensor(scr2[:, :], scr2[:, :], -1e30, scr[:, :],
                                                        OP.mult, OP.add))
                    self.edge(v, v.tensor_reduce(m1[:, 3:4], scr2[:, :], AX.X, OP.max))
                    self.edge(v, v.tensor_scalar(scr2[:, :], scr[:, :], m1[:, 3:4], None, OP.is_ge))
                    ins = v.scalar_tensor_tensor(gtile[:, par * NE:(par + 1) * NE],
                                                 scr2[:, :], 0.25, scr[:, :], OP.mult, OP.mult)
                    s["ve"].inc(ins, 1)

            @block.scalar
            def _(sc):
                for u in range(32):
                    par = u % 2
                    sc.wait_ge(s["ve"].h, VEg + 9 * u + 1)
                    ins = sc.activation(scr[:, :], plg[:, par * 512:par * 512 + NE],
                                        AF.Exp, bias=m1[:, 0:1], accum_out=m1[:, 1:2])
                    s["se"].inc(ins, 1)
                    sc.wait_ge(s["pe"].h, PEg + 2 * u + 2)
                    gi, tc = u // 16, u % 16
                    if u == 16:
                        sc.wait_ge(s["sp"].h, SPg + 16)
                    ins = sc.copy(gT[:, tc * 128:(tc + 1) * 128], ptr[:, par * 1024:par * 1024 + 128])
                    s["se"].inc(ins, 1)

            @block.sync
            def _(sy):
                for gi in (0, 1):
                    sy.wait_ge(s["se"].h, SEg + 2 * (16 * (gi + 1) - 1) + 2)
                    ins = sy.dma_start(out=self.gateT_dram[gi], in_=gT[:, :])
                    s["sp"].inc(ins, 16)

            if True:
                pass
            if gate_debug:
                @block.sync
                def _(sy):
                    sy.wait_ge(s["sp"].h, s["sp"].v)
                    for gi in (0, 1):
                        ins = sy.dma_start(out=self.rr[0:32, gi * T:(gi + 1) * T],
                                           in_=self.gateT_dram[gi])
                        s["sp"].inc(ins, 16)
                    sy.wait_ge(s["sp"].h, s["sp"].v)

            psg.close()
            psl = ExitStack()
            gu = psl.enter_context(nc.psum_tensor(self.uname("m_gu"), [128, 2048], F32))
            yd = psl.enter_context(nc.psum_tensor(self.uname("m_yd"), [128, 1024], F32))
            SP_G = s["sp"].v
            PE0 = s["pe"].v
            VE0 = s["ve"].v
            SE0 = s["se"].v
            WQ0 = s["wq"].v
            REP0 = s["rep"].v
            NK = 64 * 4
            if not do_experts:
                self.moe_debug_dump(block, (scr, scr2, m1, gtile, gT))
                psl.close()
                loc.close()
                return

            @block.gpsimd
            def _(g):
                for i in range(64):
                    gi, e = i // 32, i % 32
                    if i >= 2:
                        g.wait_ge(s["pe"].h, PE0 + 48 * (i - 1))
                    par = i % 2
                    s["wq"].inc(g.dma_start(out=wgv[:, par],
                                            in_=self.wg[gi][e].rearrange("(kc p) n -> p kc n", p=128)), 16)
                    s["wq"].inc(g.dma_start(out=wuv[:, par],
                                            in_=self.wu[gi][e].rearrange("(kc p) n -> p kc n", p=128)), 16)
                    s["wq"].inc(g.dma_start(out=wdv[:, par],
                                            in_=self.wd[gi][e].rearrange("(kc p) n -> p kc n", p=128)), 16)

            @block.sync
            def _(sy):
                sy.wait_ge(s["sp"].h, SP_G)
                for k in range(NK):
                    i, t = k // 4, k % 4
                    gi, e = i // 32, i % 32
                    if k >= 2:
                        sy.wait_ge(s["ve"].h, VE0 + 16 * (k - 2) + 8)
                    ins = sy.dma_start(out=repb[:, (k % 2) * 512:(k % 2 + 1) * 512],
                                       in_=self.gateT_dram[gi, e:e + 1, t * 512:(t + 1) * 512]
                                       .partition_broadcast(128))
                    s["rep"].inc(ins, 16)

            @block.tensor
            def _(t):
                t.wait_ge(s["se"].h, SE0)
                t.wait_ge(s["ve"].h, VE0)
                for k in range(NK):
                    i, ti = k // 4, k % 4
                    par_w = i % 2
                    if ti == 0:
                        t.wait_ge(s["wq"].h, WQ0 + 48 * (i + 1))
                    for hc in range(4):
                        par = hc % 2
                        if k > 0 and hc < 2:
                            t.wait_ge(s["ve"].h, VE0 + 16 * (k - 1) + 2 * hc + 6)
                        elif hc >= 2:
                            t.wait_ge(s["ve"].h, VE0 + 16 * k + 2 * hc - 2)
                        for kc in range(NCH):
                            t.matmul(gu[:, par * 1024:par * 1024 + 512],
                                     wgv[:, par_w, kc, hc * 128:(hc + 1) * 128],
                                     self.ch(self.ubf, kc)[:, ti * 512:(ti + 1) * 512],
                                     start=(kc == 0), stop=(kc == NCH - 1))
                        for kc in range(NCH):
                            ins = t.matmul(gu[:, par * 1024 + 512:par * 1024 + 1024],
                                           wuv[:, par_w, kc, hc * 128:(hc + 1) * 128],
                                           self.ch(self.ubf, kc)[:, ti * 512:(ti + 1) * 512],
                                           start=(kc == 0), stop=(kc == NCH - 1))
                        s["pe"].inc(ins, 1)
                    for d in range(8):
                        pard = d % 2
                        if d < 2:
                            t.wait_ge(s["ve"].h, VE0 + 16 * k + 8)
                        else:
                            t.wait_ge(s["ve"].h, VE0 + 16 * k + 7 + d)
                        for hcc in range(4):
                            ins = t.matmul(yd[:, pard * 512:(pard + 1) * 512],
                                           wdv[:, par_w, hcc, d * 128:(d + 1) * 128],
                                           hb[:, hcc * 512:(hcc + 1) * 512],
                                           start=(hcc == 0), stop=(hcc == 3))
                        s["pe"].inc(ins, 1)

            @block.scalar
            def _(sc):
                for k in range(NK):
                    for hc in range(4):
                        par = hc % 2
                        sc.wait_ge(s["pe"].h, PE0 + 12 * k + hc + 1)
                        if hc >= 1:
                            sc.wait_ge(s["ve"].h, VE0 + 16 * k + 2 * hc)
                        elif k > 0:
                            sc.wait_ge(s["ve"].h, VE0 + 16 * (k - 1) + 8)
                        ins = sc.activation(sigb[:, :], gu[:, par * 1024:par * 1024 + 512], AF.Silu)
                        s["se"].inc(ins, 1)

            @block.vector
            def _(v):
                for k in range(NK):
                    i, ti = k // 4, k % 4
                    for hc in range(4):
                        par = hc % 2
                        v.wait_ge(s["se"].h, SE0 + 4 * k + hc + 1)
                        if k > 0:
                            v.wait_ge(s["pe"].h, PE0 + 12 * k)
                        self.edge(v, v.tensor_mul(sigb[:, :], sigb[:, :],
                                                  gu[:, par * 1024 + 512:par * 1024 + 1024]))
                        v.wait_ge(s["rep"].h, REP0 + 16 * (k + 1))
                        ins = v.tensor_mul(hb[:, hc * 512:(hc + 1) * 512], sigb[:, :],
                                           repb[:, (k % 2) * 512:(k % 2 + 1) * 512])
                        s["ve"].inc(ins, 1)
                    for d in range(8):
                        pard = d % 2
                        v.wait_ge(s["pe"].h, PE0 + 12 * k + 4 + d + 1)
                        rrsl = self.ch(self.rr, d)[:, ti * 512:(ti + 1) * 512]
                        ins = v.scalar_tensor_tensor(rrsl, yd[:, pard * 512:(pard + 1) * 512],
                                                     1.0, rrsl, OP.mult, OP.add)
                        s["ve"].inc(ins, 1)

            psl.close()
            loc.close()

    def emit_lntest(self):
        nc, s = self.nc, self.s
        with nc.Block("lntest") as block:
            self.emit_ln(block, fp32_inplace=True)

    def emit_l0attn(self):
        nc, s = self.nc, self.s
        with nc.Block("l0attn") as block:
            self.emit_ln(block)
            ln_ve = s["ve"].v
            ln_se = s["se"].v
            loc = ExitStack()
            sb = lambda n, sh, d: loc.enter_context(nc.sbuf_tensor(self.uname(n), sh, d))
            B = {}
            B["wb"] = sb("a_wb", [128, 2 * 3 * NCH * 128], BF16)
            B["wob"] = sb("a_wob", [128, 2 * NCH * 128], BF16)
            B["bias4"] = sb("a_bias4", [128, 4 * NCH], F32)
            B["qph"] = sb("a_qph", [128, T], BF16)
            B["kph"] = sb("a_kph", [128, T], BF16)
            B["vbf"] = sb("a_vbf", [128, T], BF16)
            B["ktm"] = sb("a_ktm", [128, 16 * 64], BF16)
            B["vtm"] = sb("a_vtm", [128, 16 * 64], BF16)
            B["kvb"] = sb("a_kvb", [128, 64], BF16)
            B["zf"] = sb("a_zf", [128, 8], F32)
            B["zb"] = sb("a_zb", [128, 8], BF16)
            B["denr"] = sb("a_denr", [1, 512], F32)
            B["denb"] = sb("a_denb", [1, 512], BF16)
            B["obf"] = sb("a_obf", [128, 512], BF16)
            B["ao"] = sb("a_ao", [128, NCH * T], BF16)
            psl = ExitStack()
            B["pA"] = psl.enter_context(nc.psum_tensor(self.uname("a_pA"), [128, 1024], F32))
            B["pB"] = psl.enter_context(nc.psum_tensor(self.uname("a_pB"), [128, 1536], F32))
            B["ptp"] = psl.enter_context(nc.psum_tensor(self.uname("a_ptp"), [128, 2048], BF16))
            B["prp"] = psl.enter_context(nc.psum_tensor(self.uname("a_prp"), [128, 512], F32))
            B["wbv"] = B["wb"].ap().rearrange("p (b m kc n) -> p b m kc n", b=2, m=3, kc=NCH)
            B["wov"] = B["wob"].ap().rearrange("p (b kc n) -> p b kc n", b=2, kc=NCH)

            @block.sync
            def _(sy):
                for bi, nm in enumerate(("bq", "bk", "bv", "bo")):
                    ins = sy.dma_start(out=B["bias4"][:, bi * NCH:(bi + 1) * NCH], in_=self.brd[nm])
                    s["ld"].inc(ins, 16)
            B["bias_ld"] = s["ld"].v
            B["marks"] = {}

            @block.gpsimd
            def _(g):
                for c in (0, 1):
                    self._l0_load_qkv(g, c, B)

            st = {"first": True, "ln_ve": ln_ve, "ln_se": ln_se}
            for c in range(NCH):
                self._l0_chunk(block, c, B, st)

            B["womarks"] = {}

            @block.gpsimd
            def _(g):
                for c2 in (0, 1):
                    self._l0_load_wo(g, c2, B)
            wo_ve0 = s["ve"].v
            wo_se0 = s["se"].v
            for c2 in range(NCH):
                self._l0_wo_chunk(block, c2, B, wo_ve0, wo_se0)
            psl.close()
            loc.close()

    def _l0_load_qkv(self, g, c, B):
        s = self.s
        par = c % 2
        for mi, w in enumerate((self.WT["Wq"], self.WT["Wk"], self.WT["Wv"])):
            ins = g.dma_start(
                out=B["wbv"][:, par, mi],
                in_=w[:, c * 128:(c + 1) * 128].rearrange("(kc p) n -> p kc n", p=128))
            s["wq"].inc(ins, 16)
        B["marks"][c] = s["wq"].v

    def _l0_load_wo(self, g, c2, B):
        s = self.s
        par = c2 % 2
        ins = g.dma_start(
            out=B["wov"][:, par],
            in_=self.WT["Wo"][:, c2 * 128:(c2 + 1) * 128].rearrange("(kc p) n -> p kc n", p=128))
        s["wq"].inc(ins, 16)
        B["womarks"][c2] = s["wq"].v

    def _l0_chunk(self, block, c, B, st):
        nc, s = self.nc, self.s
        par_w = c % 2
        PE0 = s["pe"].v
        VE0 = s["ve"].v
        SE0 = s["se"].v
        bias4 = B["bias4"]

        @block.tensor
        def _(t):
            if st["first"]:
                t.wait_ge(s["ve"].h, st["ln_ve"])
                t.wait_ge(s["se"].h, st["ln_se"])
                t.wait_ge(s["ld"].h, B["bias_ld"])
                st["first"] = False
            t.wait_ge(s["wq"].h, B["marks"][c])
            t.wait_ge(s["ve"].h, VE0)
            t.wait_ge(s["se"].h, SE0)
            for g in range(12):
                mi, f = divmod(g, 4)
                fs = slice(f * 512, (f + 1) * 512)
                par = g % 2
                if g >= 2:
                    gp = g - 2
                    if gp < 8:
                        t.wait_ge(s["ve"].h, VE0 + 3 * gp + 2)
                    else:
                        t.wait_ge(s["se"].h, SE0 + gp + 1)
                for kc in range(NCH):
                    ins = t.matmul(B["pA"][:, par * 512:(par + 1) * 512],
                                   B["wbv"][:, par_w, mi, kc, :],
                                   self.ch(self.ubf, kc)[:, fs],
                                   start=(kc == 0), stop=(kc == NCH - 1))
                s["pe"].inc(ins, 1)
            t.wait_ge(s["ve"].h, VE0 + 24)
            t.wait_ge(s["se"].h, SE0 + 12)
            for ho in range(2):
                b = ho * 64
                PH0 = PE0 + 12 + ho * 45
                VH0 = VE0 + 24 + ho * 50
                SH0 = SE0 + 12 + ho * 5
                for i in range(32):
                    src_buf = B["kph"] if i < 16 else B["vbf"]
                    ii = i % 16
                    parp = i % 2
                    if i >= 2:
                        t.wait_ge(s["ve"].h, VH0 + i - 1)
                    ins = t.transpose(B["ptp"][:, parp * 1024:parp * 1024 + 64],
                                      src_buf[b:b + 64, ii * 128:(ii + 1) * 128],
                                      self.identb[b:b + 64, b:b + 64])
                    s["pe"].inc(ins, 1)
                t.wait_ge(s["ve"].h, VH0 + 32)
                if ho == 1:
                    t.wait_ge(s["se"].h, SE0 + 13)
                for i in range(16):
                    ins = t.matmul(B["pB"][b:b + 64, 0:64],
                                   B["ktm"][:, i * 64:(i + 1) * 64],
                                   B["vtm"][:, i * 64:(i + 1) * 64],
                                   start=(i == 0), stop=(i == 15))
                s["pe"].inc(ins, 1)
                for f in range(4):
                    fs = slice(f * 512, (f + 1) * 512)
                    t.wait_ge(s["se"].h, SH0 + 1)
                    if f >= 1:
                        t.wait_ge(s["se"].h, SH0 + 1 + f)
                    ins = t.matmul(B["pB"][b:b + 64, 512:1024], B["kvb"][b:b + 64, :],
                                   B["qph"][b:b + 64, fs], start=True, stop=True)
                    s["pe"].inc(ins, 1)
                    t.wait_ge(s["ve"].h, VH0 + 34)
                    if f >= 1:
                        t.wait_ge(s["ve"].h, VH0 + 34 + 4 * (f - 1) + 1)
                    ins = t.matmul(B["pB"][0:1, 1024:1536], B["zb"][b:b + 64, 0:1],
                                   B["qph"][b:b + 64, fs], start=True, stop=True)
                    s["pe"].inc(ins, 1)
                    if f >= 1:
                        t.wait_ge(s["ve"].h, VH0 + 34 + 4 * (f - 1) + 4)
                    t.wait_ge(s["ve"].h, VH0 + 34 + 4 * f + 3)
                    ins = t.matmul(B["prp"][b:b + 64, :], self.onesb[0:1, 0:64],
                                   B["denb"][0:1, :], start=True, stop=True)
                    s["pe"].inc(ins, 1)

        @block.scalar
        def _(sc):
            for g in range(12):
                mi, f = divmod(g, 4)
                fs = slice(f * 512, (f + 1) * 512)
                par = g % 2
                sc.wait_ge(s["pe"].h, PE0 + g + 1)
                if mi == 2:
                    ins = sc.activation(B["vbf"][:, fs], B["pA"][:, par * 512:(par + 1) * 512],
                                        AF.Identity, bias=bias4[:, 2 * NCH + c:2 * NCH + c + 1])
                else:
                    sc.wait_ge(s["ve"].h, VE0 + 3 * g + 1)
                    ins = sc.activation(self.fta[:, 1024:1536], self.fta[:, 0:512], AF.Exp)
                s["se"].inc(ins, 1)
            for ho in range(2):
                PH0 = PE0 + 12 + ho * 45
                VH0 = VE0 + 24 + ho * 50
                SH0 = SE0 + 12 + ho * 5
                b = ho * 64
                sc.wait_ge(s["pe"].h, PH0 + 33)
                ins = sc.activation(B["kvb"][b:b + 64, :], B["pB"][b:b + 64, 0:64], AF.Copy)
                s["se"].inc(ins, 1)
                for f in range(4):
                    sc.wait_ge(s["pe"].h, PH0 + 33 + 3 * f + 1)
                    if f >= 1:
                        sc.wait_ge(s["ve"].h, VH0 + 34 + 4 * (f - 1) + 4)
                    ins = sc.activation(B["obf"][b:b + 64, :], B["pB"][b:b + 64, 512:1024], AF.Copy)
                    s["se"].inc(ins, 1)

        @block.vector
        def _(v):
            for g in range(8):
                mi, f = divmod(g, 4)
                fs = slice(f * 512, (f + 1) * 512)
                par = g % 2
                bcol = bias4[:, mi * NCH + c:mi * NCH + c + 1]
                v.wait_ge(s["pe"].h, PE0 + g + 1)
                if g >= 1:
                    v.wait_ge(s["se"].h, SE0 + g)
                ins = v.tensor_scalar(self.fta[:, 0:512], B["pA"][:, par * 512:(par + 1) * 512],
                                      bcol, 0.0, OP.add, OP.min)
                s["ve"].inc(ins, 1)
                v.wait_ge(s["ve"].h, s["ve"].v)
                ins = v.tensor_scalar(self.fta[:, 512:1024], B["pA"][:, par * 512:(par + 1) * 512],
                                      bcol, 0.0, OP.add, OP.max)
                s["ve"].inc(ins, 1)
                v.wait_ge(s["ve"].h, s["ve"].v)
                v.wait_ge(s["se"].h, SE0 + g + 1)
                dst = B["qph"] if mi == 0 else B["kph"]
                ins = v.tensor_add(dst[:, fs], self.fta[:, 1024:1536], self.fta[:, 512:1024])
                s["ve"].inc(ins, 1)
            for ho in range(2):
                b = ho * 64
                PH0 = PE0 + 12 + ho * 45
                VH0 = VE0 + 24 + ho * 50
                SH0 = SE0 + 12 + ho * 5
                for i in range(32):
                    parp = i % 2
                    dstb = B["ktm"] if i < 16 else B["vtm"]
                    ii = i % 16
                    v.wait_ge(s["pe"].h, PH0 + i + 1)
                    ins = v.tensor_copy(dstb[:, ii * 64:(ii + 1) * 64],
                                        B["ptp"][:, parp * 1024:parp * 1024 + 64])
                    s["ve"].inc(ins, 1)
                ins = v.tensor_reduce(B["zf"][b:b + 64, 0:1], B["kph"][b:b + 64, :], AX.X, OP.add)
                s["ve"].inc(ins, 1)
                v.wait_ge(s["ve"].h, s["ve"].v)
                ins = v.tensor_copy(B["zb"][b:b + 64, 0:1], B["zf"][b:b + 64, 0:1])
                s["ve"].inc(ins, 1)
                for f in range(4):
                    fs = slice(f * 512, (f + 1) * 512)
                    v.wait_ge(s["pe"].h, PH0 + 33 + 3 * f + 2)
                    ins = v.tensor_scalar(B["denr"][:, :], B["pB"][0:1, 1024:1536],
                                          1e-6, None, OP.add)
                    s["ve"].inc(ins, 1)
                    v.wait_ge(s["ve"].h, s["ve"].v)
                    ins = v.reciprocal(B["denr"][:, :], B["denr"][:, :])
                    s["ve"].inc(ins, 1)
                    v.wait_ge(s["ve"].h, s["ve"].v)
                    ins = v.tensor_copy(B["denb"][:, :], B["denr"][:, :])
                    s["ve"].inc(ins, 1)
                    v.wait_ge(s["ve"].h, s["ve"].v)
                    v.wait_ge(s["pe"].h, PH0 + 33 + 3 * f + 3)
                    v.wait_ge(s["se"].h, SH0 + 1 + f + 1)
                    ins = v.tensor_mul(B["ao"][b:b + 64, c * T + f * 512:c * T + (f + 1) * 512],
                                       B["obf"][b:b + 64, :], B["prp"][b:b + 64, :])
                    s["ve"].inc(ins, 1)

        if c + 2 < NCH:
            @block.gpsimd
            def _(g):
                g.wait_ge(s["pe"].h, PE0 + 12)
                self._l0_load_qkv(g, c + 2, B)

    def _l0_wo_chunk(self, block, c2, B, VEW, SEW):
        nc, s = self.nc, self.s
        par_w = c2 % 2
        PE0 = s["pe"].v
        VE0 = s["ve"].v
        SE0 = s["se"].v

        @block.tensor
        def _(t):
            t.wait_ge(s["wq"].h, B["womarks"][c2])
            if c2 == 0:
                t.wait_ge(s["ve"].h, VEW)
                t.wait_ge(s["se"].h, SEW)
            for f in range(4):
                fs = slice(f * 512, (f + 1) * 512)
                par = f % 2
                if 4 * c2 + f >= 2:
                    t.wait_ge(s["se"].h, SE0 + f - 1)
                for kc in range(NCH):
                    ins = t.matmul(B["pA"][:, par * 512:(par + 1) * 512],
                                   B["wov"][:, par_w, kc, :],
                                   B["ao"][:, kc * T + f * 512:kc * T + (f + 1) * 512],
                                   start=(kc == 0), stop=(kc == NCH - 1))
                s["pe"].inc(ins, 1)

        @block.scalar
        def _(sc):
            for f in range(4):
                par = f % 2
                sc.wait_ge(s["pe"].h, PE0 + f + 1)
                if 4 * c2 + f >= 2:
                    sc.wait_ge(s["ve"].h, VE0 + f - 1)
                ins = sc.activation(self.fta[:, par * 512:(par + 1) * 512],
                                    B["pA"][:, par * 512:(par + 1) * 512],
                                    AF.Identity, bias=B["bias4"][:, 3 * NCH + c2:3 * NCH + c2 + 1])
                s["se"].inc(ins, 1)

        @block.vector
        def _(v):
            for f in range(4):
                par = f % 2
                fs = slice(f * 512, (f + 1) * 512)
                v.wait_ge(s["se"].h, SE0 + f + 1)
                rrsl = self.ch(self.rr, c2)[:, fs]
                ins = v.tensor_add(rrsl, rrsl, self.fta[:, par * 512:(par + 1) * 512])
                s["ve"].inc(ins, 1)

        if c2 + 2 < NCH:
            @block.gpsimd
            def _(g):
                g.wait_ge(s["pe"].h, PE0 + 4)
                self._l0_load_wo(g, c2 + 2, B)

    def emit_l1attn(self):
        nc, s = self.nc, self.s
        with nc.Block("l1attn") as block:
            self.emit_ln(block)
            ln_ve = s["ve"].v
            ln_se = s["se"].v
            loc = ExitStack()
            sb = lambda n, sh, d: loc.enter_context(nc.sbuf_tensor(self.uname(n), sh, d))
            wh = sb("e_wh", [128, 2 * 4 * NCH * 64], BF16)
            qhb = sb("e_qhb", [64, T], BF16)
            khb = sb("e_khb", [64, T], BF16)
            kyb = sb("e_kyb", [64, T], BF16)
            vhb = sb("e_vhb", [64, T], BF16)
            vtm = sb("e_vtm", [128, 16 * 64], BF16)
            Lrow = sb("e_Lrow", [128, T], F32)
            Eyrow = sb("e_Eyrow", [128, T], F32)
            Abf = self.fta.ap().bitcast(BF16)[:, 0:T]
            atb = sb("e_atb", [128, 2 * 512], BF16)
            mb = sb("e_mb", [128, 48], F32)
            mdg = sb("e_mdg", [128, 256], F32)
            ao = sb("e_ao", [128, NCH * T], BF16)
            psl = ExitStack()
            pL = psl.enter_context(nc.psum_tensor(self.uname("e_pL"), [128, 1024], F32))
            pLy = psl.enter_context(nc.psum_tensor(self.uname("e_pLy"), [128, 1024], F32))
            pt = psl.enter_context(nc.psum_tensor(self.uname("e_pt"), [128, 2048], BF16))
            pav = psl.enter_context(nc.psum_tensor(self.uname("e_pav"), [128, 1024], F32))
            whv = wh.ap().rearrange("p (b m kc n) -> p b m kc n", b=2, m=4, kc=NCH)
            wpv = wh.ap()[:, 0:2 * NCH * 128].rearrange("p (b kc n) -> p b kc n", b=2, kc=NCH)

            ev = self.EvProg(self)
            whm = {}
            wsrc = (self.wqkvT, self.wqkvT, self.wkyT, self.wqkvT)


            def em_mdg(sy):
                return sy.dma_start(out=mdg[:, :], in_=self.c_dgmask)
            mdg_ld = ev.add("sync", em_mdg, (), "ld", 16)

            def load_head(h):
                par = h % 2
                for mi in range(4):
                    def em1(g, h=h, par=par, mi=mi):
                        w = (self.wqkvT, self.wqkvT, self.wkyT, self.wqkvT)[mi]
                        col0 = (0, D, 0, 2 * D)[mi] + h * 64
                        return g.dma_start(out=whv[:, par, mi],
                                           in_=w[:, col0:col0 + 64].rearrange("(kc p) n -> p kc n", p=128))
                    waits = []
                    if h >= 2 and mi == 0:
                        waits.append(("pe", whpe[h - 2]))
                    ev.add("gpsimd", em1, waits, "wq", 16)
                whm[h] = ev.val("wq")

            whpe = {}
            evac_se = {}
            first = [True]
            avse = {0: None, 1: None}
            ao_hist = []
            Eyrow2 = self.ftb.ap()[:, 0:T]
            Abf2 = self.fta.ap().bitcast(BF16)[:, T:2 * T]
            last_pt_cp = None
            load_head(0)
            load_head(1)

            for h in range(HEADS):
                par_h = h % 2
                ch, bh = h // 2, (h % 2) * 64
                dstmap = (qhb, khb, kyb, vhb)
                scalemap = (1.0, 0.125, 0.125, 1.0)
                for g in range(16):
                    mi, f = divmod(g, 4)
                    fs = slice(f * 512, (f + 1) * 512)
                    par = g % 2
                    waits = [("wq", whm[h])]
                    if first[0]:
                        waits += [("ve", ln_ve), ("se", ln_se)]
                        first[0] = False
                    key = (h, g - 2) if g >= 2 else (h - 1, 14 + g)
                    if key in evac_se:
                        waits.append(("se", evac_se[key]))

                    def em(t, par_h=par_h, mi=mi, fs=fs, par=par):
                        for kc in range(NCH):
                            ins = t.matmul(pL[0:64, par * 512:(par + 1) * 512],
                                           whv[:, par_h, mi, kc, :],
                                           self.ch(self.ubf, kc)[:, fs],
                                           start=(kc == 0), stop=(kc == NCH - 1))
                        return ins
                    pm = ev.add("tensor", em, waits, "pe", 1)

                    def eme(sc, mi=mi, fs=fs, par=par, dst=dstmap[mi], scl=scalemap[mi]):
                        return sc.activation(dst[:, fs], pL[0:64, par * 512:(par + 1) * 512],
                                             AF.Identity, scale=scl)
                    evac_se[(h, g)] = ev.add("scalar", eme, [("pe", pm)], "se", 1)

                cp_ve = {}
                for i in range(16):
                    par = i % 2
                    waits = [("se", evac_se[(h, 12 + i // 4)])]
                    if i >= 2:
                        waits.append(("ve", cp_ve[i - 2]))
                    elif last_pt_cp is not None:
                        waits.append(("ve", last_pt_cp))

                    def em(t, i=i, par=par):
                        return t.transpose(pt[:, par * 1024:par * 1024 + 64],
                                           vhb[:, i * 128:(i + 1) * 128],
                                           self.identb[0:64, 0:64])
                    pm = ev.add("tensor", em, waits, "pe", 1)

                    def emc(v, i=i, par=par):
                        return v.tensor_copy(vtm[:, i * 64:(i + 1) * 64],
                                             pt[:, par * 1024:par * 1024 + 64])
                    cp_ve[i] = ev.add("vector", emc, [("pe", pm)], "ve", 1)
                last_cp_ve = {h: cp_ve[15]}
                last_pt_cp = cp_ve[15]

                prev_av_se = avse[h % 2]
                for qi in range(16):
                    kjend = (qi + 1) * 128
                    nt = (kjend + 511) // 512
                    qp = qi % 2
                    EY = Eyrow if qp == 0 else Eyrow2
                    AB = Abf if qp == 0 else Abf2
                    mo = qp
                    abs_se = {}
                    clip_ve = {}
                    for kt in range(nt):
                        c0 = kt * 512
                        c1 = min(kjend, c0 + 512)
                        par = kt % 2
                        wL = []
                        wLy = []
                        if kt >= 2:
                            wL.append(("se", abs_se[kt - 2]))
                            wLy.append(("ve", clip_ve[kt - 2]))
                        elif qi > 0:
                            wL.append(("se", last_abs_se))
                            wLy.append(("ve", last_clip_ve))
                        else:
                            wL.append(("se", evac_se[(h, 15)]))
                            if h > 0:
                                wLy.append(("ve", last_clip_ve))

                        def emL(t, qi=qi, c0=c0, c1=c1, par=par):
                            return t.matmul(pL[0:128, par * 512:par * 512 + (c1 - c0)],
                                            qhb[:, qi * 128:(qi + 1) * 128],
                                            khb[:, c0:c1], start=True, stop=True)
                        pmL = ev.add("tensor", emL, wL, "pe", 1)

                        def emA(sc, c0=c0, c1=c1, par=par):
                            return sc.activation(Lrow[:, c0:c1],
                                                 pL[0:128, par * 512:par * 512 + (c1 - c0)], AF.Abs)
                        abs_se[kt] = ev.add("scalar", emA, [("pe", pmL)], "se", 1)

                        def emLy(t, qi=qi, c0=c0, c1=c1, par=par):
                            return t.matmul(pLy[0:128, par * 512:par * 512 + (c1 - c0)],
                                            qhb[:, qi * 128:(qi + 1) * 128],
                                            kyb[:, c0:c1], start=True, stop=True)
                        pmLy = ev.add("tensor", emLy, wLy, "pe", 1)

                        cw = [("pe", pmLy)]
                        if kt == 0 and len(ao_hist) >= 1:
                            cw.append(("se", ao_hist[-1]))
                        def emC(v, c0=c0, c1=c1, par=par, EY=EY):
                            return v.tensor_scalar(EY[:, c0:c1],
                                                   pLy[0:128, par * 512:par * 512 + (c1 - c0)],
                                                   -15.0, 15.0, OP.max, OP.min)
                        clip_ve[kt] = ev.add("vector", emC, cw, "ve", 1)
                    last_abs_se = abs_se[nt - 1]
                    last_clip_ve = clip_ve[nt - 1]

                    dg0 = qi * 128
                    mw = [("ve", last_clip_ve)]
                    if h == 0 and qi == 0:
                        mw.append(("ld", mdg_ld))

                    def emFE(v, dg0=dg0, EY=EY):
                        return v.tensor_tensor(EY[:, dg0:dg0 + 128], EY[:, dg0:dg0 + 128],
                                               mdg[:, 0:128], OP.min)
                    veE = ev.add("vector", emFE, mw, "ve", 1)

                    def emFL(v, dg0=dg0):
                        return v.tensor_tensor(Lrow[:, dg0:dg0 + 128], Lrow[:, dg0:dg0 + 128],
                                               mdg[:, 128:256], OP.min)
                    veL = ev.add("vector", emFL, [("se", last_abs_se), ("ve", veE)], "ve", 1)

                    def emExp(sc, kjend=kjend, EY=EY, mo=mo):
                        return sc.activation(EY[:, 0:kjend], EY[:, 0:kjend], AF.Exp,
                                             accum_out=mb[:, 0 + mo:1 + mo])
                    seE = ev.add("scalar", emExp, [("ve", veE)], "se", 1)

                    def emBn(v, mo=mo):
                        return v.tensor_scalar(mb[:, 8 + mo:9 + mo], mb[:, 0 + mo:1 + mo],
                                               3e-6, None, OP.add)
                    veB = ev.add("vector", emBn, [("se", seE)], "ve", 1)

                    def emLd(sc, mo=mo):
                        return sc.activation(mb[:, 16 + mo:17 + mo], mb[:, 0 + mo:1 + mo], AF.Ln,
                                             bias=self.cstb[:, 1:2])
                    ev.add("scalar", emLd, [("se", seE), ("ve", veB)], "se", 1)

                    def emW(sc, kjend=kjend, EY=EY, mo=mo):
                        return sc.activation(EY[:, 0:kjend], Lrow[:, 0:kjend], AF.Ln,
                                             bias=mb[:, 8 + mo:9 + mo])
                    seW = ev.add("scalar", emW, [("ve", veB), ("ve", veL)], "se", 1)

                    def emWs(v, kjend=kjend, EY=EY, mo=mo):
                        return v.tensor_scalar(EY[:, 0:kjend], EY[:, 0:kjend],
                                               mb[:, 16 + mo:17 + mo], 0.0, OP.subtract, OP.add,
                                               accum_out=mb[:, 24 + mo:25 + mo])
                    veS = ev.add("vector", emWs, [("se", seW)], "ve", 1)

                    def emEps(v, mo=mo):
                        return v.tensor_scalar(mb[:, 32 + mo:33 + mo], mb[:, 24 + mo:25 + mo],
                                               1e-6, None, OP.add)
                    veP = ev.add("vector", emEps, [("ve", veS)], "ve", 1)

                    def emRc(v, mo=mo):
                        return v.reciprocal(mb[:, 40 + mo:41 + mo], mb[:, 32 + mo:33 + mo])
                    veR = ev.add("vector", emRc, [("ve", veP)], "ve", 1)

                    def emAmul(v, kjend=kjend, EY=EY, AB=AB, mo=mo):
                        return v.tensor_scalar(AB[:, 0:kjend], EY[:, 0:kjend],
                                               mb[:, 40 + mo:41 + mo], None, OP.mult)
                    veA = ev.add("vector", emAmul, [("ve", veR)], "ve", 1)

                    cpv = {}
                    nblk = (qi + 4) // 4
                    for b in range(nblk):
                        k0, k1 = b * 4, min(qi + 1, b * 4 + 4)
                        w = (k1 - k0) * 128
                        par = b % 2
                        tlast = None
                        for j in range(k0, k1):
                            waits = [("ve", veA)]
                            if j == k0:
                                if b >= 2:
                                    waits.append(("ve", cpv[b - 2]))
                                elif last_pt_cp is not None:
                                    waits.append(("ve", last_pt_cp))

                            def emT(t, j=j, k0=k0, par=par, AB=AB):
                                return t.transpose(
                                    pt[:, par * 1024 + (j - k0) * 128:par * 1024 + (j - k0 + 1) * 128],
                                    AB[:, j * 128:(j + 1) * 128],
                                    self.identb[:, :])
                            tlast = ev.add("tensor", emT, waits, "pe", 1)

                        def emCp(v, par=par, w=w):
                            return v.tensor_copy(atb[:, par * 512:par * 512 + w],
                                                 pt[:, par * 1024:par * 1024 + w])
                        cpv[b] = ev.add("vector", emCp, [("pe", tlast)], "ve", 1)

                        avw = [("ve", cpv[b])]
                        if b == 0 and avse[h % 2] is not None:
                            avw.append(("se", avse[h % 2]))

                        def emAV(t, k0=k0, k1=k1, par=par, qi=qi, bh=bh, hp=h % 2):
                            for j in range(k0, k1):
                                ins = t.matmul(pav[bh:bh + 64, hp * 512:hp * 512 + 128],
                                               vtm[:, j * 64:(j + 1) * 64],
                                               atb[:, par * 512 + (j - k0) * 128:par * 512 + (j - k0 + 1) * 128],
                                               start=(j == 0), stop=(j == qi))
                            return ins
                        pmAV = ev.add("tensor", emAV, avw, "pe", 1)
                    last_pt_cp = cpv[nblk - 1]

                    def emEv(sc, bh=bh, ch=ch, qi=qi, hp=h % 2):
                        return sc.activation(ao[bh:bh + 64, ch * T + qi * 128:ch * T + (qi + 1) * 128],
                                             pav[bh:bh + 64, hp * 512:hp * 512 + 128], AF.Copy)
                    prev_av_se = ev.add("scalar", emEv, [("pe", pmAV)], "se", 1)
                    avse[h % 2] = prev_av_se
                    ao_hist.append(prev_av_se)

                if h + 2 < HEADS:
                    whpe[h] = ev.val("pe")
                    load_head(h + 2)
                else:
                    whpe[h] = ev.val("pe")

            wpm = {}

            def load_wp(c2):
                par = c2 % 2
                def em(g, c2=c2, par=par):
                    return g.dma_start(
                        out=wpv[:, par],
                        in_=self.wprojT[:, c2 * 128:(c2 + 1) * 128].rearrange("(kc p) n -> p kc n", p=128))
                waits = []
                if c2 >= 2:
                    waits.append(("pe", wp_pe[c2 - 2]))
                else:
                    waits.append(("pe", whpe[HEADS - 1]))
                ev.add("gpsimd", em, waits, "wq", 16)
                wpm[c2] = ev.val("wq")

            wp_pe = {}
            ao_ve = ev.val("ve")
            ao_se = ev.val("se")
            load_wp(0)
            load_wp(1)
            gse = {}
            gve = {}
            for c2 in range(NCH):
                par_w = c2 % 2
                for f in range(4):
                    fs = slice(f * 512, (f + 1) * 512)
                    par = f % 2
                    gg = 4 * c2 + f
                    waits = [("wq", wpm[c2])]
                    if gg == 0:
                        waits += [("ve", ao_ve), ("se", ao_se)]
                    if gg >= 2:
                        waits.append(("se", gse[gg - 2]))

                    def em(t, par_w=par_w, fs=fs, par=par):
                        for kc in range(NCH):
                            ins = t.matmul(pL[:, par * 512:(par + 1) * 512],
                                           wpv[:, par_w, kc, :],
                                           ao[:, kc * T + fs.start:kc * T + fs.stop],
                                           start=(kc == 0), stop=(kc == NCH - 1))
                        return ins
                    pm = ev.add("tensor", em, waits, "pe", 1)

                    waits = [("pe", pm)]
                    if gg >= 2:
                        waits.append(("ve", gve[gg - 2]))

                    def emc(sc, par=par):
                        return sc.activation(self.fta[:, par * 512:(par + 1) * 512],
                                             pL[:, par * 512:(par + 1) * 512], AF.Copy)
                    gse[gg] = ev.add("scalar", emc, waits, "se", 1)

                    def emv(v, c2=c2, fs=fs, par=par):
                        rrsl = self.ch(self.rr, c2)[:, fs]
                        return v.tensor_add(rrsl, rrsl, self.fta[:, par * 512:(par + 1) * 512])
                    gve[gg] = ev.add("vector", emv, [("se", gse[gg])], "ve", 1)
                wp_pe[c2] = ev.val("pe")
                if c2 + 2 < NCH:
                    load_wp(c2 + 2)

            ev.emit(block)
            psl.close()
            loc.close()

    def emit_ffn(self):
        nc, s = self.nc, self.s
        with nc.Block(self.uname("ffn")) as block:
            sp0 = s["sp"].v

            @block.sync
            def _(sy):
                ins = sy.dma_start(out=self.rspill, in_=self.rr.ap()[:, :])
                s["sp"].inc(ins, 16)

            loc = ExitStack()
            sb = lambda n, sh, d: loc.enter_context(nc.sbuf_tensor(self.uname(n), sh, d))
            AR = sb("f_arena", [128, 36352], BF16)
            arA = AR.ap()
            kroll = arA[:, 0:6144].rearrange("p (kc n) -> p kc n", kc=NCH)
            vtm = arA[:, 6144:12288].rearrange("p (kj h n) -> p kj h n", kj=6, h=HEADS)
            qbf = arA[:, 12288:16384].rearrange("p (c n) -> p c n", c=NCH)
            AOt = arA[:, 16384:20480].rearrange("p (c n) -> p c n", c=NCH)
            wqkvb = arA[:, 20480:26624].rearrange("p (b m kc n) -> p b m kc n", b=2, m=3, kc=NCH)
            wob = arA[:, 26624:28672].rearrange("p (b kc n) -> p b kc n", b=2, kc=NCH)
            vtmp = arA[:, 28672:29184]
            Eb = arA[:, 31232:32256]
            tabb = arA[:, 29184:31232].rearrange("p (m n) -> p m n", m=4)
            nrmv = arA[:, 31232:35328].bitcast(F32)
            recb = arA[0:16, 35328:36352]
            hbuf = arA[:, 0:22528].rearrange("p (j n) -> p j n", j=FJ)
            w1b = arA[:, 22528:26624].rearrange("p (b m kc n) -> p b m kc n", b=2, m=2, kc=NCH)
            w2b = arA[:, 26624:32256].rearrange("p (b kc n) -> p b kc n", b=2, kc=FJ)

            mbs = sb("f_mb", [128, 24], F32)
            zcolb = sb("f_zcol", [128, 16], F32)
            lmb = sb("f_lmb", [128, 1536], BF16)
            lngb = sb("f_lngb", [128, 48], F32)
            lnbb = sb("f_lnbb", [128, 48], F32)
            b2b = sb("f_b2b", [128, 48], F32)
            qsb = sb("f_qsb", [128, 8], F32)
            ksb = sb("f_ksb", [128, 8], F32)
            b1ab = sb("f_b1ab", [128, 132], F32)
            b1gb = sb("f_b1gb", [128, 132], F32)
            selqb = sb("f_selqb", [128, 128], BF16)
            selrb = sb("f_selrb", [16, 1024], BF16)

            psl = ExitStack()
            PS1 = psl.enter_context(nc.psum_tensor(self.uname("f_PS1"), [128, 2048], F32))
            PS2 = psl.enter_context(nc.psum_tensor(self.uname("f_PS2"), [128, 1024], F32))
            PSb = psl.enter_context(nc.psum_tensor(self.uname("f_PSb"), [128, 2048], BF16))

            ld0 = s["ld"].v

            @block.sync
            def _(sy):
                for dst, src in ((lngb[:, :], self.f_lng),
                                 (lnbb[:, :], self.f_lnb),
                                 (b2b[:, :], self.f_b2),
                                 (qsb[:, 0:6], self.f_qs),
                                 (ksb[:, 0:6], self.f_ks),
                                 (b1ab[:, :], self.f_b1a),
                                 (b1gb[:, :], self.f_b1g),
                                 (lmb[:, 0:384], self.c_lmask[0]),
                                 (lmb[:, 384:768], self.c_lmask[1]),
                                 (lmb[:, 768:1152], self.c_lmask0[0]),
                                 (lmb[:, 1152:1536], self.c_lmask0[1]),
                                 (selqb[:, :], self.c_selq),
                                 (selrb[:, :], self.c_selr),
                                 ):
                    ins = sy.dma_start(out=dst, in_=src)
                    s["ld"].inc(ins, 16)

            self.emit_ln(block, fp32_inplace=True,
                         pstat_ext=PS2.ap()[0:1, 0:1024])

            self._st4 = {"tab_rd": None, "cast_last": None, "wo_last_ve": None,
                         "sim_last_pe": None, "av_last_pe": None, "vpe": {},
                         "q_rgn_rd": {0: None, 1: None},
                         "kv_rgn_rd": {0: None, 1: None},
                         "vtmp_rd_pe": None, "psqr": None, "psb_rd": None,
                         "u": 0, "exp_se": {}, "rmax_ve": {}, "msk_ve": {}, "amul_ve": {},
                         "sim_pe": {}, "tr_last_pe": {},
                         "psb_cp": {}, "av_pe": {}, "ao_se": {}, "eb_rd": None}
            for i in range(DBG_FFN_ITERS if DBG_FFN_ITERS else 6):
                self._ffn_attn(block, i, dict(
                    kroll=kroll, vtm=vtm, qbf=qbf, AOt=AOt, wqkvb=wqkvb, wob=wob,
                    vtmp=vtmp, tabb=tabb, nrmv=nrmv, recb=recb, Eb=Eb,
                    mbs=mbs, zcolb=zcolb, lmb=lmb, qsb=qsb, ksb=ksb,
                    selqb=selqb, selrb=selrb, PS1=PS1, PS2=PS2, PSb=PSb, ld0=ld0))
                if DBG_FFN_ITERS and i == DBG_FFN_ITERS - 1:
                    break
                self.emit_ln(block, shift=True,
                             gain_ap=lambda c, i=i: lngb[:, i * 8 + c:i * 8 + c + 1],
                             bias_ap=lambda c, i=i: lnbb[:, i * 8 + c:i * 8 + c + 1],
                             pstat_ext=PS2.ap()[0:1, 0:1024],
                             wait_ve=s["ve"].v, wait_pe=s["pe"].v)
                self._ffn_mlp(block, i, dict(
                    hbuf=hbuf, w1b=w1b, w2b=w2b, b1ab=b1ab, b1gb=b1gb, b2b=b2b,
                    PS1=PS1, PS2=PS2))

            if DBG_FFN_ITERS:
                psl.close()
                loc.close()
                return
            ve_end = s["ve"].v
            pe_end = s["pe"].v
            se_end = s["se"].v
            sp_base = s["sp"].v

            @block.sync
            def _(sy):
                sy.wait_ge(s["ve"].h, ve_end)
                sy.wait_ge(s["pe"].h, pe_end)
                sy.wait_ge(s["se"].h, se_end)
                for c in range(NCH):
                    buf = self.fta if c % 2 == 0 else self.ftb
                    if c >= 2:
                        sy.wait_ge(s["ve"].h, ve_end + c - 1)
                    ins = sy.dma_start(out=buf[:, 0:T],
                                       in_=self.rspill[:, c * T:(c + 1) * T])
                    s["sp"].inc(ins, 16)

            sp_now = sp_base

            @block.vector
            def _(v):
                spv = sp_base
                for c in range(NCH):
                    buf = self.fta if c % 2 == 0 else self.ftb
                    spv += 16
                    v.wait_ge(s["sp"].h, spv)
                    ins = v.scalar_tensor_tensor(self.ch(self.rr, c), self.ch(self.rr, c),
                                                 0.5, buf[:, 0:T], OP.mult, OP.add)
                    s["ve"].inc(ins, 1)

            psl.close()
            loc.close()

    def _ffn_attn(self, block, it, B):
        nc, s = self.nc, self.s
        ev = self.EvProg(self)
        PS1, PS2, PSb = B["PS1"], B["PS2"], B["PSb"]
        kroll, vtm, qbf, AOt = B["kroll"], B["vtm"], B["qbf"], B["AOt"]
        wqkvb, wob, vtmp, tabb = B["wqkvb"], B["wob"], B["vtmp"], B["tabb"]
        nrmv, recb, mbs = B["nrmv"], B["recb"], B["mbs"]
        zcolb, lmb, qsb, ksb = B["zcolb"], B["lmb"], B["qsb"], B["ksb"]
        selqb, selrb, Eb = B["selqb"], B["selrb"], B["Eb"]
        atb = self.ftb.ap().bitcast(BF16)[:, 0:1024]
        if it == 0:
            ev.joins.append(("ld", B["ld0"]))

        st4 = self._st4
        psim_rgn = (PS1.ap()[:, 0:512], PS1.ap()[:, 512:1024],
                    PS2.ap()[:, 0:512], PS2.ap()[:, 512:1024])

        def emZzz(v):
            return v.memset(zcolb[:, 8:16], 0.0)
        ev.add("vector", emZzz, [], "ve", 1)
        if it == 0:
            def emZk(v):
                return v.memset(kroll[:, :, 0:256], 0.0)
            ev.add("vector", emZk, [], "ve", 1)

            def emZv(v):
                return v.memset(vtm[:, 0:2], 0.0)
            ev.add("vector", emZv, [], "ve", 1)

        for tt in range(4):
            ts, te = tt * 512, tt * 512 + 512

            tabw = []
            if st4["tab_rd"] is not None:
                tabw.append(("ve", st4["tab_rd"]))
            for mi, nm in enumerate(("rotAq", "rotBq", "rotAk", "rotBk")):
                def emTab(sy, mi=mi, nm=nm, ts=ts):
                    return sy.dma_start(out=tabb[:, mi, :], in_=self.c_rot[nm][:, ts:ts + 512])
                tab_ld = ev.add("sync", emTab, tabw if mi == 0 else [], "ld", 16)

            def emZc(v, tt=tt, te=te):
                last = None
                for c in range(4, NCH):
                    last = v.tensor_copy(zcolb[:, (tt % 2) * 8 + c:(tt % 2) * 8 + c + 1],
                                         self.ch(self.rr, c)[:, te - 1:te])
                return last
            zcw = []
            if st4["cast_last"] is not None:
                zcw.append(("se", st4["cast_last"]))
            zc_ve = ev.add("vector", emZc, zcw, "ve", 1)

            cast_se = {}
            for c in range(NCH):
                def emCast(sc, c=c, tt=tt, ts=ts, te=te):
                    if c < 4:
                        return sc.copy(self.ch(self.ubf, c)[:, ts:te],
                                       self.ch(self.rr, c)[:, ts:te])
                    sc.copy(self.ch(self.ubf, c)[:, ts:ts + 1],
                            zcolb[:, ((tt - 1) % 2) * 8 + c:((tt - 1) % 2) * 8 + c + 1])
                    return sc.copy(self.ch(self.ubf, c)[:, ts + 1:te],
                                   self.ch(self.rr, c)[:, ts:te - 1])
                w = []
                if c >= 4 and tt > 0:
                    w.append(("ve", zc_prev))
                if st4["wo_last_ve"] is not None:
                    w.append(("ve", st4["wo_last_ve"]))
                cast_se[c] = ev.add("scalar", emCast, w, "se", 1)
            zc_prev = zc_ve
            st4["cast_last"] = cast_se[NCH - 1]

            if tt > 0:
                def emRollK(v):
                    return v.tensor_copy(kroll[:, :, 0:256], kroll[:, :, 512:768])
                rollk_ve = ev.add("vector", emRollK, [("pe", st4["sim_last_pe"])], "ve", 1)

                def emRollV(v):
                    return v.tensor_copy(vtm[:, 0:2], vtm[:, 4:6])
                rollv_ve = ev.add("vector", emRollV, [("pe", st4["av_last_pe"])], "ve", 1)
            else:
                rollk_ve = rollv_ve = None

            evq_se = {}
            evkv_se = {}
            vtm_cp = {}
            psq_pe = psk_pe = None
            for c2 in range(NCH):
                par = c2 % 2
                ww = []
                key = c2 - 2 if c2 >= 2 else c2 + 6
                if key in st4["vpe"]:
                    ww.append(("pe", st4["vpe"][key]))
                for mi in range(3):
                    def emW(g, it=it, c2=c2, par=par, mi=mi):
                        return g.dma_start(
                            out=wqkvb[:, par, mi],
                            in_=self.f_wqkv[it, :, mi * D + c2 * 128:mi * D + (c2 + 1) * 128]
                            .rearrange("(kc p) n -> p kc n", p=128))
                    ev.add("gpsimd", emW, ww if mi == 0 else [], "wq", 16)
                wq_mark = ev.val("wq")

                qw = [("wq", wq_mark), ("se", cast_se[NCH - 1])]
                if (c2 - 2) in evq_se:
                    qw.append(("se", evq_se[c2 - 2]))
                elif st4["q_rgn_rd"][par] is not None:
                    qw.append(("ve", st4["q_rgn_rd"][par]))

                def emQ(t, c2=c2, par=par, ts=ts, te=te):
                    for kc in range(NCH):
                        ins = t.matmul(PS2[:, par * 512:(par + 1) * 512],
                                       wqkvb[:, c2 % 2, 0, kc, :],
                                       self.ch(self.ubf, kc)[:, ts:te],
                                       start=(kc == 0), stop=(kc == NCH - 1))
                    return ins
                qpe = ev.add("tensor", emQ, qw, "pe", 1)

                def emQe(sc, c2=c2, par=par):
                    return sc.activation(qbf[:, c2, :], PS2[:, par * 512:(par + 1) * 512], AF.Copy)
                evq_se[c2] = ev.add("scalar", emQe, [("pe", qpe)], "se", 1)

                vtmpw = [("se", evq_se[c2])]
                if st4["vtmp_rd_pe"] is not None:
                    vtmpw.append(("pe", st4["vtmp_rd_pe"]))
                    st4["vtmp_rd_pe"] = None

                def emQsq(v, c2=c2):
                    return v.tensor_mul(vtmp[:, :], qbf[:, c2, :], qbf[:, c2, :])
                sq_ve = ev.add("vector", emQsq, vtmpw, "ve", 1)

                def emQpsq(t, c2=c2):
                    return t.matmul(PS1[0:16, 1024:1536], selqb[:, c2 * 16:(c2 + 1) * 16],
                                    vtmp[:, :], start=(c2 == 0), stop=(c2 == NCH - 1))
                pqw = [("ve", sq_ve)]
                if c2 == 0 and st4["psqr"] is not None:
                    pqw.append(("se", st4["psqr"]))
                psq_pe = ev.add("tensor", emQpsq, pqw, "pe", 1)

                kw = [("wq", wq_mark)]
                if (c2 - 2) in evkv_se:
                    kw.append(("se", evkv_se[c2 - 2][1]))
                elif st4["kv_rgn_rd"][par] is not None:
                    kw.append(("ve", st4["kv_rgn_rd"][par]))
                if rollk_ve is not None and c2 == 0:
                    kw.append(("ve", rollk_ve))

                def emK(t, c2=c2, par=par, ts=ts, te=te):
                    for kc in range(NCH):
                        ins = t.matmul(PS1[:, par * 512:(par + 1) * 512],
                                       wqkvb[:, c2 % 2, 1, kc, :],
                                       self.ch(self.ubf, kc)[:, ts:te],
                                       start=(kc == 0), stop=(kc == NCH - 1))
                    return ins
                kpe = ev.add("tensor", emK, kw, "pe", 1)

                kew = [("pe", kpe)]
                if rollk_ve is not None:
                    kew.append(("ve", rollk_ve))

                def emKe(sc, c2=c2, par=par):
                    return sc.activation(kroll[:, c2, 256:768], PS1[:, par * 512:(par + 1) * 512], AF.Copy)
                ke_se = ev.add("scalar", emKe, kew, "se", 1)

                def emKsq(v, c2=c2):
                    return v.tensor_mul(vtmp[:, :], kroll[:, c2, 256:768], kroll[:, c2, 256:768])
                ksq_ve = ev.add("vector", emKsq, [("se", ke_se), ("pe", psq_pe)], "ve", 1)

                def emKpsq(t, c2=c2):
                    return t.matmul(PS1[0:16, 1536:2048], selqb[:, c2 * 16:(c2 + 1) * 16],
                                    vtmp[:, :], start=(c2 == 0), stop=(c2 == NCH - 1))
                psk_pe = ev.add("tensor", emKpsq, [("ve", ksq_ve)], "pe", 1)

                def emV(t, c2=c2, par=par, ts=ts, te=te):
                    for kc in range(NCH):
                        ins = t.matmul(PS1[:, par * 512:(par + 1) * 512],
                                       wqkvb[:, c2 % 2, 2, kc, :],
                                       self.ch(self.ubf, kc)[:, ts:te],
                                       start=(kc == 0), stop=(kc == NCH - 1))
                    return ins
                vpe = ev.add("tensor", emV, [("se", ke_se)], "pe", 1)

                def emVe(sc, c2=c2, par=par):
                    return sc.activation(vtmp[:, :], PS1[:, par * 512:(par + 1) * 512], AF.Copy)
                ve_se = ev.add("scalar", emVe, [("pe", vpe), ("pe", psk_pe)], "se", 1)
                evkv_se[c2] = (ke_se, ve_se)

                for ho in range(2):
                    h = 2 * c2 + ho
                    b = ho * 64
                    for tch in range(4):
                        ti = ho * 4 + tch
                        parp = (c2 * 8 + ti) % 2
                        tw = [("se", ve_se)]
                        gkey = c2 * 8 + ti - 2
                        if gkey >= 0 and gkey in vtm_cp:
                            tw.append(("se", vtm_cp[gkey]))
                        elif st4.get("psb_rd_se") is not None and gkey < 0:
                            tw.append(("se", st4["psb_rd_se"]))
                        if rollv_ve is not None and c2 == 0 and ti == 0:
                            tw.append(("ve", rollv_ve))

                        def emT(t, b=b, tch=tch, parp=parp):
                            return t.transpose(PSb[:, parp * 1024:parp * 1024 + 64],
                                               vtmp[b:b + 64, tch * 128:(tch + 1) * 128],
                                               self.identb[b:b + 64, b:b + 64])
                        tpe = ev.add("tensor", emT, tw, "pe", 1)

                        def emTc(sc, h=h, tch=tch, parp=parp):
                            return sc.copy(vtm[:, 2 + tch, h, :],
                                           PSb[:, parp * 1024:parp * 1024 + 64])
                        vtm_cp[c2 * 8 + ti] = ev.add("scalar", emTc, [("pe", tpe)], "se", 1)
                st4["vpe"][c2] = ev.val("pe")
            st4["vtmp_rd_pe"] = ev.val("pe")
            vtm_cp_last = vtm_cp[NCH * 8 - 1]

            def emSqQ(sc):
                return sc.activation(nrmv[0:16, 0:512], PS1[0:16, 1024:1536], AF.Sqrt)
            sqw = [("pe", psq_pe)]
            if st4.get("eb_rd") is not None:
                sqw.append(("ve", st4["eb_rd"]))
            sq_se = ev.add("scalar", emSqQ, sqw, "se", 1)

            def emSqK(sc):
                return sc.activation(nrmv[0:16, 512:1024], PS1[0:16, 1536:2048], AF.Sqrt)
            sk_se = ev.add("scalar", emSqK, [("pe", psk_pe)], "se", 1)

            def emMaxQ(v):
                return v.tensor_scalar_max(nrmv[0:16, 0:1024], nrmv[0:16, 0:1024], 1e-12)
            mx_ve = ev.add("vector", emMaxQ, [("se", sq_se), ("se", sk_se)], "ve", 1)

            def emRec(v):
                return v.reciprocal(nrmv[0:16, 1024:2048], nrmv[0:16, 0:1024])
            rc_ve = ev.add("vector", emRec, [("ve", mx_ve)], "ve", 1)

            def emRecB(v):
                return v.tensor_copy(recb[:, :], nrmv[0:16, 1024:2048])
            rb_ve = ev.add("vector", emRecB, [("ve", rc_ve)], "ve", 1)

            rot_done_ve = None
            prep_rd = {}
            t2_rd = {}
            prep_rgn = (PS1.ap()[:, 0:512], PS1.ap()[:, 1024:1536])
            pswp_rgn = (PS1.ap()[:, 512:1024], PS1.ap()[:, 1536:2048])
            for c2 in range(NCH):
                for which in (0, 1):
                    dst = qbf[:, c2, :] if which == 0 else kroll[:, c2, 256:768]
                    tabA = tabb[:, 0, :] if which == 0 else tabb[:, 2, :]
                    tabB = tabb[:, 1, :] if which == 0 else tabb[:, 3, :]
                    scl = qsb if which == 0 else ksb
                    roff = 0 if which == 0 else 512
                    u = c2 * 2 + which
                    pr = u % 2

                    pw = [("ve", rb_ve)]
                    if u >= 2:
                        pw.append(("ve", prep_rd[u - 2]))
                    elif u == 0:
                        pw.append(("se", evkv_se[NCH - 1][1]))

                    def emRep(t, c2=c2, roff=roff, pr=pr):
                        return t.matmul(prep_rgn[pr], selrb[:, c2 * 128:(c2 + 1) * 128],
                                        recb[:, roff:roff + 512], start=True, stop=True)
                    prep_pe = ev.add("tensor", emRep, pw, "pe", 1)

                    def emQn(v, dst=dst, scl=scl, it=it, pr=pr):
                        return v.scalar_tensor_tensor(dst, prep_rgn[pr],
                                                      scl[:, it:it + 1], dst,
                                                      OP.mult, OP.mult)
                    qn_ve = ev.add("vector", emQn, [("pe", prep_pe), ("ld", tab_ld)], "ve", 1)

                    sww = [("ve", qn_ve)]
                    if u >= 2:
                        sww.append(("ve", t2_rd[u - 2]))

                    def emSw(t, dst=dst, pr=pr):
                        return t.matmul(pswp_rgn[pr], self.pswapb[:, :], dst,
                                        start=True, stop=True)
                    pswp_pe = ev.add("tensor", emSw, sww, "pe", 1)

                    def emT1(v, dst=dst, tabA=tabA):
                        return v.tensor_mul(self.fta[:, 0:512], dst, tabA)
                    t1_ve = ev.add("vector", emT1, [("ve", qn_ve)], "ve", 1)

                    def emT2(v, tabB=tabB, pr=pr):
                        return v.tensor_mul(self.fta[:, 512:1024], pswp_rgn[pr], tabB)
                    t2_ve = ev.add("vector", emT2, [("pe", pswp_pe), ("ve", t1_ve)], "ve", 1)
                    t2_rd[u] = t2_ve

                    def emRot(v, dst=dst):
                        return v.tensor_add(dst, self.fta[:, 0:512], self.fta[:, 512:1024])
                    rot_done_ve = ev.add("vector", emRot, [("ve", t2_ve), ("pe", pswp_pe)], "ve", 1)
                    prep_rd[u] = rot_done_ve
            st4["tab_rd"] = rot_done_ve

            units = []
            for h in range(HEADS):
                for wi in range(2):
                    for qc in range(2):
                        units.append((st4["u"], h, wi, qc))
                        st4["u"] += 1

            def w_sim(u):
                g, h, wi, qc = u
                ch2, bh = h // 2, (h % 2) * 64
                wstart, wlen = wi * 256 + qc * 128, 384
                pu4 = g % 4
                simw = [("ve", rot_done_ve)]
                if g - 4 in st4["msk_ve"]:
                    simw.append(("ve", st4["msk_ve"][g - 4]))

                def emSim(t, bh=bh, ch2=ch2, wi=wi, qc=qc, wstart=wstart,
                          wlen=wlen, pu4=pu4):
                    return t.matmul(psim_rgn[pu4][:, 0:wlen],
                                    qbf[bh:bh + 64, ch2, wi * 256 + qc * 128:wi * 256 + (qc + 1) * 128],
                                    kroll[bh:bh + 64, ch2, wstart:wstart + wlen],
                                    start=True, stop=True)
                st4["sim_pe"][g] = ev.add("tensor", emSim, simw, "pe", 1)
                st4["sim_last_pe"] = st4["sim_pe"][g]

            def w_chain(u):
                g, h, wi, qc = u
                w_glob = 2 * tt + wi
                wlen = 384
                pu4, pu2 = g % 4, g % 2
                mslice = (lmb[:, 768 + qc * 384:1152 + qc * 384] if w_glob == 0
                          else lmb[:, qc * 384:(qc + 1) * 384])
                mkw = [("pe", st4["sim_pe"][g])]
                if g - 2 in st4["exp_se"]:
                    mkw.append(("se", st4["exp_se"][g - 2]))

                def emMk(v, pu4=pu4, pu2=pu2, wlen=wlen, mslice=mslice):
                    return v.tensor_add(
                        self.fta[:, 1024 + pu2 * 512:1024 + pu2 * 512 + wlen],
                        psim_rgn[pu4][:, 0:wlen], mslice[:, 0:wlen])
                st4["msk_ve"][g] = ev.add("vector", emMk, mkw, "ve", 1)

                def emRm(v, pu2=pu2, wlen=wlen):
                    return v.tensor_reduce(mbs[:, pu2:pu2 + 1],
                                           self.fta[:, 1024 + pu2 * 512:1024 + pu2 * 512 + wlen],
                                           AX.X, OP.max, negate=True)
                st4["rmax_ve"][g] = ev.add("vector", emRm,
                                           [("ve", st4["msk_ve"][g])], "ve", 1)

                expw = [("ve", st4["rmax_ve"][g])]
                if g - 2 in st4["tr_last_pe"]:
                    expw.append(("pe", st4["tr_last_pe"][g - 2]))

                def emExp(sc, pu2=pu2, wlen=wlen):
                    return sc.activation(Eb[:, pu2 * 512:pu2 * 512 + wlen],
                                         self.fta[:, 1024 + pu2 * 512:1024 + pu2 * 512 + wlen],
                                         AF.Exp, bias=mbs[:, pu2:pu2 + 1],
                                         accum_out=mbs[:, 8 * (1 + pu2):8 * (1 + pu2) + 1])
                st4["exp_se"][g] = ev.add("scalar", emExp, expw, "se", 1)

                def emRc2(v, pu2=pu2):
                    return v.reciprocal(mbs[:, 4 + pu2:5 + pu2],
                                        mbs[:, 8 * (1 + pu2):8 * (1 + pu2) + 1])
                rc2_ve = ev.add("vector", emRc2, [("se", st4["exp_se"][g])], "ve", 1)

                def emAm(v, pu2=pu2, wlen=wlen):
                    return v.tensor_scalar(Eb[:, pu2 * 512:pu2 * 512 + wlen],
                                           Eb[:, pu2 * 512:pu2 * 512 + wlen],
                                           mbs[:, 4 + pu2:5 + pu2], None, OP.mult)
                st4["amul_ve"][g] = ev.add("vector", emAm, [("ve", rc2_ve)], "ve", 1)

            def w_tail(u, vtm_cp_last):
                g, h, wi, qc = u
                ch2, bh = h // 2, (h % 2) * 64
                wstart, wlen = wi * 256 + qc * 128, 384
                kj0 = wstart // 128
                pu2 = g % 2
                ntile = 3
                tlast = None
                for j in range(ntile):
                    tw = [("ve", st4["amul_ve"][g])]
                    if j == 0 and g - 1 in st4["psb_cp"]:
                        tw.append(("se", st4["psb_cp"][g - 1]))
                    if j == 0 and h == 0 and wi == 0 and qc == 0:
                        tw.append(("se", vtm_cp_last))

                    def emTA(t, j=j, pu2=pu2):
                        return t.transpose(PSb[:, pu2 * 1024 + j * 128:pu2 * 1024 + (j + 1) * 128],
                                           Eb[:, pu2 * 512 + j * 128:pu2 * 512 + (j + 1) * 128],
                                           self.identb[:, :])
                    tlast = ev.add("tensor", emTA, tw, "pe", 1)
                st4["tr_last_pe"][g] = tlast

                cpw = [("pe", tlast)]
                if g - 2 in st4["av_pe"]:
                    cpw.append(("pe", st4["av_pe"][g - 2]))

                def emCpA(sc, pu2=pu2, wlen=wlen):
                    return sc.copy(atb[:, pu2 * 512:pu2 * 512 + wlen],
                                   PSb[:, pu2 * 1024:pu2 * 1024 + wlen])
                st4["psb_cp"][g] = ev.add("scalar", emCpA, cpw, "se", 1)

                avw = [("se", st4["psb_cp"][g])]
                if g - 2 in st4["ao_se"]:
                    avw.append(("se", st4["ao_se"][g - 2]))

                def emAV(t, kj0=kj0, h=h, bh=bh, pu2=pu2, ntile=ntile):
                    for j in range(ntile):
                        ins = t.matmul(PS1[bh:bh + 64, 1024 + pu2 * 512:1024 + pu2 * 512 + 128],
                                       vtm[:, kj0 + j, h, :],
                                       atb[:, pu2 * 512 + j * 128:pu2 * 512 + (j + 1) * 128],
                                       start=(j == 0), stop=(j == ntile - 1))
                    return ins
                st4["av_pe"][g] = ev.add("tensor", emAV, avw, "pe", 1)
                st4["av_last_pe"] = st4["av_pe"][g]

                def emAOe(sc, bh=bh, ch2=ch2, wi=wi, qc=qc, pu2=pu2):
                    return sc.activation(AOt[bh:bh + 64, ch2, wi * 256 + qc * 128:wi * 256 + (qc + 1) * 128],
                                         PS1[bh:bh + 64, 1024 + pu2 * 512:1024 + pu2 * 512 + 128],
                                         AF.Copy)
                st4["ao_se"][g] = ev.add("scalar", emAOe, [("pe", st4["av_pe"][g])], "se", 1)

            w_sim(units[0])
            w_sim(units[1])
            for i, u in enumerate(units):
                w_chain(u)
                if i + 2 < len(units):
                    w_sim(units[i + 2])
                w_tail(u, vtm_cp_last)
            ao_all = ev.val("se")
            st4["eb_rd"] = ev.val("ve")
            st4["psqr"] = st4["ao_se"][st4["u"] - 1]
            st4["psb_rd"] = None
            st4["psb_rd_se"] = st4["psb_cp"][st4["u"] - 1]

            wo_pe = {}
            wo_ve = {}
            for c2 in range(NCH):
                par = c2 % 2

                def emWoL(g_, it=it, c2=c2, par=par):
                    return g_.dma_start(out=wob[:, par],
                                        in_=self.f_wo[it, :, c2 * 128:(c2 + 1) * 128]
                                        .rearrange("(kc p) n -> p kc n", p=128))
                ww = []
                if c2 >= 2:
                    ww.append(("pe", wo_pe[c2 - 2]))
                ev.add("gpsimd", emWoL, ww, "wq", 16)
                wo_mark = ev.val("wq")

                mmw = [("wq", wo_mark), ("se", ao_all)]
                if c2 >= 2:
                    mmw.append(("ve", wo_ve[c2 - 2]))
                else:
                    mmw.append(("ve", ev.val("ve")))
                def emWo(t, c2=c2, par=par):
                    for kc in range(NCH):
                        ins = t.matmul(PS1[:, par * 512:(par + 1) * 512],
                                       wob[:, par, kc, :], AOt[:, kc, :],
                                       start=(kc == 0), stop=(kc == NCH - 1))
                    return ins
                wo_pe[c2] = ev.add("tensor", emWo, mmw, "pe", 1)

                def emZu(v, c2=c2, par=par, ts=ts, te=te):
                    rrsl = self.ch(self.rr, c2)[:, ts:te]
                    return v.scalar_tensor_tensor(rrsl, PS1[:, par * 512:(par + 1) * 512],
                                                  1.0, rrsl, OP.mult, OP.add)
                wo_ve[c2] = ev.add("vector", emZu, [("pe", wo_pe[c2])], "ve", 1)
            st4["wo_last_ve"] = wo_ve[NCH - 1]
            st4["q_rgn_rd"] = {0: None, 1: None}
            st4["kv_rgn_rd"] = {0: wo_ve[NCH - 2], 1: wo_ve[NCH - 1]}
            st4["q_rgn_rd"][0] = ev.val("ve")
            st4["q_rgn_rd"][1] = ev.val("ve")

        ev.emit(block)

    def _ffn_mlp(self, block, it, B):
        nc, s = self.nc, self.s
        ev = self.EvProg(self)
        PS1, PS2 = B["PS1"], B["PS2"]
        hbuf, w1b, w2b = B["hbuf"], B["w1b"], B["w2b"]
        b1ab, b1gb, b2b = B["b1ab"], B["b1gb"], B["b2b"]

        for half in range(2):
            hs = half * 1024
            hv_ve = {}
            gl_se = {}
            hw_pe = {}
            for j in range(FJ):
                parw = j % 2
                ww = []
                if j >= 2:
                    ww.append(("pe", hw_pe[(j - 2, 1)]))

                def emW1a(g, it=it, j=j, parw=parw):
                    return g.dma_start(out=w1b[:, parw, 0], in_=self.f_w1a[it, :, :, j, :])

                def emW1g(g, it=it, j=j, parw=parw):
                    return g.dma_start(out=w1b[:, parw, 1], in_=self.f_w1g[it, :, :, j, :])
                ev.add("gpsimd", emW1a, ww, "wq", 16)
                ev.add("gpsimd", emW1g, [], "wq", 16)
                w1_mark = ev.val("wq")

                for fsub in range(2):
                    gidx = 2 * j + fsub
                    para = gidx % 2
                    mmw = [("wq", w1_mark)]
                    if (j - 1, fsub) in hv_ve:
                        mmw.append(("ve", hv_ve[(j - 1, fsub)]))

                    def emA(t, j=j, parw=parw, fsub=fsub, para=para, hs=hs):
                        for kc in range(NCH):
                            t.matmul(PS1[:, para * 512:(para + 1) * 512],
                                     w1b[:, parw, 0, kc, :],
                                     self.ch(self.ubf, kc)[:, hs + fsub * 512:hs + (fsub + 1) * 512],
                                     start=(kc == 0), stop=(kc == NCH - 1))
                        for kc in range(NCH):
                            ins = t.matmul(PS1[:, 1024 + para * 512:1024 + (para + 1) * 512],
                                           w1b[:, parw, 1, kc, :],
                                           self.ch(self.ubf, kc)[:, hs + fsub * 512:hs + (fsub + 1) * 512],
                                           start=(kc == 0), stop=(kc == NCH - 1))
                        return ins
                    hw_pe[(j, fsub)] = ev.add("tensor", emA, mmw, "pe", 1)

                    gw = [("pe", hw_pe[(j, fsub)])]
                    if (j - 1, fsub) in hv_ve:
                        gw.append(("ve", hv_ve[(j - 1, fsub)]))

                    def emG(sc, it=it, j=j, fsub=fsub, para=para):
                        return sc.activation(self.fta[:, fsub * 512:(fsub + 1) * 512],
                                             PS1[:, 1024 + para * 512:1024 + (para + 1) * 512],
                                             AF.Gelu,
                                             bias=b1gb[:, it * FJ + j:it * FJ + j + 1])
                    g_se = ev.add("scalar", emG, gw, "se", 1)
                    gl_se[(j, fsub)] = g_se

                    def emH(v, it=it, j=j, fsub=fsub, para=para):
                        return v.scalar_tensor_tensor(hbuf[:, j, fsub * 512:(fsub + 1) * 512],
                                                      PS1[:, para * 512:(para + 1) * 512],
                                                      b1ab[:, it * FJ + j:it * FJ + j + 1],
                                                      self.fta[:, fsub * 512:(fsub + 1) * 512],
                                                      OP.add, OP.mult)
                    hv_ve[(j, fsub)] = ev.add("vector", emH, [("se", g_se)], "ve", 1)
            h_all = ev.val("ve")

            yw_pe = {}
            y_se = {}
            y_ve = {}
            for dout in range(NCH):
                par = dout % 2
                ww = []
                if dout >= 2:
                    ww.append(("pe", yw_pe[dout - 2]))

                def emW2(g, it=it, dout=dout, par=par):
                    return g.dma_start(out=w2b[:, par],
                                       in_=self.f_w2[it, :, :, dout * 128:(dout + 1) * 128])
                ev.add("gpsimd", emW2, ww, "wq", 16)
                w2_mark = ev.val("wq")

                mmw = [("wq", w2_mark), ("ve", h_all)]
                if dout >= 2:
                    mmw.append(("se", y_se[dout - 2]))

                def emY(t, dout=dout, par=par):
                    for fsub in range(2):
                        for kc in range(FJ):
                            ins = t.matmul(PS1[:, par * 1024 + fsub * 512:par * 1024 + (fsub + 1) * 512],
                                           w2b[:, par, kc, :],
                                           hbuf[:, kc, fsub * 512:(fsub + 1) * 512],
                                           start=(kc == 0), stop=(kc == FJ - 1))
                    return ins
                yw_pe[dout] = ev.add("tensor", emY, mmw, "pe", 1)

                yew = [("pe", yw_pe[dout])]
                if dout >= 1:
                    yew.append(("ve", y_ve[dout - 1]))

                def emYe(sc, it=it, dout=dout, par=par):
                    return sc.activation(self.ftb[:, 0:1024], PS1[:, par * 1024:par * 1024 + 1024],
                                         AF.Identity, bias=b2b[:, it * 8 + dout:it * 8 + dout + 1])
                y_se[dout] = ev.add("scalar", emYe, yew, "se", 1)

                def emZa(v, dout=dout, hs=hs):
                    rrsl = self.ch(self.rr, dout)[:, hs:hs + 1024]
                    return v.tensor_add(rrsl, rrsl, self.ftb[:, 0:1024])
                y_ve[dout] = ev.add("vector", emZa, [("se", y_se[dout])], "ve", 1)

        ev.emit(block)

    def build(self):
        self.emit_preload()
        for idx, st in enumerate(self.stages):
            with self.nc.named_scope(f"{idx}_{st}"):
                getattr(self, "emit_" + st)()
        self.emit_store()
        self.glob.close()
        return self.nc

    def emit_store(self):
        nc, s = self.nc, self.s
        with nc.Block("store") as block:
            @block.sync
            def _(sy):
                ins = sy.dma_start(out=self.out.rearrange("(c p) n -> p c n", p=128),
                                   in_=self.rr.ap().rearrange("p (c n) -> p c n", c=NCH))
                s["st"].inc(ins, 16)
                sy.wait_ge(s["st"].h, s["st"].v)


def build_program(stages=ALL_STAGES, detect_races=True):
    nc = bass.Bass("TRN2", target_bir_lowering=False, detect_race_conditions=detect_races)
    p = Prog(nc, stages)
    p.build()
    return nc


def make_in_maps(inputs):
    consts = host_constants()
    lay = host_layout(inputs)
    xs = inputs["L0_attn_x"]
    in_maps = []
    for b in range(xs.shape[0]):
        m = {"x": np.ascontiguousarray(xs[b].T)}
        m.update(lay)
        for k, v in consts.items():
            m["c_" + k] = v
        in_maps.append(m)
    return in_maps


def kernel(**inputs):
    nc = build_program()
    in_maps = make_in_maps(inputs)
    res = run_bass_kernel_spmd(nc, in_maps, core_ids=list(range(8)))
    out = np.stack([np.asarray(r["out"]).T for r in res.results], 0)
    return np.ascontiguousarray(out.astype(np.float32))
